# version 08
# speedup vs baseline: 1.2358x; 1.0538x over previous
import numpy as np
from contextlib import ExitStack

import concourse.bass as bass
import concourse.mybir as mybir
from concourse.bass_utils import run_bass_kernel_spmd

F32 = mybir.dt.float32
BF16 = mybir.dt.bfloat16
I32 = mybir.dt.int32
AF = mybir.ActivationFunctionType
ALU = mybir.AluOpType
AX = mybir.AxisListType

D = 1024
PT = D // 128
LN_EPS = 1e-5

ENGINES = ("tensor", "vector", "scalar", "gpsimd", "sync")


def ceil_div(a, b):
    return (a + b - 1) // b


class Op:
    __slots__ = ("engine", "fn", "deps", "units", "pos", "needs_inc", "inc_each",
                 "stream", "idx")

    def __init__(self, engine, fn, deps, units, inc_each, stream, idx):
        self.engine = engine
        self.fn = fn
        self.deps = deps
        self.units = units
        self.inc_each = inc_each
        self.pos = None
        self.needs_inc = False
        self.stream = stream
        self.idx = idx


class Prog:

    def __init__(self, nc, ctx):
        self.nc = nc
        self.ctx = ctx
        self.ops = []
        self.sems = {}
        self.uid = 0
        self.last_on_stream = {}

    def _mk(self, p):
        self.uid += 1
        return f"{p}{self.uid}"

    def op(self, engine, fn, after=(), units=1, dma=False, stream=None):
        assert engine in ENGINES, engine
        skey = (f"dma:{engine}:" + (stream or "_")) if dma else engine
        deps = tuple(a for a in after if a is not None)
        if dma and skey in self.last_on_stream:
            deps = deps + (self.last_on_stream[skey],)
        o = Op(engine, fn, deps, units, 16 if dma else 1, skey, len(self.ops))
        for d in o.deps:
            assert d.idx < o.idx
        self.ops.append(o)
        if dma:
            self.last_on_stream[skey] = o
        return o

    def dma(self, engine, out, in_, after=(), stream=None, **kw):
        return self.op(engine, lambda e: e.dma_start(out=out, in_=in_, **kw),
                       after=after, dma=True, stream=stream)

    def emit(self):
        nc = self.nc
        for o in self.ops:
            for d in o.deps:
                d.needs_inc = True
        cnt = {}
        for o in self.ops:
            if o.inc_each == 16:
                o.needs_inc = True
            if o.needs_inc:
                cnt[o.stream] = cnt.get(o.stream, 0) + o.units * o.inc_each
                o.pos = cnt[o.stream]
        for skey in cnt:
            self.sems[skey] = self.ctx.enter_context(
                nc.semaphore(self._mk("sem_" + skey.replace(":", "_"))))
        used_eng = {e for e in ENGINES if any(o.engine == e for o in self.ops)}
        by_eng = {e: [o for o in self.ops if o.engine == e] for e in used_eng}

        with nc.Block() as block:
            def run_engine(eng_obj, ename):
                waited = {}
                for o in by_eng[ename]:
                    need = {}
                    for d in o.deps:
                        assert d.pos is not None, "dep never marked"
                        if d.pos > waited.get(d.stream, 0):
                            need[d.stream] = max(need.get(d.stream, 0), d.pos)
                    for skey, v in need.items():
                        eng_obj.wait_ge(self.sems[skey], v)
                        waited[skey] = v
                    insts = o.fn(eng_obj)
                    if o.needs_inc:
                        if isinstance(insts, (list, tuple)):
                            assert len(insts) == o.units, (len(insts), o.units)
                            for i in insts:
                                i.then_inc(self.sems[o.stream], o.inc_each)
                        else:
                            assert o.units == 1, o.units
                            insts.then_inc(self.sems[o.stream], o.inc_each)

            for ename in ("sync", "gpsimd", "scalar", "vector", "tensor"):
                if ename not in used_eng:
                    continue
                getattr(block, ename)(lambda eng, _n=ename: run_engine(eng, _n))



def sbuf(nc, ctx, shape, dt, name="t"):
    return ctx.enter_context(nc.sbuf_tensor(name + f"_{id(ctx) % 997}_{np.random.randint(1e9)}", shape, dt))


_NAME_N = [0]


def _nm(p):
    _NAME_N[0] += 1
    return f"{p}{_NAME_N[0]}"


def SB(nc, ctx, shape, dt, name="t"):
    return ctx.enter_context(nc.sbuf_tensor(_nm(name), shape, dt))


def PS(nc, ctx, shape, dt=F32, name="p"):
    return ctx.enter_context(nc.psum_tensor(_nm(name), shape, dt))


def load_weight_bf16(p, wsb, w_hbm, IN, OUT, after=()):
    src = w_hbm.rearrange("(i p) o -> p i o", p=128)
    dst = wsb[:].rearrange("p (i o) -> p i o", o=OUT)
    return p.dma("gpsimd", dst, src, after=after)


def wtile(wsb, i_blk, o_blk, OUT, width=128):
    base = i_blk * OUT + o_blk * 128
    return wsb[:, base: base + width]


def mm_chain(p, psum_ap, lhsT_tiles, rhs_tiles, after=(), extra_first=None):
    K = len(lhsT_tiles)

    def fn(te):
        ins = None
        for k in range(K):
            ins = te.matmul(psum_ap, lhsT_tiles[k], rhs_tiles[k],
                            start=(k == 0), stop=(k == K - 1))
        return ins
    return p.op("tensor", fn, after=after)


def emit_ln(p, nc, ctx, x_tiles, xn_tiles, Tt, ones_c, ones_r, bufs,
            x_ready, eps=LN_EPS, out_extra=None, xn32_tiles=None):
    p_sum1, p_sum2, p_bc1, p_bc2, stat, xbfl, sq = (
        bufs["p_sum1"], bufs["p_sum2"], bufs["p_bc1"], bufs["p_bc2"],
        bufs["stat"], bufs["xbf"], bufs["sq"])
    xbf = xbfl[0]
    sums = []
    m1 = None
    m2 = None
    sqop = None
    for i in range(PT):
        cast = p.op("vector", lambda v, i=i: v.tensor_copy(xbf[:, 0:Tt], x_tiles[i][:, 0:Tt]),
                    after=x_ready + ((m1,) if m1 is not None else ())
                    + ((sqop,) if sqop is not None else ()))
        sqop = p.op("scalar", lambda s, i=i: s.activation(sq[:, 0:Tt], xbf[:, 0:Tt], AF.Square),
                    after=(cast,) + ((m2,) if m2 is not None else ()))
        m1 = p.op("tensor", lambda te, i=i: te.matmul(p_sum1[0:1, 0:Tt], ones_c[:, 0:1], xbf[:, 0:Tt],
                                                      start=(i == 0), stop=(i == PT - 1)),
                  after=(cast,))
        m2 = p.op("tensor", lambda te, i=i: te.matmul(p_sum2[0:1, 0:Tt], ones_c[:, 0:1], sq[:, 0:Tt],
                                                      start=(i == 0), stop=(i == PT - 1)),
                  after=(sqop,))
        sums += [m1, m2]
    mean_ap = stat[0:1, 0:Tt]
    ex2_ap = stat[0:1, Tt:2 * Tt]
    sc1_ap = stat[0:1, 2 * Tt:3 * Tt]
    a1 = p.op("vector", lambda v: v.tensor_scalar_mul(mean_ap, p_sum1[0:1, 0:Tt], 1.0 / D),
              after=(sums[-2],))
    a2 = p.op("vector", lambda v: v.tensor_scalar_mul(ex2_ap, p_sum2[0:1, 0:Tt], 1.0 / D),
              after=(sums[-1],))
    a3 = p.op("vector", lambda v: v.tensor_tensor(sc1_ap, mean_ap, mean_ap, ALU.mult),
              after=(a1,))
    a4 = p.op("vector", lambda v: v.tensor_tensor(ex2_ap, ex2_ap, sc1_ap, ALU.subtract),
              after=(a2, a3))
    st1 = p.op("vector", lambda v: v.tensor_scalar_add(ex2_ap, ex2_ap, eps),
               after=(a4,))
    r1 = p.op("scalar", lambda s: s.activation(sc1_ap, ex2_ap, AF.Ln), after=(st1, a3))
    st2 = p.op("scalar", lambda s: s.activation(ex2_ap, sc1_ap, AF.Exp, scale=-0.5),
               after=(r1,))
    bc1 = p.op("tensor", lambda te: te.matmul(p_bc1[:, 0:Tt], ones_r[0:1, :], stat[0:1, 0:Tt],
                                              start=True, stop=True), after=(st1,))
    bc2 = p.op("tensor", lambda te: te.matmul(p_bc2[:, 0:Tt], ones_r[0:1, :], stat[0:1, Tt:2 * Tt],
                                              start=True, stop=True), after=(st2,))
    outs = []
    nsc = bufs["nsc"][0]
    for i in range(PT):
        n1 = p.op("vector", lambda v, i=i: v.tensor_tensor(
            nsc[:, 0:Tt], x_tiles[i][:, 0:Tt], p_bc1[:, 0:Tt], ALU.subtract),
            after=(bc1,) + tuple(x_ready) + tuple(outs[-1:]))
        if xn32_tiles is None:
            n2 = p.op("vector", lambda v, i=i: v.tensor_tensor(
                xn_tiles[i][:, 0:Tt], nsc[:, 0:Tt], p_bc2[:, 0:Tt], ALU.mult),
                after=(n1, bc2))
        else:
            n2a = p.op("vector", lambda v, i=i: v.tensor_tensor(
                xn32_tiles[i][:, 0:Tt], nsc[:, 0:Tt], p_bc2[:, 0:Tt], ALU.mult),
                after=(n1, bc2) + tuple(x_ready))
            n2 = p.op("vector", lambda v, i=i: v.tensor_copy(
                xn_tiles[i][:, 0:Tt], xn32_tiles[i][:, 0:Tt]), after=(n2a,))
        outs.append(n2)
    if out_extra is not None:
        mean_row, rstd_row = out_extra
        sv1 = p.op("vector", lambda v: v.tensor_copy(mean_row, mean_ap), after=(st2,))
        sv2 = p.op("vector", lambda v: v.tensor_copy(rstd_row, ex2_ap), after=(st2,))
        outs += [sv1, sv2]
    return outs


def emit_l0_attn(p, nc, ctx, T, xb, qkvT, projT, pbias, attn_acc, xnb_out, entry=()):
    H, HD = 32, 32
    TT = 512
    NS = T // TT
    NB = T // 128
    EPS = 1e-15

    import ml_dtypes
    ones_c_h = nc.inline_tensor(np.ones((128, 1), ml_dtypes.bfloat16), name=_nm("onc"))
    ones_r_h = nc.inline_tensor(np.ones((1, 128), np.float32), name=_nm("onr"))
    ident_h = nc.inline_tensor(np.eye(128, dtype=ml_dtypes.bfloat16), name=_nm("idn"))

    with ExitStack() as c1:
        q_fm = [SB(nc, c1, [128, T], BF16, "qfm") for _ in range(PT)]
        vk_bf = SB(nc, c1, [128, 8 * 129], BF16, "vkbf")
        vk_rep = [SB(nc, c1, [128, 33], BF16, "vkrep") for _ in range(8)]

        with ExitStack() as cp:
            qkv_sb = SB(nc, cp, [128, PT * 3 * D], BF16, "qkvw")
            x_f = [[SB(nc, cp, [128, TT], F32, "x") for _ in range(PT)] for _ in range(2)]
            xn_bf = [[SB(nc, cp, [128, TT], BF16, "xn") for _ in range(PT)] for _ in range(2)]
            kv_tm = [SB(nc, cp, [128, 2048], BF16, "kvtm") for _ in range(2)]
            vk_acc = SB(nc, cp, [128, 8 * 129], F32, "vkacc")
            ones_c = SB(nc, cp, [128, 1], BF16, "oc")
            ones_r = SB(nc, cp, [1, 128], F32, "orr")
            lnbufs = dict(
                p_sum1=None, p_sum2=None, p_bc1=None, p_bc2=None,
                stat=SB(nc, cp, [1, 3 * TT], F32, "st"),
                xbf=[SB(nc, cp, [128, TT], BF16, "xc")],
                sq=SB(nc, cp, [128, TT], BF16, "sqq"),
                nsc=[SB(nc, cp, [128, TT], F32, "nsc")],
            )
            lnbufs["p_sum1"] = PS(nc, cp, [1, TT], F32, "ps1")
            lnbufs["p_sum2"] = PS(nc, cp, [1, TT], F32, "ps2")
            lnbufs["p_bc1"] = PS(nc, cp, [128, TT], F32, "pb1")
            lnbufs["p_bc2"] = PS(nc, cp, [128, TT], F32, "pb2")
            p_mm = [PS(nc, cp, [128, TT], F32, "pmm") for _ in range(2)]
            p_vk = [PS(nc, cp, [128, 129], F32, "pvk") for _ in range(2)]

            lw = load_weight_bf16(p, qkv_sb, qkvT, D, 3 * D, after=entry)
            lo = p.op("sync", lambda e: [
                e.dma_start(out=ones_c[:], in_=ones_c_h.ap()[:, :]),
                e.dma_start(out=ones_r[:], in_=ones_r_h.ap()[:, :])],
                units=2, dma=True, stream="cst", after=entry)
            lo2 = lo
            vz = p.op("vector", lambda v: v.memset(vk_acc[:], 0.0), after=entry)

            prev_ln = []
            prev_x_cons = [None, None]
            prev_xn_cons = [(), ()]
            prev_kv_cons = [None, None]
            vk_adds = []
            q_evs_all = []
            xnb_dmas = []
            for s in range(NS):
                par = s % 2
                xl = p.op("sync", lambda e, s=s, par=par: [
                    e.dma_start(out=x_f[par][i][:], in_=xb[128 * i:128 * (i + 1), TT * s:TT * (s + 1)])
                    for i in range(PT)], units=PT, dma=True,
                    stream=f"ax{par}",
                    after=((prev_x_cons[par],) if prev_x_cons[par] else ()) + entry)
                xn_view = [xn_bf[par][i][:] for i in range(PT)]
                x_view = [x_f[par][i] for i in range(PT)]
                louts = emit_ln(p, nc, ctx, x_view, xn_view, TT, ones_c, ones_r,
                                lnbufs, x_ready=(xl, lo, lo2) + tuple(prev_ln)
                                + tuple(prev_xn_cons[par]))
                prev_ln = louts
                prev_x_cons[par] = louts[-1]
                xnb_dmas.append(p.op("sync", lambda e, s=s, par=par: [
                    e.dma_start(out=xnb_out[128 * i:128 * (i + 1), TT * s:TT * (s + 1)],
                                in_=xn_bf[par][i][:]) for i in range(PT)],
                    units=PT, dma=True, stream="xnbo", after=tuple(louts)))
                for o in range(PT):
                    pb = p_mm[o % 2]
                    mm = mm_chain(p, pb[:, 0:TT],
                                  [wtile(qkv_sb, i, o, 3 * D) for i in range(PT)],
                                  [xn_bf[par][i][:] for i in range(PT)],
                                  after=tuple(louts) + (lw,)
                                  + ((q_evs_all[-2],) if len(q_evs_all) >= 2 else ())
                                  + ((vk_adds[-1],) if vk_adds else ()))
                    ev = p.op("scalar", lambda sc, o=o, s=s, pb=pb: sc.activation(
                        q_fm[o][:, TT * s:TT * (s + 1)], pb[:, 0:TT], AF.Relu),
                        after=(mm,))
                    q_evs_all.append(ev)
                for bb in range(4):
                    b = 4 * s + bb
                    kpar = b % 2
                    kv_evs = []
                    for ot in range(4):
                        pb = p_mm[ot % 2]
                        def kvmm_fn(te, bb=bb, s=s, ot=ot, pb=pb, par=par):
                            ins = None
                            for i in range(PT):
                                ins = te.matmul(
                                    pb[:, 0:TT],
                                    xn_bf[par][i][:, 128 * bb: 128 * bb + 128],
                                    qkv_sb[:, i * 3 * D + 1024 + 512 * ot: i * 3 * D + 1024 + 512 * ot + 512],
                                    start=(i == 0), stop=(i == PT - 1))
                            return ins
                        war = (q_evs_all[-1],) if (bb == 0 and ot < 2) else ()
                        war2 = (kv_evs[-2],) if len(kv_evs) >= 2 else ()
                        mm = p.op("tensor", kvmm_fn,
                                  after=tuple(louts) + war + war2 + (lw,) +
                                  ((prev_kv_cons[kpar],) if prev_kv_cons[kpar] else ()))
                        ev = p.op("scalar", lambda sc, ot=ot, kpar=kpar, pb=pb: sc.activation(
                            kv_tm[kpar][:, 512 * ot:512 * ot + 512], pb[:, 0:TT], AF.Relu),
                            after=(mm,))
                        kv_evs.append(ev)
                    pack_adds = []
                    for pk in range(8):
                        pvk = p_vk[pk % 2]
                        def vk_fn(te, pk=pk, kpar=kpar, pvk=pvk):
                            i1 = te.matmul(pvk[:, 0:128], kv_tm[kpar][:, 128 * pk:128 * pk + 128],
                                           kv_tm[kpar][:, 1024 + 128 * pk:1024 + 128 * pk + 128],
                                           start=True, stop=False)
                            return te.matmul(pvk[:, 128:129], kv_tm[kpar][:, 128 * pk:128 * pk + 128],
                                             ones_c[:, 0:1], start=False, stop=True)
                        war = (vk_adds[-2],) if len(vk_adds) >= 2 else ()
                        mm = p.op("tensor", vk_fn, after=tuple(kv_evs) + war + (lo, vz))
                        ad = p.op("vector", lambda v, pk=pk, pvk=pvk: v.tensor_tensor(
                            vk_acc[:, 129 * pk:129 * (pk + 1)], vk_acc[:, 129 * pk:129 * (pk + 1)],
                            pvk[:, 0:129], ALU.add), after=(mm,))
                        vk_adds.append(ad)
                        pack_adds.append(ad)
                    prev_kv_cons[kpar] = pack_adds[-1]
                prev_xn_cons[par] = (xnb_dmas[-1], vk_adds[-1], q_evs_all[-1])

            vkc = p.op("vector", lambda v: v.tensor_copy(vk_bf[:], vk_acc[:]),
                       after=tuple(vk_adds[-8:]))
            def rep_fn(e):
                outl = []
                for pk in range(8):
                    for j in range(4):
                        outl.append(e.dma_start(
                            out=vk_rep[pk][32 * j:32 * j + 32, 0:32],
                            in_=vk_bf[32 * j:32 * j + 32, 129 * pk + 32 * j:129 * pk + 32 * j + 32]))
                        outl.append(e.dma_start(
                            out=vk_rep[pk][32 * j:32 * j + 32, 32:33],
                            in_=vk_bf[32 * j:32 * j + 32, 129 * pk + 128:129 * pk + 129]))
                return outl
            rep_all = p.op("sync", rep_fn, units=64, dma=True, stream="vkrep",
                           after=(vkc,))
            rep_dmas = [rep_all] * 8

        xnb_pass1_done = (q_evs_all[-1], xnb_dmas[-1])
        with ExitStack() as cp:
            projw = SB(nc, cp, [128, PT * D], BF16, "projw")
            pb_sb = SB(nc, cp, [128, PT], F32, "pbias")
            pi_tm = SB(nc, cp, [128, 4 * D], BF16, "pitm")
            pi_fm = [SB(nc, cp, [128, TT], BF16, "pifm") for _ in range(PT)]
            rcol = SB(nc, cp, [128, 8], F32, "rcol")
            ident = SB(nc, cp, [128, 128], BF16, "ident")
            xo_f = [SB(nc, cp, [128, TT], F32, "xo") for _ in range(2)]
            ao_f = [SB(nc, cp, [128, TT], F32, "ao") for _ in range(2)]

            p_h = [PS(nc, cp, [128, 33], F32, "ph") for _ in range(4)]
            p_tp = PS(nc, cp, [128, 128], BF16, "ptp")
            p_pr = [PS(nc, cp, [128, TT], F32, "ppr") for _ in range(2)]

            lw2 = load_weight_bf16(p, projw, projT, D, D,
                                   after=tuple(vk_adds[-1:]) + tuple(rep_dmas) + xnb_pass1_done)
            lpb = p.op("sync", lambda e: [
                e.dma_start(out=pb_sb[:], in_=pbias[:, :]),
                e.dma_start(out=ident[:], in_=ident_h.ap()[:, :])],
                units=2, dma=True, stream="cst", after=(vkc, rep_dmas[-1]))
            lid = lpb

            prev_pi_cons = None
            last_ad = [None, None]
            last_bi = [None, None]
            last_st = [None, None]
            last_proj = None
            hist = []
            for s in range(NS):
                pi_ops = []
                for bb in range(4):
                    b = 4 * s + bb
                    for pk in range(8):
                        deps = [rep_dmas[pk]]
                        if prev_pi_cons is not None:
                            deps.append(prev_pi_cons)
                        mm4 = []
                        for j in range(4):
                            def h_fn2(te, pk=pk, j=j, bb=bb, s=s):
                                lhsT = q_fm[pk][32 * j:32 * j + 32,
                                                TT * s + 128 * bb: TT * s + 128 * bb + 128]
                                return te.matmul(p_h[j][:, 0:33], lhsT,
                                                 vk_rep[pk][32 * j:32 * j + 32, 0:33],
                                                 start=True, stop=True,
                                                 tile_position=(32 * j, 0))
                            mm4.append(p.op("tensor", h_fn2,
                                            after=tuple(deps) + tuple(hist[-8:])))
                        for j in range(4):
                            h = 4 * pk + j
                            c1o = p.op("vector", lambda v, j=j: v.tensor_scalar_add(
                                rcol[:, j:j + 1], p_h[j][:, 32:33], EPS),
                                after=(mm4[j],) + tuple(hist[-4:]))
                            c2o = p.op("vector", lambda v, j=j: v.reciprocal(
                                rcol[:, 4 + j:5 + j], rcol[:, j:j + 1]), after=(c1o,))
                            c3o = p.op("vector", lambda v, j=j, h=h, bb=bb: v.tensor_scalar(
                                pi_tm[:, 1024 * bb + 32 * h:1024 * bb + 32 * h + 32],
                                p_h[j][:, 0:32], rcol[:, 4 + j:5 + j], None, ALU.mult),
                                after=(c2o,))
                            pi_ops.append(c3o)
                            hist.append(c3o)
                tps = []
                for bb in range(4):
                    for o in range(PT):
                        tp = p.op("tensor", lambda te, bb=bb, o=o: te.transpose(
                            p_tp[:, 0:128], pi_tm[:, 1024 * bb + 128 * o:1024 * bb + 128 * o + 128],
                            ident[:]), after=tuple(pi_ops) + ((tps[-1],) if tps else ()) + (lid,))
                        ev = p.op("vector", lambda v, bb=bb, o=o: v.tensor_copy(
                            pi_fm[o][:, 128 * bb:128 * bb + 128], p_tp[:, 0:128]),
                            after=(tp,) + ((last_proj,) if (bb == 0 and o == 0 and last_proj) else ()))
                        tps.append(ev)
                prev_pi_cons = tps[-1]
                for o in range(PT):
                    par = o % 2
                    ppr = p_pr[par]
                    xlo = p.op("sync", lambda e, s=s, o=o, par=par: [
                        e.dma_start(out=xo_f[par][:, :],
                                    in_=xb[128 * o:128 * (o + 1), TT * s:TT * (s + 1)])],
                        units=1, dma=True, stream=f"xo{par}",
                        after=(last_ad[par],) if last_ad[par] else ())
                    mm = mm_chain(p, ppr[:, 0:TT],
                                  [wtile(projw, i, o, D) for i in range(PT)],
                                  [pi_fm[i][:, 0:TT] for i in range(PT)],
                                  after=tuple(tps) + (lw2,)
                                  + ((last_bi[par],) if last_bi[par] else ()))
                    bi = p.op("scalar", lambda sc, o=o, par=par, ppr=ppr: sc.activation(
                        ao_f[par][:, 0:TT], ppr[:, 0:TT], AF.Identity, bias=pb_sb[:, o:o + 1]),
                        after=(mm, lpb) + ((last_st[par],) if last_st[par] else ()))
                    ad = p.op("vector", lambda v, par=par: v.tensor_tensor(
                        ao_f[par][:, 0:TT], ao_f[par][:, 0:TT], xo_f[par][:, 0:TT], ALU.add),
                        after=(bi, xlo))
                    st = p.op("sync", lambda e, o=o, s=s, par=par: [
                        e.dma_start(out=attn_acc[128 * o:128 * (o + 1), TT * s:TT * (s + 1)],
                                    in_=ao_f[par][:, 0:TT])], units=1, dma=True,
                        stream="aout", after=(ad,))
                    last_ad[par] = ad
                    last_bi[par] = bi
                    last_st[par] = st
                last_proj = st
            return [st] + xnb_dmas


def emit_l0_ffn(p, nc, ctx, R, NSTEP, xr, wT, bcol, cuo_out, entry=()):
    import ml_dtypes
    ones_c_h = nc.inline_tensor(np.ones((128, 1), ml_dtypes.bfloat16), name=_nm("onc"))
    ones_r_h = nc.inline_tensor(np.ones((1, 128), np.float32), name=_nm("onr"))

    with ExitStack() as c1:
        w_sb = {k: SB(nc, c1, [128, PT * D], BF16, k) for k in
                ["wfT", "wiT", "wcT", "ufT", "uiT", "ucT", "uoT"]}
        b_sb = SB(nc, c1, [128, 4 * PT], F32, "bc")
        xr_f = [[SB(nc, c1, [128, R], F32, "xr") for _ in range(PT)]]
        xr_f.append(xr_f[0])
        xn_bf = [[SB(nc, c1, [128, R], BF16, "xn") for _ in range(PT)] for _ in range(2)]
        c_f = [SB(nc, c1, [128, R], F32, "c") for _ in range(PT)]
        c_bf = [SB(nc, c1, [128, R], BF16, "cb") for _ in range(PT)]
        g_f = [SB(nc, c1, [128, R], F32, "gf") for _ in range(2)]
        g_i = [SB(nc, c1, [128, R], F32, "gi") for _ in range(2)]
        g_c = [SB(nc, c1, [128, R], F32, "gc") for _ in range(2)]
        cuo_sb = SB(nc, c1, [128, R], F32, "cuo")
        ones_c = SB(nc, c1, [128, 1], BF16, "oc")
        ones_r = SB(nc, c1, [1, 128], F32, "orr")
        lnbufs = dict(
            stat=SB(nc, c1, [1, 3 * R], F32, "st"),
            xbf=[SB(nc, c1, [128, R], BF16, "xc")],
            sq=SB(nc, c1, [128, R], BF16, "sqq"),
            nsc=[SB(nc, c1, [128, R], F32, "nsc")],
        )
        lnbufs["p_sum1"] = PS(nc, c1, [1, R], F32, "ps1")
        lnbufs["p_sum2"] = PS(nc, c1, [1, R], F32, "ps2")
        lnbufs["p_bc1"] = PS(nc, c1, [128, R], F32, "pb1")
        lnbufs["p_bc2"] = PS(nc, c1, [128, R], F32, "pb2")
        p_g = [PS(nc, c1, [128, R], F32, "pg") for _ in range(3)]

        wkeys = list(w_sb.keys())
        def lw_all(e):
            outl = []
            for k in wkeys:
                src = wT[k].rearrange("(i p) o -> p i o", p=128)
                dst = w_sb[k][:].rearrange("p (i o) -> p i o", o=D)
                outl.append(e.dma_start(out=dst, in_=src))
            return outl
        lw_op = p.op("gpsimd", lw_all, units=len(wkeys), dma=True, stream="wld",
                     after=entry)
        lws = {k: lw_op for k in wkeys}
        lb = p.dma("gpsimd", b_sb[:], None, stream="cst") if False else None
        def lb_fn(e):
            return [e.dma_start(out=b_sb[:, PT * j:PT * (j + 1)], in_=bcol[k][:, :])
                    for j, k in enumerate(["bfc", "bic", "bcc", "incc"])]
        lb = p.op("gpsimd", lb_fn, units=4, dma=True, stream="cst", after=entry)
        lo = p.op("sync", lambda e: [
            e.dma_start(out=ones_c[:], in_=ones_c_h.ap()[:, :]),
            e.dma_start(out=ones_r[:], in_=ones_r_h.ap()[:, :])],
            units=2, dma=True, stream="cst", after=entry)
        lo2 = lo

        initc = []
        for m in range(PT):
            z = p.op("vector", lambda v, m=m: v.memset(c_f[m][:], 0.0), after=entry)
            a = p.op("scalar", lambda sc, m=m: sc.activation(
                c_f[m][:], c_f[m][:], AF.Identity, bias=b_sb[:, 3 * PT + m:3 * PT + m + 1]),
                after=(z, lb))
            cb = p.op("vector", lambda v, m=m: v.tensor_copy(c_bf[m][:], c_f[m][:]),
                      after=(a,))
            initc.append(cb)

        prev_ln = []
        prev_x_cons = [None, None]
        last_xn_cons = [None, None]
        last_cb = {m: initc[m] for m in range(PT)}
        last_gev = {}
        gate_names = [("wfT", "ufT", 0), ("wiT", "uiT", 1), ("wcT", "ucT", 2)]
        for t in range(NSTEP):
            par = t % 2
            xl = p.op("sync", lambda e, t=t, par=par: [
                e.dma_start(out=xr_f[par][i][:], in_=xr[t, 128 * i:128 * (i + 1), :])
                for i in range(PT)], units=PT, dma=True, stream="xr",
                after=tuple(c for c in prev_x_cons if c) + entry)
            xr_dep = (xl, lo, lo2) + tuple(prev_ln)
            if last_xn_cons[par] is not None:
                xr_dep = xr_dep + (last_xn_cons[par],)
            louts = emit_ln(p, nc, ctx, xr_f[par], [xn_bf[par][i][:] for i in range(PT)],
                            R, ones_c, ones_r, lnbufs, x_ready=xr_dep)
            prev_ln = louts
            prev_x_cons[par] = louts[-1]
            u3s = {}
            step_mms = []
            for m in range(PT):
                gevs = []
                for (wname, uname, gi) in gate_names:
                    pg = p_g[gi]
                    def gate_mm(te, m=m, wname=wname, uname=uname, par=par, pg=pg):
                        ins = None
                        for i in range(PT):
                            ins = te.matmul(pg[:, 0:R], wtile(w_sb[wname], i, m, D),
                                            xn_bf[par][i][:], start=(i == 0), stop=False)
                        for i in range(PT):
                            ins = te.matmul(pg[:, 0:R], wtile(w_sb[uname], i, m, D),
                                            c_bf[i][:], start=False, stop=(i == PT - 1))
                        return ins
                    war = (last_gev[gi],) if gi in last_gev else ()
                    mm = p.op("tensor", gate_mm,
                              after=tuple(louts) + tuple(last_cb.values()) + war
                              + (lws[wname], lws[uname]))
                    step_mms.append(mm)
                    dst = (g_f, g_i, g_c)[gi]
                    func = (AF.Sigmoid, AF.Sigmoid, AF.Tanh)[gi]
                    ev = p.op("scalar", lambda sc, m=m, gi=gi, dst=dst, func=func, pg=pg: sc.activation(
                        dst[m % 2][:], pg[:, 0:R], func,
                        bias=b_sb[:, gi * PT + m:gi * PT + m + 1]),
                        after=(mm, lb) + ((u3s[m - 2],) if m - 2 in u3s else ()))
                    gevs.append(ev)
                    last_gev[gi] = ev
                u1 = p.op("vector", lambda v, m=m: v.tensor_tensor(
                    g_i[m % 2][:], g_i[m % 2][:], g_c[m % 2][:], ALU.mult),
                    after=tuple(gevs))
                u2 = p.op("vector", lambda v, m=m: v.tensor_tensor(
                    c_f[m][:], g_f[m % 2][:], c_f[m][:], ALU.mult),
                    after=(gevs[0], gevs[1]))
                u3 = p.op("vector", lambda v, m=m: v.tensor_tensor(
                    c_f[m][:], c_f[m][:], g_i[m % 2][:], ALU.add), after=(u1, u2))
                u3s[m] = u3
            last_xn_cons[par] = step_mms[-1]
            new_cb = {}
            for m in range(PT):
                cb = p.op("vector", lambda v, m=m: v.tensor_copy(c_bf[m][:], c_f[m][:]),
                          after=(u3s[m], step_mms[-1]))
                new_cb[m] = cb
            last_cb = new_cb
        sts = []
        last_ev = None
        for m in range(PT):
            mm = mm_chain(p, p_g[0][:, 0:R],
                          [wtile(w_sb["uoT"], i, m, D) for i in range(PT)],
                          [c_bf[i][:] for i in range(PT)],
                          after=tuple(last_cb.values()) + (lws["uoT"],)
                          + ((last_ev,) if last_ev else ()) + (last_gev[0],))
            ev = p.op("vector", lambda v, m=m: v.tensor_copy(cuo_sb[:], p_g[0][:, 0:R]),
                      after=(mm,) + ((sts[-1],) if sts else ()))
            last_ev = ev
            st = p.op("sync", lambda e, m=m: [
                e.dma_start(out=cuo_out[128 * m:128 * (m + 1), :], in_=cuo_sb[:])],
                units=1, dma=True, stream="cuoo", after=(ev,))
            sts.append(st)
        return sts


def build_L0A(T=4096, R=512, NSTEP=8):
    nc = bass.Bass("TRN2", target_bir_lowering=False)
    xb = nc.dram_tensor("xb", [D, T], F32, kind="ExternalInput").ap()
    xr = nc.dram_tensor("xr", [NSTEP, D, R], F32, kind="ExternalInput").ap()
    qkvT = nc.dram_tensor("qkvT", [D, 3 * D], F32, kind="ExternalInput").ap()
    projT = nc.dram_tensor("projT", [D, D], F32, kind="ExternalInput").ap()
    pbias = nc.dram_tensor("pbias", [128, PT], F32, kind="ExternalInput").ap()
    wT = {k: nc.dram_tensor(k, [D, D], F32, kind="ExternalInput").ap()
          for k in ["wfT", "wiT", "wcT", "ufT", "uiT", "ucT", "uoT"]}
    bcol = {k: nc.dram_tensor(k, [128, PT], F32, kind="ExternalInput").ap()
            for k in ["bfc", "bic", "bcc", "incc"]}
    attn_acc = nc.dram_tensor("attn_acc", [D, T], F32, kind="ExternalOutput").ap()
    xnb_out = nc.dram_tensor("xnb", [D, T], BF16, kind="ExternalOutput").ap()
    cuo_out = nc.dram_tensor("cuo", [D, R], F32, kind="ExternalOutput").ap()

    with ExitStack() as ctx:
        p = Prog(nc, ctx)
        o1 = emit_l0_ffn(p, nc, ctx, R, NSTEP, xr, wT, bcol, cuo_out)
        o2 = emit_l0_attn(p, nc, ctx, T, xb, qkvT, projT, pbias, attn_acc, xnb_out,
                          entry=tuple(o1))
        p.op("gpsimd", lambda g: g.nop(), after=tuple(o1) + tuple(o2))
        p.emit()
    return nc


def emit_l0b(p, nc, ctx, T, attn_acc, xnb, cuo, woT, bocol, x1_out, entry=()):
    TT = 512
    NS = T // TT
    with ExitStack() as c1:
        wo_sb = SB(nc, c1, [128, PT * D], BF16, "wow")
        bo_sb = SB(nc, c1, [128, PT], F32, "boc")
        xn_t = [[SB(nc, c1, [128, TT], BF16, "xnt") for _ in range(PT)] for _ in range(2)]
        cuo_t = [SB(nc, c1, [128, TT], F32, "cuot") for _ in range(2)]
        acc_t = [SB(nc, c1, [128, TT], F32, "acct") for _ in range(2)]
        h_t = [SB(nc, c1, [128, TT], F32, "ht") for _ in range(2)]
        p_mm = [PS(nc, c1, [128, TT], F32, "pmm") for _ in range(2)]

        lw = load_weight_bf16(p, wo_sb, woT, D, D, after=entry)
        lb = p.dma("sync", bo_sb[:], bocol[:, :], stream="cst", after=entry)

        last_xn_cons = [None, None]
        last_h = [None, None]
        last_st = None
        for s in range(NS):
            par = s % 2
            xl = p.op("sync", lambda e, s=s, par=par: [
                e.dma_start(out=xn_t[par][i][:], in_=xnb[128 * i:128 * (i + 1), TT * s:TT * (s + 1)])
                for i in range(PT)], units=PT, dma=True, stream=f"bx{par}",
                after=((last_xn_cons[par],) if last_xn_cons[par] else ()) + entry)
            for o in range(PT):
                opar = o % 2
                pb = p_mm[opar]
                cl = p.op("gpsimd", lambda e, s=s, o=o, opar=opar: [
                    e.dma_start(out=cuo_t[opar][:], in_=cuo[128 * o:128 * (o + 1), TT * s:TT * (s + 1)]),
                    e.dma_start(out=acc_t[opar][:], in_=attn_acc[128 * o:128 * (o + 1), TT * s:TT * (s + 1)])],
                    units=2, dma=True, stream=f"bc{opar}",
                    after=((last_h[opar],) if last_h[opar] else ()) + entry)
                mm = mm_chain(p, pb[:, 0:TT],
                              [wtile(wo_sb, i, o, D) for i in range(PT)],
                              [xn_t[par][i][:] for i in range(PT)],
                              after=(xl, lw) + ((last_h[opar],) if last_h[opar] else ()))
                if o == PT - 1:
                    last_xn_cons[par] = mm
                a1 = p.op("vector", lambda v, opar=opar, pb=pb: v.tensor_tensor(
                    h_t[opar][:], pb[:, 0:TT], cuo_t[opar][:], ALU.add), after=(mm, cl))
                a2 = p.op("scalar", lambda sc, o=o, opar=opar: sc.activation(
                    h_t[opar][:], h_t[opar][:], AF.Tanh, bias=bo_sb[:, o:o + 1]),
                    after=(a1, lb))
                a3 = p.op("vector", lambda v, opar=opar: v.tensor_tensor(
                    h_t[opar][:], h_t[opar][:], acc_t[opar][:], ALU.add), after=(a2, cl))
                st = p.op("sync", lambda e, o=o, s=s, opar=opar: [
                    e.dma_start(out=x1_out[128 * o:128 * (o + 1), TT * s:TT * (s + 1)],
                                in_=h_t[opar][:])], units=1, dma=True,
                    stream="x1o", after=(a3,))
                last_h[opar] = st
                last_st = st
        return [last_st]


def build_L0B(T=4096):
    nc = bass.Bass("TRN2", target_bir_lowering=False)
    attn_acc = nc.dram_tensor("attn_acc", [D, T], F32, kind="ExternalInput").ap()
    xnb = nc.dram_tensor("xnb", [D, T], BF16, kind="ExternalInput").ap()
    cuo = nc.dram_tensor("cuo", [D, T], F32, kind="ExternalInput").ap()
    woT = nc.dram_tensor("woT", [D, D], F32, kind="ExternalInput").ap()
    bocol = nc.dram_tensor("bocol", [128, PT], F32, kind="ExternalInput").ap()
    x1 = nc.dram_tensor("x1", [D, T], F32, kind="ExternalOutput").ap()
    with ExitStack() as ctx:
        p = Prog(nc, ctx)
        o = emit_l0b(p, nc, ctx, T, attn_acc, xnb, cuo, woT, bocol, x1)
        p.op("gpsimd", lambda g: g.nop(), after=tuple(o))
        p.emit()
    return nc


def _l3_host_consts(T):
    import math
    NUM_BUCKETS, MAX_DISTANCE = 32, 128
    d = np.arange(-(T - 1), T)
    nb = NUM_BUCKETS // 2
    rel = (d > 0).astype(np.int64) * nb
    rp = np.abs(d)
    max_exact = nb // 2
    is_small = rp < max_exact
    rpl = max_exact + (np.log(np.maximum(rp, 1).astype(np.float64) / max_exact)
                       / math.log(MAX_DISTANCE / max_exact) * (nb - max_exact)).astype(np.int64)
    rpl = np.minimum(rpl, nb - 1)
    buckets = rel + np.where(is_small, rp, rpl)
    onehot = np.zeros((2 * T - 1, 32), np.float32)
    onehot[np.arange(2 * T - 1), buckets] = 1.0
    return onehot


def emit_l3_phase1(p, nc, ctx, T, x3, wqkvT, relbias, wgate, bgate, ohp_rows,
                   entry=()):
    H, HD = 16, 64
    TT = 512
    NS = T // TT
    NB = T // 128
    ND = 2 * NB - 1
    DFF = 4096
    import ml_dtypes

    onehot_np = _l3_host_consts(T)
    npad = ceil_div(2 * T - 1, 128) * 128
    oh_pad = np.zeros((npad, 32), np.float32)
    oh_pad[:2 * T - 1] = onehot_np
    oh_h = nc.inline_tensor(oh_pad.astype(ml_dtypes.bfloat16), name=_nm("l3oh"))
    ones_c_h = nc.inline_tensor(np.ones((128, 1), ml_dtypes.bfloat16), name=_nm("onc"))
    ones_r_h = nc.inline_tensor(np.ones((1, 128), np.float32), name=_nm("onr"))
    nident_h = nc.inline_tensor((-np.eye(128)).astype(ml_dtypes.bfloat16), name=_nm("nid"))
    ident_h = nc.inline_tensor(np.eye(128, dtype=ml_dtypes.bfloat16), name=_nm("idn"))

    qfm_d = nc.dram_tensor(_nm("l3q"), [D, T], BF16, kind="Internal").ap()
    kfm_d = nc.dram_tensor(_nm("l3k"), [D, T], BF16, kind="Internal").ap()
    vtm_d = nc.dram_tensor(_nm("l3v"), [T, D], BF16, kind="Internal").ap()
    xn_d = nc.dram_tensor(_nm("l3xn"), [D, T], BF16, kind="Internal").ap()
    th_d = nc.dram_tensor(_nm("l3th"), [H, npad], F32, kind="Internal").ap()
    aacc_d = nc.dram_tensor(_nm("l3aa"), [D, T], F32, kind="Internal").ap()

    with ExitStack() as c1:
        qkv_sb = SB(nc, c1, [128, PT * 3 * D], BF16, "qkvw")
        x_f = [[SB(nc, c1, [128, TT], F32, "x") for _ in range(PT)] for _ in range(2)]
        xn_bf = [[SB(nc, c1, [128, TT], BF16, "xn") for _ in range(PT)] for _ in range(2)]
        ones_c = SB(nc, c1, [128, 1], BF16, "oc")
        ones_r = SB(nc, c1, [1, 128], F32, "orr")
        lnbufs = dict(
            stat=SB(nc, c1, [1, 3 * TT], F32, "st"),
            xbf=[SB(nc, c1, [128, TT], BF16, "xc")],
            sq=SB(nc, c1, [128, TT], BF16, "sqq"),
            nsc=[SB(nc, c1, [128, TT], F32, "nsc")],
        )
        oh_sb = SB(nc, c1, [128, (npad // 128) * 32], BF16, "ohs")
        ev_sb = [SB(nc, c1, [128, TT], BF16, "evb") for _ in range(2)]
        xn32 = [SB(nc, c1, [128, TT], F32, "xn32") for _ in range(PT)]
        wg_sb = SB(nc, c1, [128, PT * 8], F32, "wg")
        bg_sb = SB(nc, c1, [1, 8], F32, "bg")
        idf_sb = SB(nc, c1, [128, 128], F32, "idf")
        rbufs = dict(p_lg=None, p_tp=None, lg=SB(nc, c1, [128, 18], F32, "lg"))
        idf_h = nc.inline_tensor(np.eye(128, dtype=np.float32), name=_nm("idf"))
        rb_sb = SB(nc, c1, [32, H], BF16, "rbs")

        lnbufs["p_sum1"] = PS(nc, c1, [1, TT], F32, "ps1")
        lnbufs["p_sum2"] = PS(nc, c1, [1, TT], F32, "ps2")
        lnbufs["p_bc1"] = PS(nc, c1, [128, TT], F32, "pb1")
        lnbufs["p_bc2"] = PS(nc, c1, [128, TT], F32, "pb2")
        p_mm = [PS(nc, c1, [128, TT], F32, "pmm") for _ in range(2)]
        rbufs["p_lg"] = PS(nc, c1, [128, 8], F32, "plg")
        rbufs["p_tp"] = PS(nc, c1, [8, 128], F32, "ptp")

        lw = load_weight_bf16(p, qkv_sb, wqkvT, D, 3 * D, after=entry)
        lwg = p.op("sync", lambda e: [
            e.dma_start(out=wg_sb[:].rearrange("p (i o) -> p i o", o=8),
                        in_=wgate.rearrange("(i p) o -> p i o", p=128)),
            e.dma_start(out=bg_sb[:], in_=bgate[:, :]),
            e.dma_start(out=idf_sb[:], in_=idf_h.ap()[:, :])],
            units=3, dma=True, stream="cst", after=entry)
        lo = p.op("sync", lambda e: [
            e.dma_start(out=ones_c[:], in_=ones_c_h.ap()[:, :]),
            e.dma_start(out=ones_r[:], in_=ones_r_h.ap()[:, :]),
            e.dma_start(out=oh_sb[:].rearrange("p (n c) -> p n c", c=32),
                        in_=oh_h.ap().rearrange("(n p) c -> p n c", p=128)),
            ], units=3, dma=True, stream="cst", after=entry)
        lrb = p.dma("gpsimd", rb_sb[:], relbias.rearrange("b h -> b h"),
                    stream="cst", after=entry)

        thdmas = []
        with ExitStack() as c2:
            p_oht = p_mm[0][0:32, 0:64].bitcast(BF16)
            p_thh = p_mm[1][0:16, 0:128]
            ohT_sb = SB(nc, c2, [32, 128], BF16, "ohT")
            th_all = SB(nc, c2, [16, npad], BF16, "thall")
            idb = SB(nc, c2, [128, 128], BF16, "idb")
            lid = p.dma("sync", idb[:], ident_h.ap()[:, :], stream="cst", after=entry)
            prev_ev = None
            prev_mm = None
            for ch in range(npad // 128):
                tp = p.op("tensor", lambda te, ch=ch: te.transpose(
                    p_oht, oh_sb[:, 32 * ch:32 * ch + 32], idb[:]),
                    after=(lo, lid) + ((prev_ev,) if prev_ev else ()))
                ev = p.op("vector", lambda v: v.tensor_copy(ohT_sb[:], p_oht),
                          after=(tp,) + ((prev_mm,) if prev_mm else ()))
                mm = p.op("tensor", lambda te, ch=ch: te.matmul(
                    p_thh, rb_sb[:, :], ohT_sb[:], start=True, stop=True),
                    after=(ev, lrb))
                cp = p.op("vector", lambda v, ch=ch: v.tensor_copy(
                    th_all[:, 128 * ch:128 * ch + 128], p_thh), after=(mm,))
                prev_ev = cp
                prev_mm = cp
            thd = p.op("gpsimd", lambda e: [e.dma_start(out=th_d[:, :], in_=th_all[:, :])],
                       units=1, dma=True, stream="thd", after=(prev_ev,))
            thdmas.append(thd)

        prev_ln = [thd]
        prev_rt = []
        prev_x_cons = [None, None]
        prev_xn_cons = [(), ()]
        evs = []
        outd = []
        for s in range(NS):
            par = s % 2
            xl = p.op("sync", lambda e, s=s, par=par: [
                e.dma_start(out=x_f[par][i][:], in_=x3[128 * i:128 * (i + 1), TT * s:TT * (s + 1)])
                for i in range(PT)], units=PT, dma=True, stream=f"ax{par}",
                after=((prev_x_cons[par],) if prev_x_cons[par] else ()) + entry)
            xn_view = [xn_bf[par][i][:] for i in range(PT)]
            louts = emit_ln(p, nc, ctx, [x_f[par][i] for i in range(PT)], xn_view, TT,
                            ones_c, ones_r, lnbufs, x_ready=(xl, lo) + tuple(prev_ln)
                            + tuple(prev_xn_cons[par]),
                            xn32_tiles=xn32)
            prev_ln = list(louts)
            prev_x_cons[par] = louts[-1]
            for bb in range(4):
                rv = emit_router_tm(
                    p, nc, ctx, 8, 128,
                    [xn32[i][:, 128 * bb:128 * bb + 128] for i in range(PT)],
                    wg_sb, bg_sb, ones_r,
                    ohp_rows[0:8, TT * s + 128 * bb: TT * s + 128 * bb + 128],
                    rbufs, deps=tuple(louts) + (lwg, lo) + tuple(prev_rt), ident_f32=idf_sb)
                prev_rt = [rv]
                prev_ln.append(rv)
            outd.append(p.op("sync", lambda e, s=s, par=par: [
                e.dma_start(out=xn_d[128 * i:128 * (i + 1), TT * s:TT * (s + 1)],
                            in_=xn_bf[par][i][:]) for i in range(PT)],
                units=PT, dma=True, stream="xnd", after=tuple(louts)))
            for idx, (lo_, hi_, dst, scale) in enumerate(
                    [(0, D, qfm_d, 0.125), (D, 2 * D, kfm_d, 1.0)]):
                for o in range(PT):
                    pb = p_mm[(len(evs)) % 2]
                    mm = mm_chain(p, pb[:, 0:TT],
                                  [wtile(qkv_sb, i, 0, 3 * D, width=3 * D)[:, lo_ + 128 * o: lo_ + 128 * o + 128]
                                   for i in range(PT)],
                                  [xn_bf[par][i][:] for i in range(PT)],
                                  after=tuple(louts) + (lw,)
                                  + ((evs[-2],) if len(evs) >= 2 else ()))
                    epar = len(evs) % 2
                    ev = p.op("scalar", lambda sc, pb=pb, epar=epar, scale=scale: sc.activation(
                        ev_sb[epar][:, 0:TT], pb[:, 0:TT], AF.Identity, scale=scale),
                        after=(mm,) + ((evs[-2],) if len(evs) >= 2 else ()))
                    st = p.op("sync", lambda e, o=o, s=s, dst=dst, epar=epar: [
                        e.dma_start(out=dst[128 * o:128 * (o + 1), TT * s:TT * (s + 1)],
                                    in_=ev_sb[epar][:, 0:TT])], units=1, dma=True,
                        stream="qkout", after=(ev,))
                    evs.append(st)
            for bb in range(4):
                for half in range(2):
                    pb = p_mm[(2 * bb + half) % 2]
                    def vmm2(te, bb=bb, s=s, half=half, pb=pb, par=par):
                        ins = None
                        for i in range(PT):
                            ins = te.matmul(
                                pb[:, 0:TT],
                                xn_bf[par][i][:, 128 * bb: 128 * bb + 128],
                                qkv_sb[:, i * 3 * D + 2 * D + 512 * half: i * 3 * D + 2 * D + 512 * half + 512],
                                start=(i == 0), stop=(i == PT - 1))
                        return ins
                    mm = p.op("tensor", vmm2, after=tuple(louts) + (lw,)
                              + ((evs[-2],) if len(evs) >= 2 else ()))
                    epar = len(evs) % 2
                    ev = p.op("scalar", lambda sc, pb=pb, epar=epar: sc.activation(
                        ev_sb[epar][:, 0:TT], pb[:, 0:TT], AF.Identity),
                        after=(mm,) + ((evs[-2],) if len(evs) >= 2 else ()))
                    st = p.op("sync", lambda e, bb=bb, s=s, half=half, epar=epar: [
                        e.dma_start(out=vtm_d[TT * s + 128 * bb:TT * s + 128 * bb + 128,
                                              512 * half:512 * half + 512],
                                    in_=ev_sb[epar][:, 0:TT])], units=1, dma=True,
                        stream="qkout", after=(ev,))
                    evs.append(st)
            prev_xn_cons[par] = (outd[-1], evs[-1])
        phase1_done = tuple(evs[-4:]) + (outd[-1], thdmas[-1])
    return phase1_done, (qfm_d, kfm_d, vtm_d, xn_d, th_d, aacc_d)


def emit_l3_attn_heads(p, nc, ctx, T, scratch, wprojT, pbcol, x3, entry=()):
    qfm_d, kfm_d, vtm_d, xn_d, th_d, aacc_d = scratch
    H, HD = 16, 64
    TT = 512
    NS = T // TT
    NB = T // 128
    ND = 2 * NB - 1
    import ml_dtypes
    prev_h = nc.inline_tensor(np.eye(128)[::-1].copy().astype(ml_dtypes.bfloat16),
                              name=_nm("prev"))
    nid_h = nc.inline_tensor(np.eye(128).astype(np.float32), name=_nm("nid"))
    ones_cb_h = nc.inline_tensor(np.ones((128, 1), ml_dtypes.bfloat16), name=_nm("onc"))
    ones_r64_h = nc.inline_tensor(np.ones((1, 64), np.float32), name=_nm("on64"))
    onesrw_h = nc.inline_tensor(np.ones((1, 128), np.float32), name=_nm("onrw"))

    with ExitStack() as c2:
        q_sb = SB(nc, c2, [64, T], BF16, "qh")
        k_aug = SB(nc, c2, [65, T], BF16, "kaug")
        v_sb = SB(nc, c2, [128, NB * 64], BF16, "vh")
        th_row = SB(nc, c2, [1, 2 * T - 1], F32, "throw")
        ebt = SB(nc, c2, [128, ND * 128], BF16, "ebt")
        t2b = [SB(nc, c2, [128, 128], BF16, "t2b") for _ in range(2)]
        prev_sb = SB(nc, c2, [128, 128], BF16, "prevs")
        nid_sb = SB(nc, c2, [128, 128], F32, "nids")
        ones_cb = SB(nc, c2, [128, 1], BF16, "ocb")
        ones_r64 = SB(nc, c2, [1, 64], F32, "o64")
        ones_rw = SB(nc, c2, [1, 128], F32, "orw")
        mc8 = SB(nc, c2, [128, 8], F32, "mc8")
        m4 = SB(nc, c2, [128, 32], F32, "m4c")
        negm = SB(nc, c2, [4, 128], F32, "negm")
        bmax = SB(nc, c2, [1, 2], F32, "bmax")
        bmax_c = SB(nc, c2, [128, 1], F32, "bmaxc")
        q_aug = SB(nc, c2, [65, TT], BF16, "qaug")
        pb_sb = [SB(nc, c2, [128, TT], BF16, "pbs") for _ in range(4)]
        rl_row = SB(nc, c2, [1, TT], F32, "rlrow")
        oev = SB(nc, c2, [64, TT], BF16, "oev")
        ao_sb = [SB(nc, c2, [128, T], BF16, "ao") for _ in range(PT)]

        p_sc = [PS(nc, c2, [128, TT], F32, "psc") for _ in range(2)]
        p_eb = PS(nc, c2, [128, 128], F32, "peb")
        p_nm = PS(nc, c2, [128, 128], F32, "pnm")
        p_l = PS(nc, c2, [1, TT], F32, "pl")
        p_o = PS(nc, c2, [64, TT], F32, "po")
        p_bc = PS(nc, c2, [64, TT], F32, "pbc")

        lc = p.op("sync", lambda e: [
            e.dma_start(out=prev_sb[:], in_=prev_h.ap()[:, :]),
            e.dma_start(out=nid_sb[:], in_=nid_h.ap()[:, :]),
            e.dma_start(out=ones_cb[:], in_=ones_cb_h.ap()[:, :]),
            e.dma_start(out=ones_r64[:], in_=ones_r64_h.ap()[:, :]),
            e.dma_start(out=ones_rw[:], in_=onesrw_h.ap()[:, :])],
            units=5, dma=True, stream="cst", after=entry)

        hprev_done = entry
        for h in range(H):
            ld = p.op("sync", lambda e, h=h: [
                e.dma_start(out=q_sb[:, :], in_=qfm_d[64 * h:64 * h + 64, :]),
                e.dma_start(out=k_aug[0:64, :], in_=kfm_d[64 * h:64 * h + 64, :]),
                e.dma_start(out=v_sb[:].rearrange("p (b c) -> p b c", c=64),
                            in_=vtm_d.rearrange("(b p) c -> p b c", p=128)[:, :, 64 * h:64 * h + 64]),
                e.dma_start(out=th_row[:, :], in_=th_d[h:h + 1, 0:2 * T - 1])],
                units=4, dma=True, stream="hld", after=tuple(hprev_done) + entry)
            onr = p.op("vector", lambda v: v.memset(k_aug[64:65, :], 1.0),
                       after=(ld,) + tuple(hprev_done))
            bm1 = p.op("vector", lambda v: v.reduce_max(
                bmax[0:1, 0:1], th_row[0:1, :], AX.X, apply_absolute_value=True),
                after=(ld,) + tuple(hprev_done))
            bmm = p.op("tensor", lambda te: te.matmul(
                p_nm[0:128, 0:1] if p_nm.shape[0] >= 128 else p_nm[:, 0:1],
                ones_r64[0:1, 0:1].to_broadcast([1, 128]) if False else ones_rw[0:1, :],
                bmax[0:1, 0:1], start=True, stop=True), after=(bm1, lc))
            bmc = p.op("vector", lambda v: v.tensor_copy(
                bmax_c[:, 0:1], p_nm[0:128, 0:1] if p_nm.shape[0] >= 128 else p_nm[:, 0:1]),
                after=(bmm,))
            ebt_ops = []
            for dd in range(ND):
                par = dd % 2
                t2 = p.op("gpsimd", lambda e, dd=dd, par=par, h=h: [
                    e.dma_start(out=t2b[par][:],
                                in_=bass.AP(th_d.tensor, h * (th_d.shape[1]) + 128 * dd,
                                            [[1, 128], [1, 128]]))],
                    units=1, dma=True, stream=f"t2{par}",
                    after=(ld,) + ((ebt_ops[-2],) if len(ebt_ops) >= 2 else ()))
                rev = ND - 1 - dd
                mm = p.op("tensor", lambda te, par=par: te.matmul(
                    p_eb[:, 0:128], t2b[par][:], prev_sb[:], start=True, stop=True),
                    after=(t2, lc) + ((ebt_ops[-1],) if ebt_ops else ()))
                ex = p.op("scalar", lambda sc, rev=rev: sc.activation(
                    ebt[:, 128 * rev:128 * rev + 128], p_eb[:, 0:128], AF.Exp),
                    after=(mm,))
                ebt_ops.append(ex)
            mhat = {}
            for a in range(NB):
                rms = []
                for c in range(NS):
                    pb = p_sc[c % 2]
                    mm = p.op("tensor", lambda te, a=a, c=c, pb=pb: te.matmul(
                        pb[:, 0:TT], q_sb[:, 128 * a:128 * a + 128],
                        k_aug[0:64, TT * c:TT * (c + 1)], start=True, stop=True),
                        after=(ld, onr) + ((rms[-2],) if len(rms) >= 2 else ())
                        + ((mhat[a - 1],) if a - 1 in mhat else ()))
                    rm = p.op("vector", lambda v, c=c, pb=pb: v.reduce_max(
                        mc8[:, c:c + 1], pb[:, 0:TT], AX.X), after=(mm,))
                    rms.append(rm)
                r2 = p.op("vector", lambda v, a=a: v.reduce_max(
                    m4[:, a:a + 1], mc8[:, 0:NS], AX.X), after=tuple(rms))
                mh = p.op("vector", lambda v, a=a: v.tensor_tensor(
                    m4[:, a:a + 1], m4[:, a:a + 1], bmax_c[:, 0:1],
                    ALU.add), after=(r2, bmc))
                mhat[a] = mh
            exs = []
            prev_fin = None
            prev_nme = None
            prev_rc = None
            prev_oe = None
            last_mm_s = None
            for s in range(NS):
                tps = p.op("tensor", lambda te, s=s: te.transpose(
                    p_nm[0:4, 0:128], m4[:, 4 * s:4 * s + 4], nid_sb[:]),
                    after=(mhat[4 * s], mhat[4 * s + 1], mhat[4 * s + 2],
                           mhat[4 * s + 3], lc, bmc)
                    + ((prev_nme,) if prev_nme else ()))
                nme = p.op("vector", lambda v: v.tensor_scalar_mul(
                    negm[:], p_nm[0:4, 0:128], -1.0),
                    after=(tps,) + ((qa2,) if s > 0 else ()))
                prev_nme = nme
                qa1 = p.op("vector", lambda v, s=s: v.tensor_copy(
                    q_aug[0:64, :], q_sb[:, TT * s:TT * (s + 1)]),
                    after=(ld,) + ((last_mm_s,) if last_mm_s else ()))
                qa2 = p.op("gpsimd", lambda e: [
                    e.dma_start(out=q_aug[64:65, 0:TT], in_=negm[:, :])],
                    units=1, dma=True, stream="qaug", after=(nme, qa1))
                lmm = None
                omm = None
                tts = []
                lmms = []
                omms = []

                def flush_unit(kko):
                    nonlocal lmm, omm, last_mm_s
                    par = kko % 4
                    lmm = p.op("tensor", lambda te, kko=kko, par=par: te.matmul(
                        p_l[0:1, 0:TT], ones_cb[:, 0:1], pb_sb[par][:, 0:TT],
                        start=(kko == 0), stop=(kko == NB - 1)),
                        after=(tts[kko],) + ((prev_rc,) if (kko == 0 and prev_rc) else ()))
                    omm2 = p.op("tensor", lambda te, kko=kko, par=par: te.matmul(
                        p_o[:, 0:TT], v_sb[:, 64 * kko:64 * kko + 64],
                        pb_sb[par][:, 0:TT],
                        start=(kko == 0), stop=(kko == NB - 1)),
                        after=(tts[kko], ld) + ((prev_oe,) if (kko == 0 and prev_oe) else ()))
                    lmms.append(lmm)
                    omms.append(omm2)
                    omm = omm2
                    last_mm_s = omm2

                for kk in range(NB):
                    par = kk % 4
                    pb = p_sc[kk % 2]
                    mm = p.op("tensor", lambda te, kk=kk, pb=pb: te.matmul(
                        pb[:, 0:TT], k_aug[:, 128 * kk:128 * kk + 128],
                        q_aug[:, 0:TT], start=True, stop=True),
                        after=(qa2, onr) + ((exs[-2],) if len(exs) >= 2 else ()))
                    ex = p.op("scalar", lambda sc, kk=kk, par=par, pb=pb: sc.activation(
                        pb_sb[par][:, 0:TT], pb[:, 0:TT], AF.Exp),
                        after=(mm,) + ((omms[kk - 4],) if kk >= 4 else ()))
                    exs.append(ex)
                    rev0 = 4 * s - kk + NB - 1
                    tt = p.op("vector", lambda v, par=par, rev0=rev0: v.tensor_tensor(
                        pb_sb[par][:, 0:TT], pb_sb[par][:, 0:TT],
                        ebt[:, 128 * rev0:128 * rev0 + 512], ALU.mult),
                        after=(ex,) + tuple(ebt_ops[rev0:rev0 + 4])
                        + ((lmms[kk - 4],) if kk >= 4 else ()))
                    tts.append(tt)
                    if kk >= 2:
                        flush_unit(kk - 2)
                flush_unit(NB - 2)
                flush_unit(NB - 1)
                rc = p.op("vector", lambda v: v.reciprocal(rl_row[0:1, :], p_l[0:1, 0:TT]),
                          after=(lmm,) + ((prev_fin,) if prev_fin else ()))
                prev_rc = rc
                bc = p.op("tensor", lambda te: te.matmul(
                    p_bc[:, 0:TT], ones_r64[0:1, 0:64], rl_row[0:1, :],
                    start=True, stop=True), after=(rc, lc)
                    + ((prev_fin,) if prev_fin else ()))
                oe = p.op("vector", lambda v: v.tensor_copy(oev[:, 0:TT], p_o[:, 0:TT]),
                          after=(omm,) + ((prev_fin,) if prev_fin else ()))
                prev_oe = oe
                fin = p.op("vector", lambda v, h=h, s=s: v.tensor_tensor(
                    ao_sb[h // 2][64 * (h % 2):64 * (h % 2) + 64, TT * s:TT * (s + 1)],
                    oev[:, 0:TT], p_bc[:, 0:TT], ALU.mult), after=(oe, bc))
                prev_fin = fin
            hprev_done = (fin, exs[-1], exs[-2])
        with ExitStack() as c3:
            projw = SB(nc, c3, [128, PT * D], BF16, "projw")
            pb2 = SB(nc, c3, [128, PT], F32, "pbias")
            xo_f = [SB(nc, c3, [128, TT], F32, "xo") for _ in range(2)]
            ao_f = [SB(nc, c3, [128, TT], F32, "aof") for _ in range(2)]
            p_pr = p_sc
            lw2 = load_weight_bf16(p, projw, wprojT, D, D, after=tuple(hprev_done))
            lpb = p.dma("sync", pb2[:], pbcol[:, :], stream="cst", after=tuple(hprev_done))
            last_ad = [None, None]
            last_bi = [None, None]
            last_st2 = None
            for s in range(NS):
                for o in range(PT):
                    par = o % 2
                    ppr = p_pr[par]
                    xlo = p.op("sync", lambda e, s=s, o=o, par=par: [
                        e.dma_start(out=xo_f[par][:, :],
                                    in_=x3[128 * o:128 * (o + 1), TT * s:TT * (s + 1)])],
                        units=1, dma=True, stream=f"xo{par}",
                        after=((last_ad[par],) if last_ad[par] else ()) + tuple(hprev_done))
                    mm = mm_chain(p, ppr[:, 0:TT],
                                  [wtile(projw, i, o, D) for i in range(PT)],
                                  [ao_sb[i][:, TT * s:TT * (s + 1)] for i in range(PT)],
                                  after=(lw2,) + tuple(hprev_done)
                                  + ((last_bi[par],) if last_bi[par] else ()))
                    bi = p.op("scalar", lambda sc, o=o, par=par, ppr=ppr: sc.activation(
                        ao_f[par][:, 0:TT], ppr[:, 0:TT], AF.Identity, bias=pb2[:, o:o + 1]),
                        after=(mm, lpb) + ((last_ad[par],) if last_ad[par] else ()))
                    ad = p.op("vector", lambda v, par=par: v.tensor_tensor(
                        ao_f[par][:, 0:TT], ao_f[par][:, 0:TT], xo_f[par][:, 0:TT],
                        ALU.add), after=(bi, xlo))
                    st = p.op("sync", lambda e, o=o, s=s, par=par: [
                        e.dma_start(out=aacc_d[128 * o:128 * (o + 1), TT * s:TT * (s + 1)],
                                    in_=ao_f[par][:, 0:TT])], units=1, dma=True,
                        stream="aout", after=(ad,))
                    last_ad[par] = st
                    last_bi[par] = bi
                    last_st2 = st
            return (last_st2,)



def emit_router_tm(p, nc, ctx, NE, Tt, xn32_view, wg_sb, bg_sb, ones_r,
                   ohp_rows_dst, bufs, deps, scale_by_invsum=True,
                   ident_f32=None):
    p_lg, p_tp, lg = bufs["p_lg"], bufs["p_tp"], bufs["lg"]
    NEc = NE

    def mm_fn2(te):
        ins = None
        for i in range(PT):
            ins = te.matmul(p_lg[:, 0:NEc], xn32_view[i],
                            wg_sb[:, i * NEc:(i + 1) * NEc], start=(i == 0), stop=False)
        return te.matmul(p_lg[:, 0:NEc], ones_r[0:1, 0:128], bg_sb[0:1, 0:NEc],
                         start=False, stop=True)
    mm = p.op("tensor", mm_fn2, after=tuple(deps))
    c0 = p.op("vector", lambda v: v.tensor_copy(lg[:, 0:NEc], p_lg[:, 0:NEc]),
              after=(mm,))
    mx = p.op("vector", lambda v: v.reduce_max(lg[:, NEc:NEc + 1], lg[:, 0:NEc], AX.X),
              after=(c0,))
    oh = p.op("vector", lambda v: v.tensor_scalar(
        lg[:, NEc + 2:2 * NEc + 2], lg[:, 0:NEc], lg[:, NEc:NEc + 1], None, ALU.is_ge),
        after=(mx,))
    ops = [mm, c0, mx, oh]
    if scale_by_invsum:
        nm = p.op("vector", lambda v: v.tensor_scalar_mul(
            lg[:, NEc + 1:NEc + 2], lg[:, NEc:NEc + 1], -1.0), after=(mx,))
        ex = p.op("scalar", lambda sc: sc.activation(
            lg[:, 0:NEc], lg[:, 0:NEc], AF.Exp, bias=lg[:, NEc + 1:NEc + 2]),
            after=(nm, oh))
        sm = p.op("vector", lambda v: v.reduce_sum(
            lg[:, NEc:NEc + 1], lg[:, 0:NEc], AX.X), after=(ex,))
        rc = p.op("vector", lambda v: v.reciprocal(
            lg[:, NEc + 1:NEc + 2], lg[:, NEc:NEc + 1]), after=(sm,))
        sc2 = p.op("vector", lambda v: v.tensor_scalar(
            lg[:, NEc + 2:2 * NEc + 2], lg[:, NEc + 2:2 * NEc + 2],
            lg[:, NEc + 1:NEc + 2], None, ALU.mult), after=(rc,))
        ops += [nm, ex, sm, rc, sc2]
    tp = p.op("tensor", lambda te: te.transpose(
        p_tp[:, 0:128], lg[:, NEc + 2:2 * NEc + 2], ident_f32[:]),
        after=(ops[-1],))
    ev = p.op("vector", lambda v: v.tensor_copy(ohp_rows_dst, p_tp[0:NEc, 0:128]),
              after=(tp,))
    return ev


def emit_l3_moe(p, nc, ctx, T, scratch, wgate, bgate, W1T, b1col, W2T, b2col,
                x4_out, entry=()):
    qfm_d, kfm_d, vtm_d, xn_d, th_d, aacc_d = scratch
    NE = 8
    DFF = 4096
    TT = 512
    NS = T // TT
    NB = T // 128
    import ml_dtypes
    ones_r_h = nc.inline_tensor(np.ones((1, 128), np.float32), name=_nm("onr"))
    idf_h = nc.inline_tensor(np.eye(128, dtype=np.float32), name=_nm("idf"))

    moe_d = nc.dram_tensor(_nm("l3moe"), [D, T], F32, kind="Internal").ap()
    return moe_d


def emit_l3_moe_main(p, nc, ctx, T, xn_d, aacc_d, ohp_rows, W1T, b1col, W2T,
                     b2col, x4_out, entry=()):
    NE, DFF, TT = 8, 4096, 512
    NS = T // TT
    MH = DFF // 128
    import ml_dtypes
    ones_r_h = nc.inline_tensor(np.ones((1, 128), np.float32), name=_nm("onr"))
    moe_d = nc.dram_tensor(_nm("l3moe"), [D, T], F32, kind="Internal").ap()

    with ExitStack() as c1:
        w1_sb = SB(nc, c1, [128, PT * DFF], BF16, "w1e")
        w2_sb = SB(nc, c1, [128, MH * D], BF16, "w2e")
        b1_sb = SB(nc, c1, [128, MH], F32, "b1e")
        b2_sb = SB(nc, c1, [128, PT], F32, "b2e")
        xn_t = [[SB(nc, c1, [128, TT], BF16, "xnt") for _ in range(PT)] for _ in range(2)]
        h_sb = [SB(nc, c1, [128, TT], BF16, "hsb") for _ in range(MH)]
        y_sb = [SB(nc, c1, [128, TT], F32, "ysb") for _ in range(2)]
        ones_r = SB(nc, c1, [1, 128], F32, "orr")
        ohp_row = [SB(nc, c1, [1, TT], F32, "ohprow") for _ in range(2)]
        p_h = [PS(nc, c1, [128, TT], F32, "phh") for _ in range(2)]
        p_y = [PS(nc, c1, [128, TT], F32, "pyy") for _ in range(2)]
        p_bc = PS(nc, c1, [128, TT], F32, "pbc")

        lor = p.dma("sync", ones_r[:], ones_r_h.ap()[:, :], stream="cst", after=entry)

        last_xn_cons = [None, None]
        last_h_cons = {}
        last_y = [None, None]
        last_bc_cons = None
        last_st = None
        w1q_ops = {}
        w2q_ops = {}
        last_w1_reader = {}
        last_w2_reader = {}

        def issue_w1q(e2, q):
            if e2 >= NE or (e2, q) in w1q_ops:
                return
            war = last_w1_reader.get((e2 - 1, q))
            w1q_ops[(e2, q)] = p.op("gpsimd", lambda g, e2=e2, q=q: [
                g.dma_start(
                    out=w1_sb[:].rearrange("p (i o) -> p i o", o=DFF)[:, :, 1024 * q:1024 * (q + 1)],
                    in_=W1T[e2].rearrange("(i p) o -> p i o", p=128)[:, :, 1024 * q:1024 * (q + 1)])],
                units=1, dma=True, stream="wmoe",
                after=((war,) if war else tuple(entry)))

        def issue_w2q(e2, q):
            if e2 >= NE or (e2, q) in w2q_ops:
                return
            war = last_w2_reader.get((e2 - 1, q))
            w2q_ops[(e2, q)] = p.op("gpsimd", lambda g, e2=e2, q=q: [
                g.dma_start(
                    out=w2_sb[:].rearrange("p (i o) -> p i o", o=D)[:, :, 256 * q:256 * (q + 1)],
                    in_=W2T[e2].rearrange("(i p) o -> p i o", p=128)[:, :, 256 * q:256 * (q + 1)])],
                units=1, dma=True, stream="wmoe",
                after=((war,) if war else tuple(entry)))

        b1_ops = {}
        b2_ops = {}

        def issue_b1(e2, war=None):
            if e2 >= NE or e2 in b1_ops:
                return
            b1_ops[e2] = p.op("gpsimd", lambda g, e2=e2: [
                g.dma_start(out=b1_sb[:], in_=b1col[e2])],
                units=1, dma=True, stream="wmoe",
                after=((war,) if war else tuple(entry)))

        def issue_b2(e2, war=None):
            if e2 >= NE or e2 in b2_ops:
                return
            b2_ops[e2] = p.op("gpsimd", lambda g, e2=e2: [
                g.dma_start(out=b2_sb[:], in_=b2col[e2])],
                units=1, dma=True, stream="wmoe",
                after=((war,) if war else tuple(entry)))

        for q in range(4):
            issue_w1q(0, q)
        for q in range(4):
            issue_w2q(0, q)
        issue_b1(0)
        issue_b2(0)
        for e in range(NE):
            e_cons = []
            for s in range(NS):
                par = s % 2
                xl = p.op("sync", lambda eng, e=e, s=s, par=par: [
                    eng.dma_start(out=xn_t[par][i][:],
                                  in_=xn_d[128 * i:128 * (i + 1), TT * s:TT * (s + 1)])
                    for i in range(PT)], units=PT, dma=True, stream=f"mx{par}",
                    after=((last_xn_cons[par],) if last_xn_cons[par] else ()) + entry)
                h_evs = []
                for m in range(MH):
                    pb = p_h[m % 2]
                    mm = mm_chain(p, pb[:, 0:TT],
                                  [wtile(w1_sb, i, m, DFF) for i in range(PT)],
                                  [xn_t[par][i][:] for i in range(PT)],
                                  after=(xl, w1q_ops[(e, m // 8)],
                                         w1q_ops[(e, 0)])
                                  + ((h_evs[-2],) if len(h_evs) >= 2 else ())
                                  + ((last_h_cons.get(m),) if m in last_h_cons else ()))
                    ev = p.op("scalar", lambda sc, m=m, pb=pb: sc.activation(
                        h_sb[m][:, 0:TT], pb[:, 0:TT], AF.Relu, bias=b1_sb[:, m:m + 1]),
                        after=(mm, b1_ops[e]))
                    h_evs.append(ev)
                    if s == NS - 1 and m % 8 == 7:
                        last_w1_reader[(e, m // 8)] = mm
                        issue_w1q(e + 1, m // 8)
                    if s == NS - 1 and m == MH - 1:
                        issue_b1(e + 1, war=ev)
                last_xn_cons[par] = h_evs[-1]
                opar2 = s % 2
                ohl = p.op("gpsimd", lambda eng, e=e, s=s, opar2=opar2: [
                    eng.dma_start(out=ohp_row[opar2][0:1, :],
                                  in_=ohp_rows[e:e + 1, TT * s:TT * (s + 1)])],
                    units=1, dma=True, stream=f"ohl{opar2}",
                    after=((last_bc_cons,) if last_bc_cons else ()) + entry)
                bc = p.op("tensor", lambda te, e=e, s=s, opar2=opar2: te.matmul(
                    p_bc[:, 0:TT], ones_r[0:1, :], ohp_row[opar2][0:1, 0:TT],
                    start=True, stop=True),
                    after=(lor, ohl) + ((last_bc_cons,) if last_bc_cons else ()))
                for o in range(PT):
                    opar = o % 2
                    py = p_y[opar]
                    mm = mm_chain(p, py[:, 0:TT],
                                  [wtile(w2_sb, mi, o, D) for mi in range(MH)],
                                  [h_sb[mi][:, 0:TT] for mi in range(MH)],
                                  after=tuple(h_evs) + (w2q_ops[(e, o // 2)],)
                                  + ((last_y[opar],) if last_y[opar] else ()))
                    for mi in range(MH):
                        last_h_cons[mi] = mm
                    if s == NS - 1 and o % 2 == 1:
                        last_w2_reader[(e, o // 2)] = mm
                        issue_w2q(e + 1, o // 2)
                    bi = p.op("scalar", lambda sc, o=o, opar=opar, py=py: sc.activation(
                        y_sb[opar][:, 0:TT], py[:, 0:TT], AF.Identity,
                        bias=b2_sb[:, o:o + 1]), after=(mm, b2_ops[e])
                        + ((last_y[opar],) if last_y[opar] else ()))
                    if s == NS - 1 and o == PT - 1:
                        issue_b2(e + 1, war=bi)
                    tt = p.op("vector", lambda v, opar=opar: v.tensor_tensor(
                        y_sb[opar][:, 0:TT], y_sb[opar][:, 0:TT], p_bc[:, 0:TT],
                        ALU.mult), after=(bi, bc))
                    st = p.op("gpsimd", lambda g, e=e, o=o, s=s, opar=opar: [
                        g.dma_start(out=moe_d[128 * o:128 * (o + 1), TT * s:TT * (s + 1)],
                                    in_=y_sb[opar][:, 0:TT],
                                    **({} if e == 0 else {"accum_op": ALU.add}))],
                        units=1, dma=True, stream=f"macc{o % 4}", after=(tt,))
                    last_y[opar] = st
                    e_cons.append(st)
                last_bc_cons = e_cons[-1]
        a_t = [SB(nc, c1, [128, TT], F32, "at4") for _ in range(2)]
        m_t = [SB(nc, c1, [128, TT], F32, "mt4") for _ in range(2)]
        last_fin = [None, None]
        fins = []
        for s in range(NS):
            for o in range(PT):
                par = (s * PT + o) % 2
                ld = p.op("sync", lambda eng, s=s, o=o, par=par: [
                    eng.dma_start(out=a_t[par][:], in_=aacc_d[128 * o:128 * (o + 1), TT * s:TT * (s + 1)]),
                    eng.dma_start(out=m_t[par][:], in_=moe_d[128 * o:128 * (o + 1), TT * s:TT * (s + 1)])],
                    units=2, dma=True, stream=f"cmb{par}",
                    after=tuple(e_cons[-8:]) + ((last_fin[par],) if last_fin[par] else ()))
                ad = p.op("vector", lambda v, par=par: v.tensor_tensor(
                    a_t[par][:], a_t[par][:], m_t[par][:], ALU.add), after=(ld,))
                st = p.op("sync", lambda eng, s=s, o=o, par=par: [
                    eng.dma_start(out=x4_out[128 * o:128 * (o + 1), TT * s:TT * (s + 1)],
                                  in_=a_t[par][:])], units=1, dma=True,
                    stream="x4o", after=(ad,))
                last_fin[par] = st
                fins.append(st)
        return (fins[-1],)


def build_L3(T=4096):
    nc = bass.Bass("TRN2", target_bir_lowering=False)
    x3 = nc.dram_tensor("x3", [D, T], F32, kind="ExternalInput").ap()
    wqkvT = nc.dram_tensor("wqkvT", [D, 3 * D], F32, kind="ExternalInput").ap()
    wprojT = nc.dram_tensor("wprojT", [D, D], F32, kind="ExternalInput").ap()
    pbcol = nc.dram_tensor("pbcol", [128, PT], F32, kind="ExternalInput").ap()
    relbias = nc.dram_tensor("relbias", [32, 16], F32, kind="ExternalInput").ap()
    wgate = nc.dram_tensor("wgate", [D, 8], F32, kind="ExternalInput").ap()
    bgate = nc.dram_tensor("bgate", [1, 8], F32, kind="ExternalInput").ap()
    W1T = nc.dram_tensor("W1T", [8, D, 4096], F32, kind="ExternalInput").ap()
    b1col = nc.dram_tensor("b1col", [8, 128, 32], F32, kind="ExternalInput").ap()
    W2T = nc.dram_tensor("W2T", [8, 4096, D], F32, kind="ExternalInput").ap()
    b2col = nc.dram_tensor("b2col", [8, 128, PT], F32, kind="ExternalInput").ap()
    x4 = nc.dram_tensor("x4", [D, T], F32, kind="ExternalOutput").ap()

    with ExitStack() as ctx:
        p = Prog(nc, ctx)
        ohp_rows = SB(nc, ctx, [8, T], BF16, "ohp")
        ph1, scratch = emit_l3_phase1(p, nc, ctx, T, x3, wqkvT, relbias,
                                      wgate, bgate, ohp_rows)
        ph2 = emit_l3_attn_heads(p, nc, ctx, T, scratch, wprojT, pbcol, x3,
                                 entry=tuple(ph1))
        xn_d = scratch[3]
        aacc_d = scratch[5]
        fins = emit_l3_moe_main(p, nc, ctx, T, xn_d, aacc_d, ohp_rows,
                                W1T, b1col, W2T, b2col, x4, entry=tuple(ph2))
        p.op("gpsimd", lambda g: g.nop(), after=tuple(fins))
        p.emit()
    return nc


def emit_l1(p, nc, ctx, T, x1, WqT, WkT, WvT, WgT, bg, WdlT, Lpar_row, WoT,
            lngcol, lnbcol, rwT, rb_row, w1T, b1col, w2T, b2col, x2_out, entry=()):
    H, S = 8, 128
    LV = 12
    TT = 512
    NS = T // TT
    NB = T // 128
    import ml_dtypes
    ones_c_h = nc.inline_tensor(np.ones((128, 1), ml_dtypes.bfloat16), name=_nm("onc"))
    ones_r_h = nc.inline_tensor(np.ones((1, 128), np.float32), name=_nm("onr"))
    idf_h = nc.inline_tensor(np.eye(128, dtype=np.float32), name=_nm("idf"))
    mask_np = np.where(np.arange(128)[None, :] >= np.arange(128)[:, None],
                       0.0, -1e30).astype(np.float32)
    mask_h = nc.inline_tensor(mask_np, name=_nm("msk"))

    xn_d = nc.dram_tensor(_nm("l1xn"), [D, T], BF16, kind="Internal").ap()
    qfm_d = nc.dram_tensor(_nm("l1q"), [D, T], BF16, kind="Internal").ap()
    kfm_d = nc.dram_tensor(_nm("l1k"), [D, T], BF16, kind="Internal").ap()
    vtm_d = nc.dram_tensor(_nm("l1v"), [T, D], BF16, kind="Internal").ap()
    aacc_d = nc.dram_tensor(_nm("l1aa"), [D, T], F32, kind="Internal").ap()
    lam_d = nc.dram_tensor(_nm("l1lam"), [96, T], F32, kind="Internal").ap()
    g_d = nc.dram_tensor(_nm("l1g"), [H, T], F32, kind="Internal").ap()

    ohp_rows = SB(nc, ctx, [32, T], BF16, "ohp1")

    with ExitStack() as c1:
        qw = SB(nc, c1, [128, PT * D], BF16, "qw")
        kw = SB(nc, c1, [128, PT * D], BF16, "kw")
        vw = SB(nc, c1, [128, PT * D], BF16, "vw")
        gw = SB(nc, c1, [128, PT * 8], BF16, "gw")
        dlw = SB(nc, c1, [128, PT * 96], BF16, "dlw")
        rw = SB(nc, c1, [128, PT * 32], F32, "rw")
        bg_sb = SB(nc, c1, [1, 8], F32, "bgs")
        lp_sb = SB(nc, c1, [1, 96], F32, "lps")
        rb_sb = SB(nc, c1, [1, 32], F32, "rbs")
        x_f = [[SB(nc, c1, [128, TT], F32, "x") for _ in range(PT)]]
        x_f.append(x_f[0])
        xn_bf = [[SB(nc, c1, [128, TT], BF16, "xn") for _ in range(PT)] for _ in range(2)]
        xn32 = [SB(nc, c1, [128, TT], F32, "xn32") for _ in range(PT)]
        ones_c = SB(nc, c1, [128, 1], BF16, "oc")
        ones_r = SB(nc, c1, [1, 128], F32, "orr")
        idf_sb = SB(nc, c1, [128, 128], F32, "idf")
        onesf = SB(nc, c1, [128, 1], F32, "onesf")
        gz = SB(nc, c1, [8, T], F32, "gz")
        g2_pm = SB(nc, c1, [8, T], F32, "g2pm")
        ev_sb = [SB(nc, c1, [128, TT], BF16, "evb") for _ in range(2)]
        gl_tm = SB(nc, c1, [128, 8], F32, "gltm")
        dl_tm = SB(nc, c1, [128, 96], F32, "dltm")
        sp_sc = SB(nc, c1, [128, 96], F32, "spsc")
        g_pm = SB(nc, c1, [8, T], F32, "gpm")
        lam_pm = SB(nc, c1, [96, T], F32, "lampm")
        lnbufs = dict(
            stat=SB(nc, c1, [1, 3 * TT], F32, "st"),
            xbf=[SB(nc, c1, [128, TT], BF16, "xc")],
            sq=SB(nc, c1, [128, TT], BF16, "sqq"),
            nsc=[SB(nc, c1, [128, TT], F32, "nsc")],
        )
        rbufs = dict(lg=SB(nc, c1, [128, 66], F32, "lg"))
        lnbufs["p_sum1"] = PS(nc, c1, [1, TT], F32, "ps1")
        lnbufs["p_sum2"] = PS(nc, c1, [1, TT], F32, "ps2")
        lnbufs["p_bc1"] = PS(nc, c1, [128, TT], F32, "pb1")
        lnbufs["p_bc2"] = PS(nc, c1, [128, TT], F32, "pb2")
        p_mm = [PS(nc, c1, [128, TT], F32, "pmm") for _ in range(2)]
        p_sm = PS(nc, c1, [128, 128], F32, "psm")
        rbufs["p_lg"] = p_sm[:, 0:32]
        rbufs["p_tp"] = PS(nc, c1, [32, 128], F32, "ptp")

        lw = p.op("gpsimd", lambda g: [
            g.dma_start(out=qw[:].rearrange("p (i o) -> p i o", o=D),
                        in_=WqT.rearrange("(i p) o -> p i o", p=128)),
            g.dma_start(out=kw[:].rearrange("p (i o) -> p i o", o=D),
                        in_=WkT.rearrange("(i p) o -> p i o", p=128)),
            g.dma_start(out=vw[:].rearrange("p (i o) -> p i o", o=D),
                        in_=WvT.rearrange("(i p) o -> p i o", p=128)),
            g.dma_start(out=gw[:].rearrange("p (i o) -> p i o", o=8),
                        in_=WgT.rearrange("(i p) o -> p i o", p=128)),
            g.dma_start(out=dlw[:].rearrange("p (i o) -> p i o", o=96),
                        in_=WdlT.rearrange("(i p) o -> p i o", p=128))],
            units=5, dma=True, stream="wld", after=entry)
        lw2 = p.op("sync", lambda e: [
            e.dma_start(out=rw[:].rearrange("p (i o) -> p i o", o=32),
                        in_=rwT.rearrange("(i p) o -> p i o", p=128)),
            e.dma_start(out=bg_sb[:], in_=bg[:, :]),
            e.dma_start(out=lp_sb[:], in_=Lpar_row[:, :]),
            e.dma_start(out=rb_sb[:], in_=rb_row[:, :]),
            e.dma_start(out=ones_c[:], in_=ones_c_h.ap()[:, :]),
            e.dma_start(out=ones_r[:], in_=ones_r_h.ap()[:, :]),
            e.dma_start(out=idf_sb[:], in_=idf_h.ap()[:, :])],
            units=7, dma=True, stream="cst", after=entry)
        oz = p.op("vector", lambda v: v.memset(onesf[:], 1.0), after=entry)
        oz2 = p.op("vector", lambda v: v.memset(gz[0:8, :], 0.0), after=entry)

        prev_ln = []
        prev_rt = []
        prev_x_cons = [None, None]
        prev_xn_cons = [(), ()]
        evs = []
        gl_ops = []
        prev_small = []
        for s in range(NS):
            par = s % 2
            xl = p.op("sync", lambda e, s=s, par=par: [
                e.dma_start(out=x_f[par][i][:], in_=x1[128 * i:128 * (i + 1), TT * s:TT * (s + 1)])
                for i in range(PT)], units=PT, dma=True, stream="ax",
                after=tuple(c for c in prev_x_cons if c) + entry)
            xn_view = [xn_bf[par][i][:] for i in range(PT)]
            louts = emit_ln(p, nc, ctx, [x_f[par][i] for i in range(PT)], xn_view, TT,
                            ones_c, ones_r, lnbufs, x_ready=(xl, lw2) + tuple(prev_ln)
                            + tuple(prev_xn_cons[par]),
                            xn32_tiles=xn32)
            prev_ln = list(louts)
            prev_x_cons[par] = louts[-1]
            xnst = p.op("sync", lambda e, s=s, par=par: [
                e.dma_start(out=xn_d[128 * i:128 * (i + 1), TT * s:TT * (s + 1)],
                            in_=xn_bf[par][i][:]) for i in range(PT)],
                units=PT, dma=True, stream="xnd", after=tuple(louts))
            for bb in range(4):
                rv = emit_router_tm(
                    p, nc, ctx, 32, 128,
                    [xn32[i][:, 128 * bb:128 * bb + 128] for i in range(PT)],
                    rw, rb_sb, ones_r,
                    ohp_rows[0:32, TT * s + 128 * bb: TT * s + 128 * bb + 128],
                    rbufs, deps=tuple(louts) + (lw2,) + tuple(prev_rt) + tuple(prev_small),
                    scale_by_invsum=False, ident_f32=idf_sb)
                prev_rt = [rv]
                prev_ln.append(rv)
            for (wsb, dst, scale) in [(qw, qfm_d, S ** -0.5), (kw, kfm_d, 1.0)]:
                for o in range(PT):
                    pb = p_mm[len(evs) % 2]
                    mm = mm_chain(p, pb[:, 0:TT],
                                  [wtile(wsb, i, o, D) for i in range(PT)],
                                  [xn_bf[par][i][:] for i in range(PT)],
                                  after=tuple(louts) + (lw,) + tuple(prev_small)
                                  + ((evs[-2],) if len(evs) >= 2 else ()))
                    epar = len(evs) % 2
                    ev = p.op("scalar", lambda sc, pb=pb, epar=epar, scale=scale: sc.activation(
                        ev_sb[epar][:, 0:TT], pb[:, 0:TT], AF.Identity, scale=scale),
                        after=(mm,) + ((evs[-2],) if len(evs) >= 2 else ()))
                    st = p.op("sync", lambda e, o=o, s=s, dst=dst, epar=epar: [
                        e.dma_start(out=dst[128 * o:128 * (o + 1), TT * s:TT * (s + 1)],
                                    in_=ev_sb[epar][:, 0:TT])], units=1, dma=True,
                        stream="qkout", after=(ev,))
                    evs.append(st)
            for bb in range(4):
                for half in range(2):
                    pb = p_mm[len(evs) % 2]
                    def vmm(te, bb=bb, s=s, half=half, pb=pb, par=par):
                        ins = None
                        for i in range(PT):
                            ins = te.matmul(
                                pb[:, 0:TT],
                                xn_bf[par][i][:, 128 * bb: 128 * bb + 128],
                                vw[:, i * D + 512 * half: i * D + 512 * half + 512],
                                start=(i == 0), stop=(i == PT - 1))
                        return ins
                    mm = p.op("tensor", vmm, after=tuple(louts) + (lw,)
                              + ((evs[-2],) if len(evs) >= 2 else ()))
                    epar = len(evs) % 2
                    ev = p.op("scalar", lambda sc, pb=pb, epar=epar: sc.activation(
                        ev_sb[epar][:, 0:TT], pb[:, 0:TT], AF.Identity),
                        after=(mm,) + ((evs[-2],) if len(evs) >= 2 else ()))
                    st = p.op("sync", lambda e, bb=bb, s=s, half=half, epar=epar: [
                        e.dma_start(out=vtm_d[TT * s + 128 * bb:TT * s + 128 * bb + 128,
                                              512 * half:512 * half + 512],
                                    in_=ev_sb[epar][:, 0:TT])], units=1, dma=True,
                        stream="qkout", after=(ev,))
                    evs.append(st)
            for bb in range(4):
                def gmm(te, s=s, bb=bb, par=par):
                    ins = None
                    for i in range(PT):
                        ins = te.matmul(p_sm[:, 32:40],
                                        xn_bf[par][i][:, 128 * bb:128 * bb + 128],
                                        gw[:, i * 8:(i + 1) * 8], start=(i == 0), stop=False)
                    return te.matmul(p_sm[:, 32:40], ones_r[0:1, :], bg_sb[0:1, :],
                                     start=False, stop=True)
                g1 = p.op("tensor", gmm, after=tuple(louts) + (lw, lw2)
                          + tuple(prev_small) + tuple(prev_rt))
                def dmm(te, s=s, bb=bb, par=par):
                    ins = None
                    for i in range(PT):
                        ins = te.matmul(p_mm[0][:, 0:96],
                                        xn_bf[par][i][:, 128 * bb:128 * bb + 128],
                                        dlw[:, i * 96:(i + 1) * 96], start=(i == 0), stop=False)
                    return te.matmul(p_mm[0][:, 0:96], ones_r[0:1, :], lp_sb[0:1, :],
                                     start=False, stop=True)
                d1 = p.op("tensor", dmm, after=tuple(louts) + (lw, lw2)
                          + ((evs[-2],) if len(evs) >= 2 else ()) + tuple(prev_small))
                sp1 = p.op("scalar", lambda sc: sc.activation(
                    gl_tm[:, 0:8], p_sm[:, 32:40], AF.Abs), after=(g1,))
                sp2 = p.op("scalar", lambda sc: sc.activation(
                    gl_tm[:, 0:8], gl_tm[:, 0:8], AF.Exp, scale=-1.0), after=(sp1,))
                sp3 = p.op("scalar", lambda sc: sc.activation(
                    gl_tm[:, 0:8], gl_tm[:, 0:8], AF.Ln, bias=onesf[:, 0:1]),
                    after=(sp2, oz))
                sp4 = p.op("vector", lambda v: v.tensor_scalar_max(
                    sp_sc[:, 0:8], p_sm[:, 32:40], 0.0), after=(g1, sp1))
                sp5 = p.op("vector", lambda v: v.tensor_tensor(
                    gl_tm[:, 0:8], gl_tm[:, 0:8], sp_sc[:, 0:8], ALU.add), after=(sp3, sp4))
                sp6 = p.op("vector", lambda v: v.tensor_scalar_mul(
                    gl_tm[:, 0:8], gl_tm[:, 0:8], -1.0), after=(sp5,))
                l1o = p.op("scalar", lambda sc: sc.activation(
                    dl_tm[:, 0:96], p_mm[0][:, 0:96], AF.Abs), after=(d1,))
                l2o = p.op("scalar", lambda sc: sc.activation(
                    dl_tm[:, 0:96], dl_tm[:, 0:96], AF.Exp, scale=-1.0), after=(l1o,))
                l3o = p.op("scalar", lambda sc: sc.activation(
                    dl_tm[:, 0:96], dl_tm[:, 0:96], AF.Ln, bias=onesf[:, 0:1]),
                    after=(l2o, oz))
                l4o = p.op("vector", lambda v: v.tensor_scalar_max(
                    sp_sc[:, 0:96], p_mm[0][:, 0:96], 0.0), after=(d1, l1o))
                l5o = p.op("vector", lambda v: v.tensor_tensor(
                    dl_tm[:, 0:96], dl_tm[:, 0:96], sp_sc[:, 0:96], ALU.add),
                    after=(l3o, l4o))
                l6o = p.op("scalar", lambda sc: sc.activation(
                    dl_tm[:, 0:96], dl_tm[:, 0:96], AF.Ln), after=(l5o,))
                tg = p.op("tensor", lambda te: te.transpose(
                    rbufs["p_tp"][0:8, 0:128],
                    gl_tm[:, 0:8], idf_sb[:]),
                    after=(sp6,) + tuple(prev_small) + tuple(prev_rt))
                ge = p.op("vector", lambda v, s=s, bb=bb: v.tensor_copy(
                    g_pm[:, TT * s + 128 * bb:TT * s + 128 * bb + 128],
                    rbufs["p_tp"][0:8, 0:128]), after=(tg,))
                tl = p.op("tensor", lambda te: te.transpose(
                    rbufs["p_tp"][0:32, 0:128], dl_tm[:, 0:32], idf_sb[:]),
                    after=(l6o, ge))
                le1 = p.op("vector", lambda v, s=s, bb=bb: v.tensor_copy(
                    lam_pm[0:32, TT * s + 128 * bb:TT * s + 128 * bb + 128],
                    rbufs["p_tp"][0:32, 0:128]), after=(tl,))
                tl2 = p.op("tensor", lambda te: te.transpose(
                    rbufs["p_tp"][0:32, 0:128], dl_tm[:, 32:64], idf_sb[:]),
                    after=(l6o, le1))
                le2 = p.op("vector", lambda v, s=s, bb=bb: v.tensor_copy(
                    lam_pm[32:64, TT * s + 128 * bb:TT * s + 128 * bb + 128],
                    rbufs["p_tp"][0:32, 0:128]), after=(tl2,))
                tl3 = p.op("tensor", lambda te: te.transpose(
                    rbufs["p_tp"][0:32, 0:128], dl_tm[:, 64:96], idf_sb[:]),
                    after=(l6o, le2))
                le3 = p.op("vector", lambda v, s=s, bb=bb: v.tensor_copy(
                    lam_pm[64:96, TT * s + 128 * bb:TT * s + 128 * bb + 128],
                    rbufs["p_tp"][0:32, 0:128]), after=(tl3,))
                prev_small = [sp6, le1, le2, le3, ge]
                gl_ops += [sp6, le3]
            prev_xn_cons[par] = (xnst, evs[-1], prev_small[-1])
        gcs = p.op("vector", lambda v: v.tensor_tensor_scan(
            g2_pm[0:8, :], gz[0:8, :], g_pm[0:8, :], 0.0, ALU.add, ALU.add),
            after=tuple(gl_ops) + (oz2,))
        gst = p.op("sync", lambda e: [e.dma_start(out=g_d[:, :], in_=g2_pm[0:8, :])],
                   units=1, dma=True, stream="gst", after=(gcs,))
        lst = p.op("sync", lambda e: [e.dma_start(out=lam_d[:, :], in_=lam_pm[:, :])],
                   units=1, dma=True, stream="gst", after=tuple(gl_ops))
        ph1 = (gst, lst, evs[-1], evs[-2], xnst)
    return ph1, (xn_d, qfm_d, kfm_d, vtm_d, aacc_d, lam_d, g_d), ohp_rows


def _l1_level_vectors(T):
    LV = 12
    d = np.arange(-(T - 1), T)
    lvl = np.where(d <= 0, 0, np.floor(np.log2(np.maximum(d, 1))).astype(np.int64) + 1)
    lvl = np.minimum(lvl, LV - 1)
    av = np.full((LV, 2 * T - 1), -1e30, np.float32)
    for l in range(LV):
        av[l, (lvl == l) & (d >= 0)] = 0.0
    return av


def emit_l1_attn_heads(p, nc, ctx, T, scratch, WoT, lngcol, lnbcol, x1,
                       aacc_d, entry=()):
    xn_d, qfm_d, kfm_d, vtm_d, _aacc, lam_d, g_d = scratch
    H, S, LV = 8, 128, 12
    TT = 512
    NS = T // TT
    NB = T // 128
    import ml_dtypes
    av_np = _l1_level_vectors(T)
    npad = ceil_div(2 * T - 1 + 128, 128) * 128
    import ml_dtypes as _mld
    av_pad = np.full((LV, npad), -1e30, np.float32)
    av_pad[:, :2 * T - 1] = av_np
    av_h = nc.inline_tensor(av_pad.reshape(-1).astype(_mld.bfloat16), name=_nm("l1av"))
    av_int = nc.dram_tensor(_nm("l1avi"), [LV * npad], BF16, kind="Internal")
    av_d = av_int.ap()
    prev_h = nc.inline_tensor(np.eye(128)[::-1].copy().astype(_mld.bfloat16),
                              name=_nm("prev"))
    def levels_for(doff):
        lo, hi = 128 * doff - 127, 128 * doff + 511
        out = []
        for l in range(LV):
            if l == 0:
                blo, bhi = 0, 0
            elif l < LV - 1:
                blo, bhi = 2 ** (l - 1), 2 ** l - 1
            else:
                blo, bhi = 2 ** (l - 1), 10 ** 9
            if bhi >= lo and blo <= hi:
                pure = blo <= max(lo, 0) and bhi >= hi
                out.append((l, pure))
        return out

    with ExitStack() as c2:
        ao_sb = [SB(nc, c2, [128, T], BF16, "ao1") for _ in range(PT)]
        p_sc = [PS(nc, c2, [128, TT], F32, "psc") for _ in range(2)]
        p_ar = [PS(nc, c2, [128, TT], F32, "par") for _ in range(2)]
        p_o = PS(nc, c2, [128, TT], F32, "po1")
        c2b = ctx.enter_context(ExitStack()) if False else None
        import contextlib
        c2b = c2.enter_context(contextlib.ExitStack())
        q_sb = SB(nc, c2b, [128, T], BF16, "qh1")
        k_sb = SB(nc, c2b, [128, T], BF16, "kh1")
        v_sb = SB(nc, c2b, [128, T], BF16, "vh1")
        ga_sb = SB(nc, c2b, [3, T], F32, "ga")
        garhs = SB(nc, c2b, [3, LV * TT], F32, "garhs")
        t2c = [SB(nc, c2b, [128, TT], BF16, "t2c") for _ in range(4)]
        prev_sb = SB(nc, c2b, [128, 128], BF16, "prevs")
        e_sb = [SB(nc, c2b, [128, TT], F32, "esb") for _ in range(4)]
        a_sb = [SB(nc, c2b, [128, TT], BF16, "asb") for _ in range(4)]
        oev = SB(nc, c2b, [128, TT], BF16, "oev1")

        neg1_h = nc.inline_tensor(np.full((1, LV * TT), -1.0, np.float32),
                                  name=_nm("neg1"))
        lc = p.op("sync", lambda e: [
            e.dma_start(out=prev_sb[:], in_=prev_h.ap()[:, :]),
            e.dma_start(out=garhs[1:2, :], in_=neg1_h.ap()[:, :]),
            e.dma_start(out=av_int.ap()[:], in_=av_h.ap()[:])],
            units=3, dma=True, stream="cst", after=entry)

        hdone = entry
        t2_reader = {}
        for h in range(H):
            ld = p.op("sync", lambda e, h=h: [
                e.dma_start(out=q_sb[:, :], in_=qfm_d[128 * h:128 * h + 128, :]),
                e.dma_start(out=k_sb[:, :], in_=kfm_d[128 * h:128 * h + 128, :]),
                e.dma_start(out=v_sb[:].rearrange("p (b c) -> p b c", c=128),
                            in_=vtm_d.rearrange("(b p) c -> p b c", p=128)[:, :, 128 * h:128 * h + 128])],
                units=3, dma=True, stream="hld", after=tuple(hdone) + entry)
            gz1 = p.op("vector", lambda v: v.memset(ga_sb[0:3, :], 1.0),
                       after=tuple(hdone) + entry)
            gz2 = gz1
            gld = p.op("gpsimd", lambda e, h=h: [
                e.dma_start(out=ga_sb[1:2, :], in_=g_d[h:h + 1, :])],
                units=1, dma=True, stream="gld", after=tuple(hdone) + (gz1,) + entry)
            a_ops = []
            ex_ops = []
            t2ops = []
            prev_garhs_cons = None
            fin = None
            for tc in range(NS):
                gz3 = lc
                grh = p.op("gpsimd", lambda e, h=h, tc=tc: [
                    e.dma_start(out=garhs[0:1, l * TT:(l + 1) * TT],
                                in_=g_d[h:h + 1, TT * tc:TT * (tc + 1)])
                    for l in range(LV)] + [
                    e.dma_start(out=garhs[2:3, l * TT:(l + 1) * TT],
                                in_=lam_d[12 * h + l:12 * h + l + 1, TT * tc:TT * (tc + 1)])
                    for l in range(LV)],
                    units=2 * LV, dma=True, stream="gld",
                    after=((prev_garhs_cons,) if prev_garhs_cons else tuple(hdone)))
                units = []
                for kk in range(4 * tc + 4):
                    doff = 4 * tc - kk
                    for (l, pure) in levels_for(doff):
                        units.append((kk, l, pure))
                NU = len(units)
                t2_of = {}
                mi_of = {}
                mcount = 0
                for ui, (kk, l, pure) in enumerate(units):
                    if not pure:
                        mi_of[ui] = mcount
                        mcount += 1

                def issue_t2(ui2):
                    kk2, l2, pure2 = units[ui2]
                    if pure2 or ui2 in t2_of:
                        return
                    mi = mi_of[ui2]
                    slot = mi % 4
                    doff2 = 4 * tc - kk2
                    c0 = 128 * doff2 + (T - 1) - 127
                    war = t2_reader.get(slot)
                    t2 = p.op("gpsimd", lambda e, l2=l2, c0=c0, slot=slot: [
                        e.dma_start(out=t2c[slot][:],
                                    in_=bass.AP(av_d.tensor, l2 * npad + c0,
                                                [[1, 128], [1, TT]]))],
                        units=1, dma=True, stream=f"t2{slot % 2}",
                        after=((war,) if war else ()) + (lc,) + tuple(t2ops[-2:]))
                    t2ops.append(t2)
                    t2_of[ui2] = (t2, slot)

                smm_bykk = {}
                oacc = []
                upars = {}
                at_tc = []
                for ui, (kk, l, pure) in enumerate(units):
                    for ahead in range(3):
                        if ui + ahead < NU:
                            issue_t2(ui + ahead)
                    if kk not in smm_bykk:
                        psc = p_sc[len(smm_bykk) % 2]
                        smm = p.op("tensor", lambda te, kk=kk, tc=tc, psc=psc: te.matmul(
                            psc[:, 0:TT], k_sb[:, 128 * kk:128 * kk + 128],
                            q_sb[:, TT * tc:TT * (tc + 1)], start=True, stop=True),
                            after=(ld,) + tuple(a_ops[-4:]))
                        smm_bykk[kk] = (smm, psc)
                    smm, psc = smm_bykk[kk]
                    par = len(ex_ops) % 4
                    upars[ui] = par
                    pa = p_ar[len(ex_ops) % 2]
                    t2slot = t2_of[ui][1] if not pure else None
                    def argmm(te, l=l, kk=kk, tc=tc, pure=pure, par=par, pa=pa,
                              t2slot=t2slot):
                        ins = None
                        if not pure:
                            ins = te.matmul(pa[:, 0:TT], prev_sb[:],
                                            t2c[t2slot][:],
                                            start=True, stop=False)
                        ins = te.matmul(pa[:, 0:TT], ga_sb[:, 128 * kk:128 * kk + 128],
                                        garhs[:, l * TT:(l + 1) * TT],
                                        start=pure, stop=True)
                        return ins
                    t2dep = (t2_of[ui][0],) if not pure else ()
                    am = p.op("tensor", argmm,
                              after=(gld, grh, gz1, gz2, gz3, lc) + t2dep
                              + ((ex_ops[-2],) if len(ex_ops) >= 2 else ()))
                    if not pure:
                        t2_reader[t2_of[ui][1]] = am
                    ex = p.op("scalar", lambda sc, par=par, pa=pa: sc.activation(
                        e_sb[par][:, 0:TT], pa[:, 0:TT], AF.Exp),
                        after=(am,) + ((a_ops[-4],) if len(a_ops) >= 4 else ()))
                    ex_ops.append(ex)
                    at = p.op("vector", lambda v, par=par, psc=psc: v.tensor_tensor(
                        a_sb[par][:, 0:TT], psc[:, 0:TT], e_sb[par][:, 0:TT],
                        ALU.mult), after=(ex, smm)
                        + ((a_ops[-4],) if len(a_ops) >= 4 else ())
                        + ((oacc[ui - 3],) if ui >= 3 else ()))
                    a_ops.append(at)
                    at_tc.append(at)
                    for uo in ([ui - 2] if ui >= 2 else []) + ([NU - 2, NU - 1] if ui == NU - 1 else []):
                        kko = units[uo][0]
                        om = p.op("tensor", lambda te, kko=kko, uo=uo, NU=NU, sl=upars[uo]: te.matmul(
                            p_o[:, 0:TT], v_sb[:, 128 * kko:128 * kko + 128],
                            a_sb[sl][:, 0:TT], start=(uo == 0), stop=(uo == NU - 1)),
                            after=(at_tc[uo], ld) + (((fin,)) if (uo == 0 and fin) else ()))
                        oacc.append(om)
                oe = p.op("vector", lambda v, tc=tc: v.tensor_copy(
                    oev[:, 0:TT], p_o[:, 0:TT]), after=(oacc[-1],))
                fin = p.op("vector", lambda v, h=h, tc=tc: v.tensor_copy(
                    ao_sb[h][:, TT * tc:TT * (tc + 1)], oev[:, 0:TT]), after=(oe,))
                prev_garhs_cons = am
            hdone = (fin, ex_ops[-1], ex_ops[-2] if len(ex_ops) >= 2 else ex_ops[-1])
        c2b.close()
        with ExitStack() as c3:
            import ml_dtypes as mld
            ones_c2_h = nc.inline_tensor(np.ones((128, 1), mld.bfloat16), name=_nm("onc"))
            ones_r2_h = nc.inline_tensor(np.ones((1, 128), np.float32), name=_nm("onr"))
            wow = SB(nc, c3, [128, PT * D], BF16, "wow")
            lng = SB(nc, c3, [128, PT], F32, "lng")
            lnb = SB(nc, c3, [128, PT], F32, "lnb")
            y2 = [SB(nc, c3, [128, TT], F32, "y2") for _ in range(PT)]
            y2n = [SB(nc, c3, [128, TT], F32, "y2n") for _ in range(PT)]
            xn_t = [SB(nc, c3, [128, TT], BF16, "xnt") for _ in range(PT)]
            x1_t = [SB(nc, c3, [128, TT], F32, "x1t") for _ in range(2)]
            ones_c2 = SB(nc, c3, [128, 1], BF16, "oc2")
            ones_r2 = SB(nc, c3, [1, 128], F32, "or2")
            lnbufs2 = dict(
                stat=SB(nc, c3, [1, 3 * TT], F32, "st2"),
                xbf=[SB(nc, c3, [128, TT], BF16, "xc2")],
                sq=SB(nc, c3, [128, TT], BF16, "sq2"),
                nsc=[SB(nc, c3, [128, TT], F32, "ns2")],
            )
            lnbufs2["p_sum1"] = PS(nc, c3, [1, TT], F32, "ps1b")
            lnbufs2["p_sum2"] = PS(nc, c3, [1, TT], F32, "ps2b")
            lnbufs2["p_bc1"] = p_sc[0]
            lnbufs2["p_bc2"] = p_sc[1]
            p_pr = p_ar
            lw2 = load_weight_bf16(p, wow, WoT, D, D, after=tuple(hdone))
            lcs = p.op("sync", lambda e: [
                e.dma_start(out=lng[:], in_=lngcol[:, :]),
                e.dma_start(out=lnb[:], in_=lnbcol[:, :]),
                e.dma_start(out=ones_c2[:], in_=ones_c2_h.ap()[:, :]),
                e.dma_start(out=ones_r2[:], in_=ones_r2_h.ap()[:, :])],
                units=4, dma=True, stream="cst", after=tuple(hdone))
            prev_ln2 = list(hdone)
            prev_st = None
            for s in range(NS):
                xld = p.op("sync", lambda e, s=s: [
                    e.dma_start(out=xn_t[i][:], in_=xn_d[128 * i:128 * (i + 1), TT * s:TT * (s + 1)])
                    for i in range(PT)], units=PT, dma=True, stream="pl",
                    after=tuple(prev_ln2))
                mms = []
                for o in range(PT):
                    ppr = p_pr[o % 2]
                    mm = mm_chain(p, ppr[:, 0:TT],
                                  [wtile(wow, i, o, D) for i in range(PT)],
                                  [ao_sb[i][:, TT * s:TT * (s + 1)] for i in range(PT)],
                                  after=(lw2, xld) + tuple(prev_ln2)
                                  + ((mms[-2],) if len(mms) >= 2 else ()))
                    ad = p.op("vector", lambda v, o=o, ppr=ppr: v.tensor_tensor(
                        y2[o][:, 0:TT], ppr[:, 0:TT], xn_t[o][:], ALU.add),
                        after=(mm, xld))
                    mms.append(ad)
                louts = emit_ln(p, nc, ctx, y2, y2n, TT, ones_c2, ones_r2, lnbufs2,
                                x_ready=tuple(mms) + (lcs,))
                sts = []
                for o in range(PT):
                    xpar = o % 2
                    x1l = p.op("gpsimd", lambda e, s=s, o=o, xpar=xpar: [
                        e.dma_start(out=x1_t[xpar][:],
                                    in_=x1[128 * o:128 * (o + 1), TT * s:TT * (s + 1)])],
                        units=1, dma=True, stream=f"x1l{xpar}",
                        after=((sts[-2],) if len(sts) >= 2 else tuple(prev_ln2)))
                    gb = p.op("vector", lambda v, o=o: v.tensor_scalar(
                        y2n[o][:, 0:TT], y2n[o][:, 0:TT], lng[:, o:o + 1],
                        lnb[:, o:o + 1], ALU.mult, ALU.add), after=tuple(louts))
                    ax = p.op("vector", lambda v, o=o, xpar=xpar: v.tensor_tensor(
                        y2n[o][:, 0:TT], y2n[o][:, 0:TT], x1_t[xpar][:], ALU.add),
                        after=(gb, x1l))
                    st = p.op("sync", lambda e, o=o, s=s: [
                        e.dma_start(out=aacc_d[128 * o:128 * (o + 1), TT * s:TT * (s + 1)],
                                    in_=y2n[o][:, 0:TT])], units=1, dma=True,
                        stream="aout", after=(ax,))
                    sts.append(st)
                prev_ln2 = sts
                prev_st = st
            return (prev_st,)


def emit_l1_moe(p, nc, ctx, T, xn_d, aacc_d, ohp_rows, w1T, b1col, w2T, b2col,
                x2_out, entry=()):
    NE, HH, TT = 32, 256, 512
    NS = T // TT
    MH = HH // 128
    import ml_dtypes
    ones_r_h = nc.inline_tensor(np.ones((1, 128), np.float32), name=_nm("onr"))
    moe_d = nc.dram_tensor(_nm("l1moe"), [D, T], F32, kind="Internal").ap()

    with ExitStack() as c1:
        w1_sb = [SB(nc, c1, [128, PT * HH], BF16, "w1e") for _ in range(2)]
        w2_sb = [SB(nc, c1, [128, MH * D], BF16, "w2e") for _ in range(2)]
        b1_sb = [SB(nc, c1, [128, MH], F32, "b1e") for _ in range(2)]
        b2_sb = [SB(nc, c1, [128, PT], F32, "b2e") for _ in range(2)]
        xn_t = [SB(nc, c1, [128, T], BF16, "xnt") for _ in range(PT)]
        h_sb = [SB(nc, c1, [128, TT], BF16, "hsb") for _ in range(MH)]
        y_sb = [SB(nc, c1, [128, TT], BF16, "ysb") for _ in range(2)]
        bc_sb = SB(nc, c1, [128, TT], BF16, "bcs")
        ohp_row = SB(nc, c1, [1, T], F32, "ohpr")
        a_t = [SB(nc, c1, [128, TT], F32, "at2") for _ in range(2)]
        o_t = [SB(nc, c1, [128, TT], F32, "ot2") for _ in range(2)]
        ones_r = SB(nc, c1, [1, 128], F32, "orr")
        p_h = [PS(nc, c1, [128, TT], F32, "phh") for _ in range(2)]
        p_y = [PS(nc, c1, [128, TT], F32, "pyy") for _ in range(2)]
        p_bc = PS(nc, c1, [128, TT], F32, "pbc")

        lor = p.dma("sync", ones_r[:], ones_r_h.ap()[:, :], stream="cst", after=entry)
        xlall = p.op("sync", lambda e: [
            e.dma_start(out=xn_t[i][:], in_=xn_d[128 * i:128 * (i + 1), :])
            for i in range(PT)], units=PT, dma=True, stream="mx", after=entry)

        last_st = {}
        prev_bc = None
        last_tt = None
        prev_y = [None, None]
        lws = {}
        for e in range(NE):
            wpar = e % 2
            lw = p.op("gpsimd", lambda g, e=e, wpar=wpar: [
                g.dma_start(out=w1_sb[wpar][:].rearrange("p (i o) -> p i o", o=HH),
                            in_=w1T[e].rearrange("(i p) o -> p i o", p=128)),
                g.dma_start(out=w2_sb[wpar][:].rearrange("p (i o) -> p i o", o=D),
                            in_=w2T[e].rearrange("(i p) o -> p i o", p=128)),
                g.dma_start(out=b1_sb[wpar][:], in_=b1col[e]),
                g.dma_start(out=b2_sb[wpar][:], in_=b2col[e])],
                units=4, dma=True, stream=f"wmoe{wpar}",
                after=(lws[e - 2] if e - 2 in lws else tuple(entry)) if False else
                (tuple(lws[e - 2]) if e - 2 in lws else tuple(entry)))
            ohl = p.op("gpsimd", lambda eng, e=e: [
                eng.dma_start(out=ohp_row[0:1, :], in_=ohp_rows[e:e + 1, :])],
                units=1, dma=True, stream="ohl", after=((prev_bc,) if prev_bc else entry))
            e_read_w = []
            for s in range(NS):
                bc = p.op("tensor", lambda te, s=s: te.matmul(
                    p_bc[:, 0:TT], ones_r[0:1, :], ohp_row[0:1, TT * s:TT * (s + 1)],
                    start=True, stop=True),
                    after=(lor, ohl) + ((prev_bc,) if prev_bc else ()))
                bce = p.op("vector", lambda v: v.tensor_copy(bc_sb[:, 0:TT], p_bc[:, 0:TT]),
                           after=(bc,) + ((last_tt,) if last_tt else ()))
                prev_bc = bce
                h_evs = []
                for m in range(MH):
                    pb = p_h[m % 2]
                    mm = mm_chain(p, pb[:, 0:TT],
                                  [wtile(w1_sb[wpar], i, m, HH) for i in range(PT)],
                                  [xn_t[i][:, TT * s:TT * (s + 1)] for i in range(PT)],
                                  after=(xlall, lw)
                                  + ((h_evs[-2],) if len(h_evs) >= 2 else ())
                                  + ((last_st.get((s, 0)),) if (s, 0) in last_st else ()))
                    ev = p.op("scalar", lambda sc, m=m, pb=pb, wpar=wpar: sc.activation(
                        h_sb[m][:, 0:TT], pb[:, 0:TT], AF.Relu,
                        bias=b1_sb[wpar][:, m:m + 1]), after=(mm,))
                    h_evs.append(ev)
                    e_read_w.append(mm)
                    e_read_w.append(ev)
                for o in range(PT):
                    opar = o % 2
                    py = p_y[opar]
                    mm = mm_chain(p, py[:, 0:TT],
                                  [wtile(w2_sb[wpar], mi, o, D) for mi in range(MH)],
                                  [h_sb[mi][:, 0:TT] for mi in range(MH)],
                                  after=tuple(h_evs) + (lw,)
                                  + ((prev_y[opar],) if prev_y[opar] else ()))
                    e_read_w.append(mm)
                    bi = p.op("scalar", lambda sc, o=o, opar=opar, py=py, wpar=wpar: sc.activation(
                        y_sb[opar][:, 0:TT], py[:, 0:TT], AF.Relu,
                        bias=b2_sb[wpar][:, o:o + 1]), after=(mm,)
                        + ((prev_y[opar],) if prev_y[opar] else ()))
                    e_read_w.append(bi)
                    tt = p.op("vector", lambda v, opar=opar: v.tensor_tensor(
                        y_sb[opar][:, 0:TT], y_sb[opar][:, 0:TT], bc_sb[:, 0:TT],
                        ALU.mult), after=(bi, bce))
                    last_tt = tt
                    ac = p.op("gpsimd", lambda g, e=e, o=o, s=s, opar=opar: [
                        g.dma_start(out=moe_d[128 * o:128 * (o + 1), TT * s:TT * (s + 1)],
                                    in_=y_sb[opar][:, 0:TT],
                                    **({} if e == 0 else {"accum_op": ALU.add}))],
                        units=1, dma=True, stream=f"macc{o % 4}", after=(tt,))
                    prev_y[opar] = ac
                    last_st[(s, o)] = ac
            lws[e] = tuple(e_read_w[-2:])
        fins = []
        last_fin = [None, None]
        for s in range(NS):
            for o in range(PT):
                par = (s * PT + o) % 2
                ld = p.op("sync", lambda eng, s=s, o=o, par=par: [
                    eng.dma_start(out=a_t[par][:],
                                  in_=aacc_d[128 * o:128 * (o + 1), TT * s:TT * (s + 1)]),
                    eng.dma_start(out=o_t[par][:],
                                  in_=moe_d[128 * o:128 * (o + 1), TT * s:TT * (s + 1)])],
                    units=2, dma=True, stream=f"cmb{par}",
                    after=((last_fin[par],) if last_fin[par] else ())
                    + (last_st[(s, o)],))
                ad = p.op("vector", lambda v, s=s, o=o, par=par: v.tensor_tensor(
                    o_t[par][:], a_t[par][:], o_t[par][:], ALU.add), after=(ld,))
                st = p.op("sync", lambda eng, s=s, o=o, par=par: [
                    eng.dma_start(out=x2_out[128 * o:128 * (o + 1), TT * s:TT * (s + 1)],
                                  in_=o_t[par][:])], units=1, dma=True,
                    stream="x2o", after=(ad,))
                last_fin[par] = st
                fins.append(st)
        return (fins[-1],)


def build_L1(T=4096):
    nc = bass.Bass("TRN2", target_bir_lowering=False)
    x1 = nc.dram_tensor("x1", [D, T], F32, kind="ExternalInput").ap()
    WqT = nc.dram_tensor("WqT", [D, D], F32, kind="ExternalInput").ap()
    WkT = nc.dram_tensor("WkT", [D, D], F32, kind="ExternalInput").ap()
    WvT = nc.dram_tensor("WvT", [D, D], F32, kind="ExternalInput").ap()
    WgT = nc.dram_tensor("WgT", [D, 8], F32, kind="ExternalInput").ap()
    bg = nc.dram_tensor("bg", [1, 8], F32, kind="ExternalInput").ap()
    WdlT = nc.dram_tensor("WdlT", [D, 96], F32, kind="ExternalInput").ap()
    Lpar = nc.dram_tensor("Lpar", [1, 96], F32, kind="ExternalInput").ap()
    WoT = nc.dram_tensor("WoT", [D, D], F32, kind="ExternalInput").ap()
    lngcol = nc.dram_tensor("lngcol", [128, PT], F32, kind="ExternalInput").ap()
    lnbcol = nc.dram_tensor("lnbcol", [128, PT], F32, kind="ExternalInput").ap()
    rwT = nc.dram_tensor("rwT", [D, 32], F32, kind="ExternalInput").ap()
    rb = nc.dram_tensor("rb", [1, 32], F32, kind="ExternalInput").ap()
    w1T = nc.dram_tensor("w1T", [32, D, 256], F32, kind="ExternalInput").ap()
    b1col = nc.dram_tensor("b1col", [32, 128, 2], F32, kind="ExternalInput").ap()
    w2T = nc.dram_tensor("w2T", [32, 256, D], F32, kind="ExternalInput").ap()
    b2col = nc.dram_tensor("b2col", [32, 128, PT], F32, kind="ExternalInput").ap()
    x2 = nc.dram_tensor("x2", [D, T], F32, kind="ExternalOutput").ap()
    with ExitStack() as ctx:
        p = Prog(nc, ctx)
        ph1, scratch, ohp_rows = emit_l1(p, nc, ctx, T, x1, WqT, WkT, WvT, WgT,
                                         bg, WdlT, Lpar, WoT, lngcol, lnbcol,
                                         rwT, rb, w1T, b1col, w2T, b2col, x2)
        ph2 = emit_l1_attn_heads(p, nc, ctx, T, scratch, WoT, lngcol, lnbcol,
                                 x1, scratch[4], entry=tuple(ph1))
        fins = emit_l1_moe(p, nc, ctx, T, scratch[0], scratch[4], ohp_rows,
                           w1T, b1col, w2T, b2col, x2, entry=tuple(ph2))
        p.op("gpsimd", lambda g: g.nop(), after=tuple(fins))
        p.emit()
    return nc


def emit_l2(p, nc, ctx, T, x2, WinT, wconvc, bconvc, WxT, WdtT, bdtc, AlogC,
            lngcol, lnbcol, WoutT, boutc, WrT, br_row, W1T, b1col, W2T, b2col,
            x3_out, entry=()):
    TT = 512
    NS = T // TT
    NN = 16
    DFF = 4096
    import ml_dtypes
    ones_c_h = nc.inline_tensor(np.ones((128, 1), ml_dtypes.bfloat16), name=_nm("onc"))
    ones_r_h = nc.inline_tensor(np.ones((1, 128), np.float32), name=_nm("onr"))
    idf_h = nc.inline_tensor(np.eye(128, dtype=np.float32), name=_nm("idf"))

    xn_d = nc.dram_tensor(_nm("l2xn"), [D, T], BF16, kind="Internal").ap()
    gate_d = nc.dram_tensor(_nm("l2gt"), [D, T], BF16, kind="Internal").ap()
    delta_d = nc.dram_tensor(_nm("l2dl"), [D, T], F32, kind="Internal").ap()
    bc_d = nc.dram_tensor(_nm("l2bc"), [32, T], F32, kind="Internal").ap()
    y_d = nc.dram_tensor(_nm("l2y"), [D, T], F32, kind="Internal").ap()
    aacc_d = nc.dram_tensor(_nm("l2aa"), [D, T], F32, kind="Internal").ap()

    oh_sb = SB(nc, ctx, [1, 32], F32, "l2oh")
    idx_sb = SB(nc, ctx, [1, 1], I32, "l2idx")
    iota8_np = np.arange(8, dtype=np.float32)[None, :]
    iota8_h = nc.inline_tensor(iota8_np, name=_nm("iota8"))
    iota8_sb = SB(nc, ctx, [1, 8], F32, "iota8")
    with ExitStack() as c1:
        winw = SB(nc, c1, [128, PT * 2 * D], BF16, "winw")
        wdtw = SB(nc, c1, [128, PT * D], BF16, "wdtw")
        wxw = SB(nc, c1, [128, PT * 32], BF16, "wxw")
        wrw = SB(nc, c1, [128, PT * 8], F32, "wrw")
        br_sb = SB(nc, c1, [1, 8], F32, "brs")
        cc_sb = SB(nc, c1, [128, 4 * PT], F32, "ccs")
        cb_sb = SB(nc, c1, [128, PT], F32, "cbs")
        dt_sb = SB(nc, c1, [128, PT], F32, "dts")
        x_f = [[SB(nc, c1, [128, TT], F32, "x") for _ in range(PT)] for _ in range(2)]
        xn_bf = [[SB(nc, c1, [128, TT], BF16, "xn") for _ in range(PT)] for _ in range(2)]
        xn32 = [SB(nc, c1, [128, TT], F32, "xn32") for _ in range(PT)]
        xin_sb = [SB(nc, c1, [128, TT + 4], BF16, "xin") for _ in range(PT)]
        z_sb = [SB(nc, c1, [128, TT], F32, "z") for _ in range(2)]
        gt_sb = [SB(nc, c1, [128, TT], BF16, "gt") for _ in range(PT)]
        cv_sb = [SB(nc, c1, [128, TT], F32, "cv") for _ in range(2)]
        sp_sb = [SB(nc, c1, [128, TT], F32, "sp") for _ in range(2)]
        xmean = SB(nc, c1, [128, PT + 2], F32, "xmean")
        ones_c = SB(nc, c1, [128, 1], BF16, "oc")
        ones_r = SB(nc, c1, [1, 128], F32, "orr")
        idf_sb = SB(nc, c1, [128, 128], F32, "idf")
        onesf = SB(nc, c1, [128, 1], F32, "onesf")
        lnbufs = dict(
            stat=SB(nc, c1, [1, 3 * TT], F32, "st"),
            xbf=[SB(nc, c1, [128, TT], BF16, "xc")],
            sq=SB(nc, c1, [128, TT], BF16, "sqq"),
            nsc=[SB(nc, c1, [128, TT], F32, "nsc")],
        )
        lnbufs["p_sum1"] = PS(nc, c1, [1, TT], F32, "ps1")
        lnbufs["p_sum2"] = PS(nc, c1, [1, TT], F32, "ps2")
        lnbufs["p_bc1"] = PS(nc, c1, [128, TT], F32, "pb1")
        lnbufs["p_bc2"] = PS(nc, c1, [128, TT], F32, "pb2")
        p_mm = [PS(nc, c1, [128, TT], F32, "pmm") for _ in range(2)]
        p_sm = PS(nc, c1, [128, 64], F32, "psm")

        lw = p.op("gpsimd", lambda g: [
            g.dma_start(out=winw[:].rearrange("p (i o) -> p i o", o=2 * D),
                        in_=WinT.rearrange("(i p) o -> p i o", p=128)),
            g.dma_start(out=wdtw[:].rearrange("p (i o) -> p i o", o=D),
                        in_=WdtT.rearrange("(i p) o -> p i o", p=128)),
            g.dma_start(out=wxw[:].rearrange("p (i o) -> p i o", o=32),
                        in_=WxT.rearrange("(i p) o -> p i o", p=128))],
            units=3, dma=True, stream="wld", after=entry)
        lw2 = p.op("sync", lambda e: [
            e.dma_start(out=wrw[:].rearrange("p (i o) -> p i o", o=8),
                        in_=WrT.rearrange("(i p) o -> p i o", p=128)),
            e.dma_start(out=br_sb[:], in_=br_row[:, :]),
            e.dma_start(out=cc_sb[:], in_=wconvc[:, :]),
            e.dma_start(out=cb_sb[:], in_=bconvc[:, :]),
            e.dma_start(out=dt_sb[:], in_=bdtc[:, :]),
            e.dma_start(out=ones_c[:], in_=ones_c_h.ap()[:, :]),
            e.dma_start(out=ones_r[:], in_=ones_r_h.ap()[:, :]),
            e.dma_start(out=idf_sb[:], in_=idf_h.ap()[:, :]),
            e.dma_start(out=iota8_sb[:], in_=iota8_h.ap()[:, :])],
            units=9, dma=True, stream="cst", after=entry)
        lio = lw2
        zpad = [p.op("vector", lambda v, i=i: v.memset(xin_sb[i][:, 0:4], 0.0),
                     after=entry) for i in range(PT)]
        oz = p.op("vector", lambda v: v.memset(onesf[:], 1.0), after=entry)
        mz = p.op("vector", lambda v: v.memset(xmean[:], 0.0), after=entry)

        prev_ln = []
        prev_x_cons = [None, None]
        prev_xn_cons = [(), ()]
        evs = []
        mean_ops = []
        xz_done = []
        last_conv = {}
        for s in range(NS):
            par = s % 2
            xl = p.op("sync", lambda e, s=s, par=par: [
                e.dma_start(out=x_f[par][i][:], in_=x2[128 * i:128 * (i + 1), TT * s:TT * (s + 1)])
                for i in range(PT)], units=PT, dma=True, stream=f"ax{par}",
                after=((prev_x_cons[par],) if prev_x_cons[par] else ()) + entry)
            xn_view = [xn_bf[par][i][:] for i in range(PT)]
            louts = emit_ln(p, nc, ctx, [x_f[par][i] for i in range(PT)], xn_view, TT,
                            ones_c, ones_r, lnbufs, x_ready=(xl, lw2) + tuple(prev_ln)
                            + tuple(prev_xn_cons[par]),
                            xn32_tiles=xn32)
            prev_ln = list(louts)
            prev_x_cons[par] = louts[-1]
            xnst = p.op("sync", lambda e, s=s, par=par: [
                e.dma_start(out=xn_d[128 * i:128 * (i + 1), TT * s:TT * (s + 1)],
                            in_=xn_bf[par][i][:]) for i in range(PT)],
                units=PT, dma=True, stream="xnd", after=tuple(louts))
            for i in range(PT):
                mo = p.op("vector", lambda v, i=i, s=s: v.tensor_scalar(
                    xmean[:, PT:PT + 1], xn32[i][:], 1.0, None, ALU.mult,
                    accum_out=None) if False else v.reduce_sum(
                    xmean[:, PT + 1:PT + 2], xn32[i][:], AX.X),
                    after=tuple(louts) + tuple(mean_ops[-1:]))
                ac = p.op("vector", lambda v, i=i, s=s: v.tensor_tensor(
                    xmean[:, i:i + 1], xmean[:, i:i + 1], xmean[:, PT + 1:PT + 2],
                    ALU.add), after=(mo, mz) + ((mean_ops[-1],) if mean_ops else ()))
                mean_ops.append(ac)
            xz_mm_last = None
            for o in range(2 * PT):
                pb = p_mm[o % 2]
                mm = mm_chain(p, pb[:, 0:TT],
                              [wtile(winw, i, o, 2 * D) for i in range(PT)],
                              [xn_bf[par][i][:] for i in range(PT)],
                              after=tuple(louts) + (lw,)
                              + ((evs[-2],) if len(evs) >= 2 else ()))
                xz_mm_last = mm
                if o < PT:
                    if s == 0:
                        carry = zpad[o]
                    else:
                        carry = p.op("vector", lambda v, o=o: v.tensor_copy(
                            xin_sb[o][:, 0:4], xin_sb[o][:, TT:TT + 4]),
                            after=(zpad[o],) + tuple(last_conv.get(o, ())))
                    ev = p.op("vector", lambda v, o=o, s=s, pb=pb: v.tensor_copy(
                        xin_sb[o][:, 4:4 + TT], pb[:, 0:TT]),
                        after=(mm, carry) + ((evs[-2],) if len(evs) >= 2 else ()))
                else:
                    ev = p.op("vector", lambda v, o=o, s=s, pb=pb: v.tensor_copy(
                        z_sb[s % 2][:] if False else z_sb[(o + s) % 2][:], pb[:, 0:TT]),
                        after=(mm,) + ((evs[-2],) if len(evs) >= 2 else ()))
                    oo = o - PT
                    def conv_fn(v, oo=oo, s=s, zpar=(o + s) % 2):
                        r = v.tensor_scalar(cv_sb[s % 2][:, 0:TT],
                                            xin_sb[oo][:, 1: 1 + TT],
                                            cc_sb[:, 4 * oo + 0:4 * oo + 1], None,
                                            ALU.mult)
                        return r
                    cv1 = p.op("vector", conv_fn, after=(ev, evs[-1] if evs else ev, lw2)
                               + tuple(xz_done[-1:]))
                    cvs = [cv1]
                    for j in range(1, 4):
                        cvj = p.op("vector", lambda v, oo=oo, s=s, j=j: v.scalar_tensor_tensor(
                            cv_sb[s % 2][:, 0:TT],
                            xin_sb[oo][:, 1 + j: 1 + j + TT],
                            cc_sb[:, 4 * oo + j:4 * oo + j + 1],
                            cv_sb[s % 2][:, 0:TT], ALU.mult, ALU.add),
                            after=(cvs[-1],))
                        cvs.append(cvj)
                    si = p.op("scalar", lambda sc, oo=oo, s=s: sc.activation(
                        sp_sb[s % 2][:, 0:TT], cv_sb[s % 2][:, 0:TT], AF.Silu,
                        bias=cb_sb[:, oo:oo + 1]), after=(cvs[-1], lw2))
                    gt = p.op("vector", lambda v, oo=oo, s=s, zpar=(o + s) % 2: v.tensor_tensor(
                        gt_sb[oo][:, 0:TT], sp_sb[s % 2][:, 0:TT], z_sb[zpar][:],
                        ALU.mult), after=(si, ev))
                    gst = p.op("sync", lambda e, oo=oo, s=s: [
                        e.dma_start(out=gate_d[128 * oo:128 * (oo + 1), TT * s:TT * (s + 1)],
                                    in_=gt_sb[oo][:, 0:TT])], units=1, dma=True,
                        stream="gout", after=(gt,))
                    xz_done.append(gst)
                    last_conv[oo] = (cvs[-1], si)
                evs.append(ev)
            gates_ready = tuple(xz_done[-PT:])
            for o in range(PT):
                pb = p_mm[o % 2]
                mm = mm_chain(p, pb[:, 0:TT],
                              [wtile(wdtw, i, o, D) for i in range(PT)],
                              [gt_sb[i][:, 0:TT] for i in range(PT)],
                              after=gates_ready + (lw,)
                              + ((evs[-2],) if len(evs) >= 2 else ()))
                par2 = o % 2
                d0 = p.op("scalar", lambda sc, o=o, par2=par2, pb=pb: sc.activation(
                    cv_sb[par2][:, 0:TT], pb[:, 0:TT], AF.Identity,
                    bias=dt_sb[:, o:o + 1]), after=(mm, lw2)
                    + ((evs[-2],) if len(evs) >= 2 else ()))
                s1 = p.op("scalar", lambda sc, par2=par2: sc.activation(
                    sp_sb[par2][:, 0:TT], cv_sb[par2][:, 0:TT], AF.Abs), after=(d0,))
                s2 = p.op("scalar", lambda sc, par2=par2: sc.activation(
                    sp_sb[par2][:, 0:TT], sp_sb[par2][:, 0:TT], AF.Exp, scale=-1.0),
                    after=(s1,))
                s3 = p.op("scalar", lambda sc, par2=par2: sc.activation(
                    sp_sb[par2][:, 0:TT], sp_sb[par2][:, 0:TT], AF.Ln,
                    bias=onesf[:, 0:1]), after=(s2, oz))
                s4 = p.op("vector", lambda v, par2=par2: v.tensor_scalar_max(
                    cv_sb[par2][:, 0:TT], cv_sb[par2][:, 0:TT], 0.0), after=(s3,))
                s5 = p.op("vector", lambda v, par2=par2: v.tensor_tensor(
                    cv_sb[par2][:, 0:TT], cv_sb[par2][:, 0:TT], sp_sb[par2][:, 0:TT],
                    ALU.add), after=(s4,))
                dst_op = p.op("sync", lambda e, o=o, s=s, par2=par2: [
                    e.dma_start(out=delta_d[128 * o:128 * (o + 1), TT * s:TT * (s + 1)],
                                in_=cv_sb[par2][:, 0:TT])], units=1, dma=True,
                    stream="dout", after=(s5,))
                evs.append(dst_op)
            pbx = p_mm[0]
            def bcmm_fn(te, s=s, pbx=pbx):
                ins = None
                for i in range(PT):
                    ins = te.matmul(pbx[0:32, 0:TT], wtile(wxw, i, 0, 32, width=32),
                                    gt_sb[i][:, 0:TT], start=(i == 0),
                                    stop=(i == PT - 1))
                return ins
            mmbc = p.op("tensor", bcmm_fn,
                        after=gates_ready + (lw,)
                        + ((evs[-2],) if len(evs) >= 2 else ()))
            bce = p.op("vector", lambda v, pbx=pbx: v.tensor_copy(
                sp_sb[0][0:32, 0:TT], pbx[0:32, 0:TT]),
                after=(mmbc,) + ((evs[-2],) if len(evs) >= 2 else ()))
            bst = p.op("sync", lambda e, s=s: [
                e.dma_start(out=bc_d[:, TT * s:TT * (s + 1)], in_=sp_sb[0][0:32, 0:TT])],
                units=1, dma=True, stream="bcout", after=(bce,))
            evs.append(bst)
            evs.append(bst)
            prev_xn_cons[par] = (xnst, xz_mm_last)
        mdone = mean_ops[-1]
        msc = p.op("vector", lambda v: v.tensor_scalar_mul(
            xmean[:, 0:PT], xmean[:, 0:PT], 1.0 / T), after=(mdone,))
        def rmm(te):
            ins = None
            for i in range(PT):
                ins = te.matmul(p_sm[0:1, 0:8], xmean[:, i:i + 1],
                                wrw[:, i * 8:(i + 1) * 8], start=(i == 0), stop=False)
            return te.matmul(p_sm[0:1, 0:8], onesf[0:1, 0:1], br_sb[0:1, :],
                             start=False, stop=True)
        rmo = p.op("tensor", rmm, after=(msc, lw2, oz)
                   + ((evs[-2],) if len(evs) >= 2 else ()))
        rme = p.op("vector", lambda v: v.tensor_copy(oh_sb[0:1, 0:8], p_sm[0:1, 0:8]),
                   after=(rmo,))
        rmx = p.op("vector", lambda v: v.reduce_max(
            oh_sb[0:1, 8:9], oh_sb[0:1, 0:8], AX.X), after=(rme,))
        roh = p.op("vector", lambda v: v.tensor_scalar(
            oh_sb[0:1, 16:24], oh_sb[0:1, 0:8], oh_sb[0:1, 8:9], None,
            ALU.is_ge), after=(rmx,))
        iot = p.op("vector", lambda v: v.tensor_tensor(
            oh_sb[0:1, 24:32], oh_sb[0:1, 16:24], iota8_sb[0:1, 0:8], ALU.mult),
            after=(roh, lio))
        isum = p.op("vector", lambda v: v.reduce_sum(
            oh_sb[0:1, 9:10], oh_sb[0:1, 24:32], AX.X), after=(iot,))
        icast = p.op("vector", lambda v: v.tensor_copy(
            idx_sb[0:1, 0:1], oh_sb[0:1, 9:10]), after=(isum,))
        ph1 = tuple(evs[-3:]) + (xnst, icast)
    return ph1, (xn_d, gate_d, delta_d, bc_d, y_d, aacc_d), (oh_sb, idx_sb, icast)


def emit_l2_scan(p, nc, ctx, T, scratch, AlogC, entry=()):
    xn_d, gate_d, delta_d, bc_d, y_d, aacc_d = scratch
    TT = 512
    NS = T // TT
    NN = 16
    import ml_dtypes
    ones_r_h = nc.inline_tensor(np.ones((1, 128), np.float32), name=_nm("onr"))
    with ExitStack() as c2:
        dl_sb = [SB(nc, c2, [128, T], F32, "dl2") for _ in range(2)]
        gt_sb = [SB(nc, c2, [128, T], BF16, "gt2") for _ in range(2)]
        u_sb = SB(nc, c2, [128, T], F32, "u2")
        da_sb = SB(nc, c2, [128, T], F32, "da2")
        dbx_sb = SB(nc, c2, [128, T], F32, "dbx2")
        h_sb = SB(nc, c2, [128, T], F32, "h2")
        y_sb = SB(nc, c2, [128, T], F32, "y2acc")
        bcrow = [SB(nc, c2, [1, 2 * T], BF16, "bcr") for _ in range(2)]
        ones_rb = SB(nc, c2, [1, 128], BF16, "orb")
        acol = SB(nc, c2, [128, 2 * NN * PT // 16 + 2 * NN] if False else [128, 16 * PT + 2], F32, "acol")
        ones_r = SB(nc, c2, [1, 128], F32, "orr")
        tmp_sb = SB(nc, c2, [128, TT], F32, "tmp2")
        p_bc = [PS(nc, c2, [128, TT], F32, "pbc") for _ in range(2)]

        onesb_h = nc.inline_tensor(np.ones((1, 128), ml_dtypes.bfloat16),
                                   name=_nm("onb"))
        lc = p.op("sync", lambda e: [
            e.dma_start(out=ones_r[:], in_=ones_r_h.ap()[:, :]),
            e.dma_start(out=ones_rb[:], in_=onesb_h.ap()[:, :]),
            e.dma_start(out=acol[:, 0:16 * PT], in_=AlogC[:, :])],
            units=3, dma=True, stream="cst", after=entry)
        a1 = p.op("scalar", lambda sc: sc.activation(
            acol[:, 0:16 * PT], acol[:, 0:16 * PT], AF.Exp), after=(lc,))
        a2 = p.op("vector", lambda v: v.tensor_scalar_mul(
            acol[:, 0:16 * PT], acol[:, 0:16 * PT], -1.0), after=(a1,))

        prev_y_st = None
        prev_o_ops = entry
        lds = {}
        last_dao = {}
        last_bc_cons = {}

        def issue_ld(o2):
            if o2 >= PT or o2 in lds:
                return
            ld = p.op("sync", lambda e, o2=o2: [
                e.dma_start(out=dl_sb[o2 % 2][:], in_=delta_d[128 * o2:128 * (o2 + 1), :]),
                e.dma_start(out=gt_sb[o2 % 2][:], in_=gate_d[128 * o2:128 * (o2 + 1), :])],
                units=2, dma=True, stream="sld",
                after=(tuple(entry) if o2 == 0 else (last_dao[o2 - 2],) if o2 >= 2 else ()))
            lds[o2] = ld

        def issue_bl(o, n):
            npar = n % 2
            war = last_bc_cons.get((o, n - 2)) or last_bc_cons.get((o - 1, NN + n - 2))
            return p.op("gpsimd", lambda e, n=n, npar=npar: [
                e.dma_start(out=bcrow[npar][0:1, 0:T], in_=bc_d[n:n + 1, :]),
                e.dma_start(out=bcrow[npar][0:1, T:2 * T], in_=bc_d[16 + n:17 + n, :])],
                units=2, dma=True, stream=f"bl{npar}",
                after=((war,) if war else tuple(entry)))

        bls = {}
        for o in range(PT):
            issue_ld(o)
            issue_ld(o + 1)
            ld = lds[o]
            uo = p.op("vector", lambda v, o=o: v.tensor_tensor(
                u_sb[:], dl_sb[o % 2][:], gt_sb[o % 2][:], ALU.mult),
                after=(ld,) + tuple(prev_o_ops))
            yz = p.op("vector", lambda v: v.memset(y_sb[:], 0.0),
                      after=((prev_y_st,) if prev_y_st else ()))
            nn_ops = [uo, yz]
            for n in range(NN):
                npar = n % 2
                if (o, n) not in bls:
                    bls[(o, n)] = issue_bl(o, n)
                bl = bls[(o, n)]
                dao = p.op("scalar", lambda sc, o=o, n=n: sc.activation(
                    da_sb[:], dl_sb[o % 2][:], AF.Exp,
                    scale=acol[:, 16 * o + n:16 * o + n + 1]),
                    after=(a2, ld) + tuple(nn_ops[-2:]))
                last_dao[o] = dao
                dbs = []
                for scnk in range(NS):
                    pb = p_bc[scnk % 2]
                    bcm = p.op("tensor", lambda te, scnk=scnk, npar=npar, pb=pb: te.matmul(
                        pb[:, 0:TT], ones_rb[0:1, :],
                        bcrow[npar][0:1, TT * scnk:TT * (scnk + 1)],
                        start=True, stop=True), after=(bl, lc) + tuple(dbs[-2:])
                        + tuple(nn_ops[-2:]))
                    db = p.op("vector", lambda v, scnk=scnk, pb=pb: v.tensor_tensor(
                        dbx_sb[:, TT * scnk:TT * (scnk + 1)], u_sb[:, TT * scnk:TT * (scnk + 1)],
                        pb[:, 0:TT], ALU.mult), after=(bcm, uo))
                    dbs.append(db)
                sc_op = p.op("vector", lambda v: v.tensor_tensor_scan(
                    h_sb[:], da_sb[:], dbx_sb[:], 0.0, ALU.mult, ALU.add),
                    after=tuple(dbs) + (dao,))
                yops = []
                for scnk in range(NS):
                    pb = p_bc[scnk % 2]
                    ccm = p.op("tensor", lambda te, scnk=scnk, npar=npar, pb=pb: te.matmul(
                        pb[:, 0:TT], ones_rb[0:1, :],
                        bcrow[npar][0:1, T + TT * scnk:T + TT * (scnk + 1)],
                        start=True, stop=True), after=(bl, sc_op) + tuple(yops[-2:]))
                    y1 = p.op("vector", lambda v, scnk=scnk, pb=pb: v.tensor_tensor(
                        tmp_sb[:, 0:TT], h_sb[:, TT * scnk:TT * (scnk + 1)], pb[:, 0:TT],
                        ALU.mult), after=(ccm, sc_op) + tuple(yops[-1:]))
                    y2 = p.op("vector", lambda v, scnk=scnk: v.tensor_tensor(
                        y_sb[:, TT * scnk:TT * (scnk + 1)], y_sb[:, TT * scnk:TT * (scnk + 1)],
                        tmp_sb[:, 0:TT], ALU.add), after=(y1, yz))
                    yops.append(y2)
                last_bc_cons[(o, n)] = yops[-1]
                nn_ops += [sc_op, yops[-1]]
                if n + 2 < NN:
                    bls[(o, n + 2)] = issue_bl(o, n + 2)
                elif o + 1 < PT:
                    bls[(o + 1, n + 2 - NN)] = issue_bl(o + 1, n + 2 - NN)
            yst = p.op("sync", lambda e, o=o: [
                e.dma_start(out=y_d[128 * o:128 * (o + 1), :], in_=y_sb[:])],
                units=1, dma=True, stream="yout", after=tuple(nn_ops[-2:]))
            prev_y_st = yst
            prev_o_ops = nn_ops[-2:] + [yst]
        return (yst,)


def emit_l2_out(p, nc, ctx, T, x2, scratch, lngcol, lnbcol, WoutT, boutc, entry=()):
    xn_d, gate_d, delta_d, bc_d, y_d, aacc_d = scratch
    TT = 512
    NS = T // TT
    import ml_dtypes
    ones_c2_h = nc.inline_tensor(np.ones((128, 1), ml_dtypes.bfloat16), name=_nm("onc"))
    ones_r2_h = nc.inline_tensor(np.ones((1, 128), np.float32), name=_nm("onr"))
    with ExitStack() as c3:
        wow = SB(nc, c3, [128, PT * D], BF16, "wout")
        lng = SB(nc, c3, [128, PT], F32, "lng")
        lnb = SB(nc, c3, [128, PT], F32, "lnb")
        bo = SB(nc, c3, [128, PT], F32, "bo")
        y2 = [SB(nc, c3, [128, TT], F32, "y3") for _ in range(PT)]
        y2n = [SB(nc, c3, [128, TT], BF16, "y3n") for _ in range(PT)]
        x2_t = [SB(nc, c3, [128, TT], F32, "x2t") for _ in range(PT)]
        ao_f = [SB(nc, c3, [128, TT], F32, "aof") for _ in range(2)]
        ones_c2 = SB(nc, c3, [128, 1], BF16, "oc2")
        ones_r2 = SB(nc, c3, [1, 128], F32, "or2")
        lnbufs2 = dict(
            stat=SB(nc, c3, [1, 3 * TT], F32, "st2"),
            xbf=[SB(nc, c3, [128, TT], BF16, "xc2") for _ in range(PT)],
            sq=SB(nc, c3, [128, TT], BF16, "sq2"),
            nsc=[SB(nc, c3, [128, TT], F32, "ns2")],
        )
        lnbufs2["p_sum1"] = PS(nc, c3, [1, TT], F32, "ps1b")
        lnbufs2["p_sum2"] = PS(nc, c3, [1, TT], F32, "ps2b")
        lnbufs2["p_bc1"] = PS(nc, c3, [128, TT], F32, "pb1b")
        lnbufs2["p_bc2"] = PS(nc, c3, [128, TT], F32, "pb2b")
        p_pr = [PS(nc, c3, [128, TT], F32, "ppr") for _ in range(2)]
        lw2 = load_weight_bf16(p, wow, WoutT, D, D, after=entry)
        lcs = p.op("sync", lambda e: [
            e.dma_start(out=lng[:], in_=lngcol[:, :]),
            e.dma_start(out=lnb[:], in_=lnbcol[:, :]),
            e.dma_start(out=bo[:], in_=boutc[:, :]),
            e.dma_start(out=ones_c2[:], in_=ones_c2_h.ap()[:, :]),
            e.dma_start(out=ones_r2[:], in_=ones_r2_h.ap()[:, :])],
            units=5, dma=True, stream="cst", after=entry)
        prev_ln2 = list(entry)
        last_st = None
        for s in range(NS):
            xld = p.op("sync", lambda e, s=s: [
                e.dma_start(out=y2[i][:], in_=y_d[128 * i:128 * (i + 1), TT * s:TT * (s + 1)])
                for i in range(PT)] + [
                e.dma_start(out=x2_t[i][:], in_=x2[128 * i:128 * (i + 1), TT * s:TT * (s + 1)])
                for i in range(PT)], units=2 * PT, dma=True, stream="pl",
                after=tuple(prev_ln2) + entry)
            xnl = p.op("gpsimd", lambda e, s=s: [
                e.dma_start(out=lnbufs2["xbf"][i][:],
                            in_=xn_d[128 * i:128 * (i + 1), TT * s:TT * (s + 1)])
                for i in range(PT)], units=PT, dma=True, stream="pl2",
                after=tuple(prev_ln2) + entry)
            ads = []
            for i in range(PT):
                ad = p.op("vector", lambda v, i=i: v.tensor_tensor(
                    y2[i][:], y2[i][:], lnbufs2["xbf"][i][:], ALU.add),
                    after=(xld, xnl))
                ads.append(ad)
            louts = emit_ln(p, nc, ctx, y2, y2n, TT, ones_c2, ones_r2, lnbufs2,
                            x_ready=tuple(ads) + (lcs,))
            sts = []
            gbs = []
            for o in range(PT):
                gb = p.op("vector", lambda v, o=o: v.tensor_scalar(
                    y2n[o][:, 0:TT], y2n[o][:, 0:TT], lng[:, o:o + 1],
                    lnb[:, o:o + 1], ALU.mult, ALU.add), after=tuple(louts))
                gbs.append(gb)
            for o in range(PT):
                ppr = p_pr[o % 2]
                mm = mm_chain(p, ppr[:, 0:TT],
                              [wtile(wow, i, o, D) for i in range(PT)],
                              [y2n[i][:, 0:TT] for i in range(PT)],
                              after=tuple(gbs) + (lw2,)
                              + ((sts[-2],) if len(sts) >= 2 else ()))
                bi = p.op("scalar", lambda sc, o=o, ppr=ppr: sc.activation(
                    ao_f[o % 2][:, 0:TT], ppr[:, 0:TT], AF.Identity, bias=bo[:, o:o + 1]),
                    after=(mm,) + ((sts[-2],) if len(sts) >= 2 else ()))
                ax = p.op("vector", lambda v, o=o: v.tensor_tensor(
                    ao_f[o % 2][:, 0:TT], ao_f[o % 2][:, 0:TT], x2_t[o][:], ALU.add),
                    after=(bi, xld))
                st = p.op("sync", lambda e, o=o, s=s: [
                    e.dma_start(out=aacc_d[128 * o:128 * (o + 1), TT * s:TT * (s + 1)],
                                in_=ao_f[o % 2][:, 0:TT])], units=1, dma=True,
                    stream="aout", after=(ax,))
                sts.append(st)
            prev_ln2 = sts
            last_st = st
        return (last_st,)


def emit_l2_moe(p, nc, ctx, T, scratch, l2r, W1T, b1col, W2T, b2col, x3_out,
                entry=()):
    from concourse.bass_core_rust import IndirectOffsetOnAxis
    xn_d, gate_d, delta_d, bc_d, y_d, aacc_d = scratch
    oh_sb, idx_sb, icast = l2r
    TT = 512
    NS = T // TT
    DFF = 4096
    MH = DFF // 128
    import ml_dtypes
    iota_h = nc.inline_tensor(np.arange(128, dtype=np.float32)[:, None], name=_nm("io128"))
    ones_r_h = nc.inline_tensor(np.ones((1, 128), np.float32), name=_nm("onr"))
    with ExitStack() as c4:
        w1_sb = SB(nc, c4, [128, PT * DFF], BF16, "w1s")
        w2_sb = SB(nc, c4, [128, MH * D], BF16, "w2s")
        b1_sb = SB(nc, c4, [128, MH], F32, "b1s")
        b2_sb = SB(nc, c4, [128, PT], F32, "b2s")
        iota_sb = SB(nc, c4, [128, 1], F32, "iot")
        idxf = SB(nc, c4, [128, 4], F32, "idxf")
        idxi = SB(nc, c4, [128, 2], I32, "idxi")
        ones_r = SB(nc, c4, [1, 128], F32, "orr")
        xn_t = [[SB(nc, c4, [128, TT], BF16, "xnt") for _ in range(PT)] for _ in range(2)]
        h_sb = [SB(nc, c4, [128, TT], BF16, "hsb") for _ in range(MH)]
        y_sb = [SB(nc, c4, [128, TT], F32, "ysb") for _ in range(2)]
        a_t = [SB(nc, c4, [128, TT], F32, "at3") for _ in range(2)]
        p_h = [PS(nc, c4, [128, TT], F32, "phh") for _ in range(2)]
        p_y = [PS(nc, c4, [128, TT], F32, "pyy") for _ in range(2)]
        p_b = PS(nc, c4, [128, 1], F32, "pbb")

        lc = p.op("sync", lambda e: [
            e.dma_start(out=iota_sb[:], in_=iota_h.ap()[:, :]),
            e.dma_start(out=ones_r[:], in_=ones_r_h.ap()[:, :])],
            units=2, dma=True, stream="cst", after=entry)
        bcm = p.op("tensor", lambda te: te.matmul(
            p_b[:, 0:1], ones_r[0:1, :], oh_sb[0:1, 9:10], start=True, stop=True),
            after=(lc, icast) + entry)
        bce = p.op("vector", lambda v: v.tensor_copy(idxf[:, 0:1], p_b[:, 0:1]),
                   after=(bcm,))
        i1 = p.op("vector", lambda v: v.tensor_scalar(
            idxf[:, 1:2], idxf[:, 0:1], 1024.0, None, ALU.mult), after=(bce,))
        i2 = p.op("vector", lambda v: v.tensor_tensor(
            idxf[:, 1:2], idxf[:, 1:2], iota_sb[:, 0:1], ALU.add), after=(i1,))
        i3 = p.op("vector", lambda v: v.tensor_scalar(
            idxf[:, 2:3], idxf[:, 0:1], 128.0, None, ALU.mult), after=(bce,))
        i4 = p.op("vector", lambda v: v.tensor_tensor(
            idxf[:, 2:3], idxf[:, 2:3], iota_sb[:, 0:1], ALU.add), after=(i3,))
        w1flat = bass.AP(W1T.tensor, 0, [[4096, 8 * 1024], [1, 4096]])
        w2flat = bass.AP(W2T.tensor, 0, [[D, 8 * 4096], [1, D]])
        b1flat = bass.AP(b1col.tensor, 0, [[32, 8 * 128], [1, 32]])
        b2flat = bass.AP(b2col.tensor, 0, [[PT, 8 * 128], [1, PT]])
        gops = []
        icasts = []
        for b in range(PT):
            ic = p.op("vector", lambda v, b=b: v.tensor_scalar(
                idxf[:, 3:4],
                idxf[:, 1:2], float(128 * b), None, ALU.add), after=(i2,)
                + tuple(icasts[-1:]) + tuple(gops[-1:]))
            ic2 = p.op("vector", lambda v: v.tensor_copy(idxi[:, 0:1], idxf[:, 3:4]),
                       after=(ic,))
            icasts.append(ic2)
            g1 = p.op("gpsimd", lambda e, b=b: [
                e.indirect_dma_start(
                    out=w1_sb[:].rearrange("p (i o) -> p i o", o=DFF)[:, b, :],
                    out_offset=None,
                    in_=w1flat,
                    in_offset=IndirectOffsetOnAxis(idxi[:, 0:1], 0))],
                units=1, dma=True, stream="gw", after=(ic2,) + entry)
            gops.append(g1)
        for b in range(MH):
            ic = p.op("vector", lambda v, b=b: v.tensor_scalar(
                idxf[:, 3:4], idxf[:, 0:1], 4096.0, None, ALU.mult),
                after=(bce,) + tuple(icasts[-1:]) + tuple(gops[-1:]))
            ic1b = p.op("vector", lambda v, b=b: v.tensor_scalar(
                idxf[:, 3:4], idxf[:, 3:4], float(128 * b), None, ALU.add),
                after=(ic,))
            ic1c = p.op("vector", lambda v: v.tensor_tensor(
                idxf[:, 3:4], idxf[:, 3:4], iota_sb[:, 0:1], ALU.add), after=(ic1b,))
            ic2 = p.op("vector", lambda v: v.tensor_copy(idxi[:, 0:1], idxf[:, 3:4]),
                       after=(ic1c,))
            icasts.append(ic2)
            g2 = p.op("gpsimd", lambda e, b=b: [
                e.indirect_dma_start(
                    out=w2_sb[:].rearrange("p (i o) -> p i o", o=D)[:, b, :],
                    out_offset=None,
                    in_=w2flat,
                    in_offset=IndirectOffsetOnAxis(idxi[:, 0:1], 0))],
                units=1, dma=True, stream="gw", after=(ic2,) + entry)
            gops.append(g2)
        icb1 = p.op("vector", lambda v: v.tensor_copy(idxi[:, 1:2], idxf[:, 2:3]),
                    after=(i4,) + tuple(icasts[-1:]) + tuple(gops[-1:]))
        gb1 = p.op("gpsimd", lambda e: [
            e.indirect_dma_start(out=b1_sb[:, :], out_offset=None,
                                 in_=b1flat,
                                 in_offset=IndirectOffsetOnAxis(idxi[:, 1:2], 0)),
            e.indirect_dma_start(out=b2_sb[:, :], out_offset=None,
                                 in_=b2flat,
                                 in_offset=IndirectOffsetOnAxis(idxi[:, 1:2], 0))],
            units=2, dma=True, stream="gw", after=(icb1,) + entry)
        last_xn_cons = [None, None]
        last_h_cons = {}
        last_y = [None, None]
        fins = []
        for s in range(NS):
            par = s % 2
            xl = p.op("sync", lambda eng, s=s, par=par: [
                eng.dma_start(out=xn_t[par][i][:],
                              in_=xn_d[128 * i:128 * (i + 1), TT * s:TT * (s + 1)])
                for i in range(PT)], units=PT, dma=True, stream=f"mx{par}",
                after=((last_xn_cons[par],) if last_xn_cons[par] else ()) + entry)
            h_evs = []
            for m in range(MH):
                pb = p_h[m % 2]
                mm = mm_chain(p, pb[:, 0:TT],
                              [wtile(w1_sb, i, m, DFF) for i in range(PT)],
                              [xn_t[par][i][:] for i in range(PT)],
                              after=(xl,) + tuple(gops)
                              + ((h_evs[-2],) if len(h_evs) >= 2 else ())
                              + ((last_h_cons.get(m),) if m in last_h_cons else ()))
                ev = p.op("scalar", lambda sc, m=m, pb=pb: sc.activation(
                    h_sb[m][:, 0:TT], pb[:, 0:TT], AF.Relu, bias=b1_sb[:, m:m + 1]),
                    after=(mm, gb1))
                h_evs.append(ev)
            last_xn_cons[par] = h_evs[-1]
            for o in range(PT):
                opar = o % 2
                py = p_y[opar]
                mm = mm_chain(p, py[:, 0:TT],
                              [wtile(w2_sb, mi, o, D) for mi in range(MH)],
                              [h_sb[mi][:, 0:TT] for mi in range(MH)],
                              after=tuple(h_evs) + tuple(gops)
                              + ((last_y[opar],) if last_y[opar] else ()))
                for mi in range(MH):
                    last_h_cons[mi] = mm
                bi = p.op("scalar", lambda sc, o=o, opar=opar, py=py: sc.activation(
                    y_sb[opar][:, 0:TT], py[:, 0:TT], AF.Identity,
                    bias=b2_sb[:, o:o + 1]), after=(mm, gb1)
                    + ((last_y[opar],) if last_y[opar] else ()))
                al = p.op("gpsimd", lambda eng, s=s, o=o, opar=opar: [
                    eng.dma_start(out=a_t[opar][:],
                                  in_=aacc_d[128 * o:128 * (o + 1), TT * s:TT * (s + 1)])],
                    units=1, dma=True, stream=f"al{opar}",
                    after=((last_y[opar],) if last_y[opar] else ()) + entry)
                ax = p.op("vector", lambda v, opar=opar: v.tensor_tensor(
                    y_sb[opar][:, 0:TT], y_sb[opar][:, 0:TT], a_t[opar][:], ALU.add),
                    after=(bi, al))
                st = p.op("sync", lambda eng, s=s, o=o, opar=opar: [
                    eng.dma_start(out=x3_out[128 * o:128 * (o + 1), TT * s:TT * (s + 1)],
                                  in_=y_sb[opar][:, 0:TT])], units=1, dma=True,
                    stream="x3o", after=(ax,))
                last_y[opar] = st
                fins.append(st)
        return (fins[-1],)


def build_L2(T=4096):
    nc = bass.Bass("TRN2", target_bir_lowering=False)
    x2 = nc.dram_tensor("x2", [D, T], F32, kind="ExternalInput").ap()
    WinT = nc.dram_tensor("WinT", [D, 2 * D], F32, kind="ExternalInput").ap()
    wconvc = nc.dram_tensor("wconvc", [128, 4 * PT], F32, kind="ExternalInput").ap()
    bconvc = nc.dram_tensor("bconvc", [128, PT], F32, kind="ExternalInput").ap()
    WxT = nc.dram_tensor("WxT", [D, 32], F32, kind="ExternalInput").ap()
    WdtT = nc.dram_tensor("WdtT", [D, D], F32, kind="ExternalInput").ap()
    bdtc = nc.dram_tensor("bdtc", [128, PT], F32, kind="ExternalInput").ap()
    AlogC = nc.dram_tensor("AlogC", [128, 16 * PT], F32, kind="ExternalInput").ap()
    lngcol = nc.dram_tensor("lngcol", [128, PT], F32, kind="ExternalInput").ap()
    lnbcol = nc.dram_tensor("lnbcol", [128, PT], F32, kind="ExternalInput").ap()
    WoutT = nc.dram_tensor("WoutT", [D, D], F32, kind="ExternalInput").ap()
    boutc = nc.dram_tensor("boutc", [128, PT], F32, kind="ExternalInput").ap()
    WrT = nc.dram_tensor("WrT", [D, 8], F32, kind="ExternalInput").ap()
    br = nc.dram_tensor("br", [1, 8], F32, kind="ExternalInput").ap()
    W1T = nc.dram_tensor("W1T", [8, D, 4096], F32, kind="ExternalInput").ap()
    b1col = nc.dram_tensor("b1col", [8, 128, 32], F32, kind="ExternalInput").ap()
    W2T = nc.dram_tensor("W2T", [8, 4096, D], F32, kind="ExternalInput").ap()
    b2col = nc.dram_tensor("b2col", [8, 128, PT], F32, kind="ExternalInput").ap()
    x3 = nc.dram_tensor("x3", [D, T], F32, kind="ExternalOutput").ap()
    with ExitStack() as ctx:
        p = Prog(nc, ctx)
        ph1, scratch, l2r = emit_l2(p, nc, ctx, T, x2, WinT, wconvc, bconvc, WxT,
                                    WdtT, bdtc, AlogC, lngcol, lnbcol, WoutT, boutc,
                                    WrT, br, W1T, b1col, W2T, b2col, x3)
        ph2 = emit_l2_scan(p, nc, ctx, T, scratch, AlogC, entry=tuple(ph1))
        ph3 = emit_l2_out(p, nc, ctx, T, x2, scratch, lngcol, lnbcol, WoutT, boutc,
                          entry=tuple(ph2))
        fins = emit_l2_moe(p, nc, ctx, T, scratch, l2r, W1T, b1col, W2T, b2col,
                           x3, entry=tuple(ph3))
        p.op("gpsimd", lambda g: g.nop(), after=tuple(fins))
        p.emit()
    return nc


def _cols(v, nb=8):
    return np.ascontiguousarray(np.asarray(v, np.float32).reshape(nb, 128).T)


def _t(a):
    return np.ascontiguousarray(np.asarray(a, np.float32).T)


def kernel(**I):
    T = 4096
    x = np.asarray(I["L0_attn_x"], np.float32)
    n = 8

    ncA = build_L0A(T=T, R=512, NSTEP=8)
    common_A = {
        "qkvT": _t(I["L0_attn_qkv_weight"]),
        "projT": _t(I["L0_attn_proj_weight"]),
        "pbias": _cols(I["L0_attn_proj_bias"]),
        "wfT": _t(I["L0_ffn_wf"]), "wiT": _t(I["L0_ffn_wi"]),
        "wcT": _t(I["L0_ffn_wc"]), "ufT": _t(I["L0_ffn_uf"]),
        "uiT": _t(I["L0_ffn_ui"]), "ucT": _t(I["L0_ffn_uc"]),
        "uoT": _t(I["L0_ffn_uo"]),
        "bfc": _cols(I["L0_ffn_bf"]), "bic": _cols(I["L0_ffn_bi"]),
        "bcc": _cols(I["L0_ffn_bc"]), "incc": _cols(I["L0_ffn_in_c"]),
    }
    in_maps = []
    for i in range(n):
        m = dict(common_A)
        m["xb"] = np.ascontiguousarray(x[i].T)
        m["xr"] = np.ascontiguousarray(x[:, 512 * i:512 * (i + 1), :].transpose(0, 2, 1))
        in_maps.append(m)
    resA = run_bass_kernel_spmd(ncA, in_maps, core_ids=list(range(n)))
    attn_acc = [r["attn_acc"] for r in resA.results]
    xnb = [r["xnb"] for r in resA.results]
    cuo_full = np.concatenate([r["cuo"] for r in resA.results], axis=1)

    ncB = build_L0B(T=T)
    common_B = {"woT": _t(I["L0_ffn_wo"]), "bocol": _cols(I["L0_ffn_bo"]),
                "cuo": np.ascontiguousarray(cuo_full)}
    in_maps = [dict(common_B, attn_acc=attn_acc[i], xnb=xnb[i]) for i in range(n)]
    resB = run_bass_kernel_spmd(ncB, in_maps, core_ids=list(range(n)))
    x1 = [r["x1"] for r in resB.results]

    nc1 = build_L1(T=T)
    common_1 = {
        "WqT": _t(I["L1_attn_Wq"]), "WkT": _t(I["L1_attn_Wk"]),
        "WvT": _t(I["L1_attn_Wv"]), "WgT": _t(I["L1_attn_Wg"]),
        "bg": np.asarray(I["L1_attn_bg"], np.float32)[None, :],
        "WdlT": _t(I["L1_attn_Wdl"]),
        "Lpar": np.ascontiguousarray(np.asarray(I["L1_attn_Lparam"], np.float32).reshape(1, 96)),
        "WoT": _t(I["L1_attn_Wo"]),
        "lngcol": _cols(I["L1_attn_ln_g"]), "lnbcol": _cols(I["L1_attn_ln_b"]),
        "rwT": _t(I["L1_ffn_router_w"]),
        "rb": np.asarray(I["L1_ffn_router_b"], np.float32)[None, :],
        "w1T": np.ascontiguousarray(np.asarray(I["L1_ffn_weights1"], np.float32)),
        "b1col": np.ascontiguousarray(np.asarray(I["L1_ffn_bias1"], np.float32)
                                      .reshape(32, 2, 128).transpose(0, 2, 1)),
        "w2T": np.ascontiguousarray(np.asarray(I["L1_ffn_weights2"], np.float32)),
        "b2col": np.ascontiguousarray(np.asarray(I["L1_ffn_bias2"], np.float32)
                                      .reshape(32, 8, 128).transpose(0, 2, 1)),
    }
    in_maps = [dict(common_1, x1=x1[i]) for i in range(n)]
    res1 = run_bass_kernel_spmd(nc1, in_maps, core_ids=list(range(n)))
    x2 = [r["x2"] for r in res1.results]

    nc2 = build_L2(T=T)
    wconv = np.asarray(I["L2_attn_W_conv"], np.float32)[:, 0, :]
    common_2 = {
        "WinT": _t(I["L2_attn_W_in"]),
        "wconvc": np.ascontiguousarray(wconv.reshape(8, 128, 4).transpose(1, 0, 2)
                                       .reshape(128, 32)),
        "bconvc": _cols(I["L2_attn_b_conv"]),
        "WxT": _t(I["L2_attn_W_xproj"]),
        "WdtT": _t(I["L2_attn_W_dt"]),
        "bdtc": _cols(I["L2_attn_b_dt"]),
        "AlogC": np.ascontiguousarray(np.asarray(I["L2_attn_A_log"], np.float32)
                                      .reshape(8, 128, 16).transpose(1, 0, 2)
                                      .reshape(128, 128)),
        "lngcol": _cols(I["L2_attn_ln_gamma"]), "lnbcol": _cols(I["L2_attn_ln_beta"]),
        "WoutT": _t(I["L2_attn_W_out"]),
        "boutc": _cols(I["L2_attn_b_out"]),
        "WrT": np.ascontiguousarray(np.asarray(I["L2_ffn_Wr"], np.float32)),
        "br": np.asarray(I["L2_ffn_br"], np.float32)[None, :],
        "W1T": np.ascontiguousarray(np.asarray(I["L2_ffn_W1"], np.float32)),
        "b1col": np.ascontiguousarray(np.asarray(I["L2_ffn_b1"], np.float32)
                                      .reshape(8, 32, 128).transpose(0, 2, 1)),
        "W2T": np.ascontiguousarray(np.asarray(I["L2_ffn_W2"], np.float32)),
        "b2col": np.ascontiguousarray(np.asarray(I["L2_ffn_b2"], np.float32)
                                      .reshape(8, 8, 128).transpose(0, 2, 1)),
    }
    in_maps = [dict(common_2, x2=x2[i]) for i in range(n)]
    res2 = run_bass_kernel_spmd(nc2, in_maps, core_ids=list(range(n)))
    x3 = [r["x3"] for r in res2.results]

    nc3 = build_L3(T=T)
    common_3 = {
        "wqkvT": _t(I["L3_attn_w_qkv"]),
        "wprojT": _t(I["L3_attn_w_proj"]),
        "pbcol": _cols(I["L3_attn_b_proj"]),
        "relbias": np.ascontiguousarray(np.asarray(I["L3_attn_rel_bias_table"], np.float32)),
        "wgate": np.ascontiguousarray(np.asarray(I["L3_ffn_w_gate"], np.float32)),
        "bgate": np.asarray(I["L3_ffn_b_gate"], np.float32)[None, :],
        "W1T": np.ascontiguousarray(np.asarray(I["L3_ffn_W1"], np.float32)),
        "b1col": np.ascontiguousarray(np.asarray(I["L3_ffn_b1"], np.float32)
                                      .reshape(8, 32, 128).transpose(0, 2, 1)),
        "W2T": np.ascontiguousarray(np.asarray(I["L3_ffn_W2"], np.float32)),
        "b2col": np.ascontiguousarray(np.asarray(I["L3_ffn_b2"], np.float32)
                                      .reshape(8, 8, 128).transpose(0, 2, 1)),
    }
    in_maps = [dict(common_3, x3=x3[i]) for i in range(n)]
    res3 = run_bass_kernel_spmd(nc3, in_maps, core_ids=list(range(n)))
    out = np.stack([res3.results[i]["x4"].T for i in range(n)], axis=0)
    return out


def build_full(T=4096):
    nc = bass.Bass("TRN2", target_bir_lowering=False)
    xb = nc.dram_tensor("xb", [D, T], F32, kind="ExternalInput").ap()
    xfull = nc.dram_tensor("xfull", [8, D, T], F32, kind="ExternalInput").ap()
    qkvT = nc.dram_tensor("qkvT", [D, 3 * D], F32, kind="ExternalInput").ap()
    projT = nc.dram_tensor("projT", [D, D], F32, kind="ExternalInput").ap()
    pbias = nc.dram_tensor("pbias", [128, PT], F32, kind="ExternalInput").ap()
    wT = {k: nc.dram_tensor(k, [D, D], F32, kind="ExternalInput").ap()
          for k in ["wfT", "wiT", "wcT", "ufT", "uiT", "ucT", "uoT", "woT"]}
    bcol = {k: nc.dram_tensor(k, [128, PT], F32, kind="ExternalInput").ap()
            for k in ["bfc", "bic", "bcc", "incc", "bocol"]}
    WqT = nc.dram_tensor("WqT", [D, D], F32, kind="ExternalInput").ap()
    WkT = nc.dram_tensor("WkT", [D, D], F32, kind="ExternalInput").ap()
    WvT = nc.dram_tensor("WvT", [D, D], F32, kind="ExternalInput").ap()
    WgT = nc.dram_tensor("WgT", [D, 8], F32, kind="ExternalInput").ap()
    bg = nc.dram_tensor("bg", [1, 8], F32, kind="ExternalInput").ap()
    WdlT = nc.dram_tensor("WdlT", [D, 96], F32, kind="ExternalInput").ap()
    Lpar = nc.dram_tensor("Lpar", [1, 96], F32, kind="ExternalInput").ap()
    WoT1 = nc.dram_tensor("WoT1", [D, D], F32, kind="ExternalInput").ap()
    lngcol1 = nc.dram_tensor("lngcol1", [128, PT], F32, kind="ExternalInput").ap()
    lnbcol1 = nc.dram_tensor("lnbcol1", [128, PT], F32, kind="ExternalInput").ap()
    rwT1 = nc.dram_tensor("rwT1", [D, 32], F32, kind="ExternalInput").ap()
    rb1 = nc.dram_tensor("rb1", [1, 32], F32, kind="ExternalInput").ap()
    w1T1 = nc.dram_tensor("w1T1", [32, D, 256], F32, kind="ExternalInput").ap()
    b1col1 = nc.dram_tensor("b1col1", [32, 128, 2], F32, kind="ExternalInput").ap()
    w2T1 = nc.dram_tensor("w2T1", [32, 256, D], F32, kind="ExternalInput").ap()
    b2col1 = nc.dram_tensor("b2col1", [32, 128, PT], F32, kind="ExternalInput").ap()
    WinT = nc.dram_tensor("WinT", [D, 2 * D], F32, kind="ExternalInput").ap()
    wconvc = nc.dram_tensor("wconvc", [128, 4 * PT], F32, kind="ExternalInput").ap()
    bconvc = nc.dram_tensor("bconvc", [128, PT], F32, kind="ExternalInput").ap()
    WxT = nc.dram_tensor("WxT", [D, 32], F32, kind="ExternalInput").ap()
    WdtT = nc.dram_tensor("WdtT", [D, D], F32, kind="ExternalInput").ap()
    bdtc = nc.dram_tensor("bdtc", [128, PT], F32, kind="ExternalInput").ap()
    AlogC = nc.dram_tensor("AlogC", [128, 16 * PT], F32, kind="ExternalInput").ap()
    lngcol2 = nc.dram_tensor("lngcol2", [128, PT], F32, kind="ExternalInput").ap()
    lnbcol2 = nc.dram_tensor("lnbcol2", [128, PT], F32, kind="ExternalInput").ap()
    WoutT = nc.dram_tensor("WoutT", [D, D], F32, kind="ExternalInput").ap()
    boutc = nc.dram_tensor("boutc", [128, PT], F32, kind="ExternalInput").ap()
    WrT2 = nc.dram_tensor("WrT2", [D, 8], F32, kind="ExternalInput").ap()
    br2 = nc.dram_tensor("br2", [1, 8], F32, kind="ExternalInput").ap()
    W1T2 = nc.dram_tensor("W1T2", [8, D, 4096], F32, kind="ExternalInput").ap()
    b1col2 = nc.dram_tensor("b1col2", [8, 128, 32], F32, kind="ExternalInput").ap()
    W2T2 = nc.dram_tensor("W2T2", [8, 4096, D], F32, kind="ExternalInput").ap()
    b2col2 = nc.dram_tensor("b2col2", [8, 128, PT], F32, kind="ExternalInput").ap()
    wqkvT3 = nc.dram_tensor("wqkvT3", [D, 3 * D], F32, kind="ExternalInput").ap()
    wprojT3 = nc.dram_tensor("wprojT3", [D, D], F32, kind="ExternalInput").ap()
    pbcol3 = nc.dram_tensor("pbcol3", [128, PT], F32, kind="ExternalInput").ap()
    relbias3 = nc.dram_tensor("relbias3", [32, 16], F32, kind="ExternalInput").ap()
    wgate3 = nc.dram_tensor("wgate3", [D, 8], F32, kind="ExternalInput").ap()
    bgate3 = nc.dram_tensor("bgate3", [1, 8], F32, kind="ExternalInput").ap()
    W1T3 = nc.dram_tensor("W1T3", [8, D, 4096], F32, kind="ExternalInput").ap()
    b1col3 = nc.dram_tensor("b1col3", [8, 128, 32], F32, kind="ExternalInput").ap()
    W2T3 = nc.dram_tensor("W2T3", [8, 4096, D], F32, kind="ExternalInput").ap()
    b2col3 = nc.dram_tensor("b2col3", [8, 128, PT], F32, kind="ExternalInput").ap()
    x4 = nc.dram_tensor("x4", [D, T], F32, kind="ExternalOutput").ap()

    attn_acc_d = nc.dram_tensor(_nm("aacc0"), [D, T], F32, kind="Internal").ap()
    xnb_d = nc.dram_tensor(_nm("xnb0"), [D, T], BF16, kind="Internal").ap()
    cuo_d = nc.dram_tensor(_nm("cuo0"), [D, T], F32, kind="Internal").ap()
    x1_d = nc.dram_tensor(_nm("x1d"), [D, T], F32, kind="Internal").ap()
    x2_d = nc.dram_tensor(_nm("x2d"), [D, T], F32, kind="Internal").ap()
    x3_d = nc.dram_tensor(_nm("x3d"), [D, T], F32, kind="Internal").ap()

    with ExitStack() as ctx:
        p = Prog(nc, ctx)
        ent = ()
        for rc in range(8):
            outs = emit_l0_ffn(p, nc, ctx, 512, 8,
                               xfull[:, :, 512 * rc:512 * (rc + 1)],
                               wT, bcol, cuo_d[:, 512 * rc:512 * (rc + 1)],
                               entry=ent)
            ent = tuple(outs)
        o2 = emit_l0_attn(p, nc, ctx, T, xb, qkvT, projT, pbias, attn_acc_d,
                          xnb_d, entry=ent)
        o3 = emit_l0b(p, nc, ctx, T, attn_acc_d, xnb_d, cuo_d, wT["woT"],
                      bcol["bocol"], x1_d, entry=tuple(o2))
        with ExitStack() as lctx1:
            ph1, scr1, ohp1 = emit_l1(p, nc, lctx1, T, x1_d, WqT, WkT, WvT, WgT, bg,
                                      WdlT, Lpar, WoT1, lngcol1, lnbcol1, rwT1, rb1,
                                      w1T1, b1col1, w2T1, b2col1, x2_d,
                                      entry=tuple(o3))
            ph2 = emit_l1_attn_heads(p, nc, lctx1, T, scr1, WoT1, lngcol1, lnbcol1,
                                     x1_d, scr1[4], entry=tuple(ph1))
            o4 = emit_l1_moe(p, nc, lctx1, T, scr1[0], scr1[4], ohp1, w1T1, b1col1,
                             w2T1, b2col1, x2_d, entry=tuple(ph2))
        with ExitStack() as lctx2:
            p1, scr2, l2r = emit_l2(p, nc, lctx2, T, x2_d, WinT, wconvc, bconvc, WxT,
                                    WdtT, bdtc, AlogC, lngcol2, lnbcol2, WoutT, boutc,
                                    WrT2, br2, W1T2, b1col2, W2T2, b2col2, x3_d,
                                    entry=tuple(o4))
            p2 = emit_l2_scan(p, nc, lctx2, T, scr2, AlogC, entry=tuple(p1))
            p3 = emit_l2_out(p, nc, lctx2, T, x2_d, scr2, lngcol2, lnbcol2, WoutT,
                             boutc, entry=tuple(p2))
            o5 = emit_l2_moe(p, nc, lctx2, T, scr2, l2r, W1T2, b1col2, W2T2, b2col2,
                             x3_d, entry=tuple(p3))
        with ExitStack() as lctx3:
            ohp3 = SB(nc, lctx3, [8, T], BF16, "ohp3")
            q1, scr3 = emit_l3_phase1(p, nc, lctx3, T, x3_d, wqkvT3, relbias3, wgate3,
                                      bgate3, ohp3, entry=tuple(o5))
            q2 = emit_l3_attn_heads(p, nc, lctx3, T, scr3, wprojT3, pbcol3, x3_d,
                                    entry=tuple(q1))
            fins = emit_l3_moe_main(p, nc, lctx3, T, scr3[3], scr3[5],
                                    ohp3, W1T3, b1col3, W2T3, b2col3,
                                    x4, entry=tuple(q2))
        p.op("gpsimd", lambda g: g.nop(), after=tuple(fins))
        p.emit()
    return nc


def kernel_fused(**I):
    T = 4096
    x = np.asarray(I["L0_attn_x"], np.float32)
    n = 8
    nc = build_full(T=T)
    xfull = np.ascontiguousarray(x.transpose(0, 2, 1))
    wconv = np.asarray(I["L2_attn_W_conv"], np.float32)[:, 0, :]
    common = {
        "xfull": xfull,
        "qkvT": _t(I["L0_attn_qkv_weight"]),
        "projT": _t(I["L0_attn_proj_weight"]),
        "pbias": _cols(I["L0_attn_proj_bias"]),
        "wfT": _t(I["L0_ffn_wf"]), "wiT": _t(I["L0_ffn_wi"]),
        "wcT": _t(I["L0_ffn_wc"]), "ufT": _t(I["L0_ffn_uf"]),
        "uiT": _t(I["L0_ffn_ui"]), "ucT": _t(I["L0_ffn_uc"]),
        "uoT": _t(I["L0_ffn_uo"]), "woT": _t(I["L0_ffn_wo"]),
        "bfc": _cols(I["L0_ffn_bf"]), "bic": _cols(I["L0_ffn_bi"]),
        "bcc": _cols(I["L0_ffn_bc"]), "incc": _cols(I["L0_ffn_in_c"]),
        "bocol": _cols(I["L0_ffn_bo"]),
        "WqT": _t(I["L1_attn_Wq"]), "WkT": _t(I["L1_attn_Wk"]),
        "WvT": _t(I["L1_attn_Wv"]), "WgT": _t(I["L1_attn_Wg"]),
        "bg": np.asarray(I["L1_attn_bg"], np.float32)[None, :],
        "WdlT": _t(I["L1_attn_Wdl"]),
        "Lpar": np.ascontiguousarray(np.asarray(I["L1_attn_Lparam"], np.float32).reshape(1, 96)),
        "WoT1": _t(I["L1_attn_Wo"]),
        "lngcol1": _cols(I["L1_attn_ln_g"]), "lnbcol1": _cols(I["L1_attn_ln_b"]),
        "rwT1": _t(I["L1_ffn_router_w"]),
        "rb1": np.asarray(I["L1_ffn_router_b"], np.float32)[None, :],
        "w1T1": np.ascontiguousarray(np.asarray(I["L1_ffn_weights1"], np.float32)),
        "b1col1": np.ascontiguousarray(np.asarray(I["L1_ffn_bias1"], np.float32)
                                       .reshape(32, 2, 128).transpose(0, 2, 1)),
        "w2T1": np.ascontiguousarray(np.asarray(I["L1_ffn_weights2"], np.float32)),
        "b2col1": np.ascontiguousarray(np.asarray(I["L1_ffn_bias2"], np.float32)
                                       .reshape(32, 8, 128).transpose(0, 2, 1)),
        "WinT": _t(I["L2_attn_W_in"]),
        "wconvc": np.ascontiguousarray(wconv.reshape(8, 128, 4).transpose(1, 0, 2)
                                       .reshape(128, 32)),
        "bconvc": _cols(I["L2_attn_b_conv"]),
        "WxT": _t(I["L2_attn_W_xproj"]),
        "WdtT": _t(I["L2_attn_W_dt"]),
        "bdtc": _cols(I["L2_attn_b_dt"]),
        "AlogC": np.ascontiguousarray(np.asarray(I["L2_attn_A_log"], np.float32)
                                      .reshape(8, 128, 16).transpose(1, 0, 2)
                                      .reshape(128, 128)),
        "lngcol2": _cols(I["L2_attn_ln_gamma"]), "lnbcol2": _cols(I["L2_attn_ln_beta"]),
        "WoutT": _t(I["L2_attn_W_out"]),
        "boutc": _cols(I["L2_attn_b_out"]),
        "WrT2": np.ascontiguousarray(np.asarray(I["L2_ffn_Wr"], np.float32)),
        "br2": np.asarray(I["L2_ffn_br"], np.float32)[None, :],
        "W1T2": np.ascontiguousarray(np.asarray(I["L2_ffn_W1"], np.float32)),
        "b1col2": np.ascontiguousarray(np.asarray(I["L2_ffn_b1"], np.float32)
                                       .reshape(8, 32, 128).transpose(0, 2, 1)),
        "W2T2": np.ascontiguousarray(np.asarray(I["L2_ffn_W2"], np.float32)),
        "b2col2": np.ascontiguousarray(np.asarray(I["L2_ffn_b2"], np.float32)
                                       .reshape(8, 8, 128).transpose(0, 2, 1)),
        "wqkvT3": _t(I["L3_attn_w_qkv"]),
        "wprojT3": _t(I["L3_attn_w_proj"]),
        "pbcol3": _cols(I["L3_attn_b_proj"]),
        "relbias3": np.ascontiguousarray(np.asarray(I["L3_attn_rel_bias_table"], np.float32)),
        "wgate3": np.ascontiguousarray(np.asarray(I["L3_ffn_w_gate"], np.float32)),
        "bgate3": np.asarray(I["L3_ffn_b_gate"], np.float32)[None, :],
        "W1T3": np.ascontiguousarray(np.asarray(I["L3_ffn_W1"], np.float32)),
        "b1col3": np.ascontiguousarray(np.asarray(I["L3_ffn_b1"], np.float32)
                                       .reshape(8, 32, 128).transpose(0, 2, 1)),
        "W2T3": np.ascontiguousarray(np.asarray(I["L3_ffn_W2"], np.float32)),
        "b2col3": np.ascontiguousarray(np.asarray(I["L3_ffn_b2"], np.float32)
                                       .reshape(8, 8, 128).transpose(0, 2, 1)),
    }
    in_maps = [dict(common, xb=np.ascontiguousarray(x[i].T)) for i in range(n)]
    res = run_bass_kernel_spmd(nc, in_maps, core_ids=list(range(n)))
    return np.stack([res.results[i]["x4"].T for i in range(n)], axis=0)


_KERNEL_MULTI = kernel


def kernel(**I):
    import os
    if os.environ.get("MK_MULTI"):
        return _KERNEL_MULTI(**I)
    return kernel_fused(**I)


# version 09 (final)
# speedup vs baseline: 1.3043x; 1.0555x over previous
import numpy as np
from contextlib import ExitStack

import concourse.bass as bass
import concourse.mybir as mybir
from concourse.bass_utils import run_bass_kernel_spmd

F32 = mybir.dt.float32
BF16 = mybir.dt.bfloat16
I32 = mybir.dt.int32
AF = mybir.ActivationFunctionType
ALU = mybir.AluOpType
AX = mybir.AxisListType

D = 1024
PT = D // 128
LN_EPS = 1e-5

ENGINES = ("tensor", "vector", "scalar", "gpsimd", "sync")


def ceil_div(a, b):
    return (a + b - 1) // b


class Op:
    __slots__ = ("engine", "fn", "deps", "units", "pos", "needs_inc", "inc_each",
                 "stream", "idx")

    def __init__(self, engine, fn, deps, units, inc_each, stream, idx):
        self.engine = engine
        self.fn = fn
        self.deps = deps
        self.units = units
        self.inc_each = inc_each
        self.pos = None
        self.needs_inc = False
        self.stream = stream
        self.idx = idx


class Prog:

    def __init__(self, nc, ctx):
        self.nc = nc
        self.ctx = ctx
        self.ops = []
        self.sems = {}
        self.uid = 0
        self.last_on_stream = {}

    def _mk(self, p):
        self.uid += 1
        return f"{p}{self.uid}"

    def op(self, engine, fn, after=(), units=1, dma=False, stream=None):
        assert engine in ENGINES, engine
        skey = (f"dma:{engine}:" + (stream or "_")) if dma else engine
        deps = tuple(a for a in after if a is not None)
        if dma and skey in self.last_on_stream:
            deps = deps + (self.last_on_stream[skey],)
        o = Op(engine, fn, deps, units, 16 if dma else 1, skey, len(self.ops))
        for d in o.deps:
            assert d.idx < o.idx
        self.ops.append(o)
        if dma:
            self.last_on_stream[skey] = o
        return o

    def dma(self, engine, out, in_, after=(), stream=None, **kw):
        return self.op(engine, lambda e: e.dma_start(out=out, in_=in_, **kw),
                       after=after, dma=True, stream=stream)

    def emit(self):
        nc = self.nc
        for o in self.ops:
            for d in o.deps:
                d.needs_inc = True
        cnt = {}
        for o in self.ops:
            if o.inc_each == 16:
                o.needs_inc = True
            if o.needs_inc:
                cnt[o.stream] = cnt.get(o.stream, 0) + o.units * o.inc_each
                o.pos = cnt[o.stream]
        for skey in cnt:
            self.sems[skey] = self.ctx.enter_context(
                nc.semaphore(self._mk("sem_" + skey.replace(":", "_"))))
        used_eng = {e for e in ENGINES if any(o.engine == e for o in self.ops)}
        by_eng = {e: [o for o in self.ops if o.engine == e] for e in used_eng}

        with nc.Block() as block:
            def run_engine(eng_obj, ename):
                waited = {}
                for o in by_eng[ename]:
                    need = {}
                    for d in o.deps:
                        assert d.pos is not None, "dep never marked"
                        if d.pos > waited.get(d.stream, 0):
                            need[d.stream] = max(need.get(d.stream, 0), d.pos)
                    for skey, v in need.items():
                        eng_obj.wait_ge(self.sems[skey], v)
                        waited[skey] = v
                    insts = o.fn(eng_obj)
                    if o.needs_inc:
                        if isinstance(insts, (list, tuple)):
                            assert len(insts) == o.units, (len(insts), o.units)
                            for i in insts:
                                i.then_inc(self.sems[o.stream], o.inc_each)
                        else:
                            assert o.units == 1, o.units
                            insts.then_inc(self.sems[o.stream], o.inc_each)

            for ename in ("sync", "gpsimd", "scalar", "vector", "tensor"):
                if ename not in used_eng:
                    continue
                getattr(block, ename)(lambda eng, _n=ename: run_engine(eng, _n))



def sbuf(nc, ctx, shape, dt, name="t"):
    return ctx.enter_context(nc.sbuf_tensor(name + f"_{id(ctx) % 997}_{np.random.randint(1e9)}", shape, dt))


_NAME_N = [0]


def _nm(p):
    _NAME_N[0] += 1
    return f"{p}{_NAME_N[0]}"


def SB(nc, ctx, shape, dt, name="t"):
    return ctx.enter_context(nc.sbuf_tensor(_nm(name), shape, dt))


def PS(nc, ctx, shape, dt=F32, name="p"):
    return ctx.enter_context(nc.psum_tensor(_nm(name), shape, dt))


def load_weight_bf16(p, wsb, w_hbm, IN, OUT, after=()):
    src = w_hbm.rearrange("(i p) o -> p i o", p=128)
    dst = wsb[:].rearrange("p (i o) -> p i o", o=OUT)
    return p.dma("gpsimd", dst, src, after=after)


def wtile(wsb, i_blk, o_blk, OUT, width=128):
    base = i_blk * OUT + o_blk * 128
    return wsb[:, base: base + width]


def mm_chain(p, psum_ap, lhsT_tiles, rhs_tiles, after=(), extra_first=None):
    K = len(lhsT_tiles)

    def fn(te):
        ins = None
        for k in range(K):
            ins = te.matmul(psum_ap, lhsT_tiles[k], rhs_tiles[k],
                            start=(k == 0), stop=(k == K - 1))
        return ins
    return p.op("tensor", fn, after=after)


def emit_ln(p, nc, ctx, x_tiles, xn_tiles, Tt, ones_c, ones_r, bufs,
            x_ready, eps=LN_EPS, out_extra=None, xn32_tiles=None):
    p_sum1, p_sum2, p_bc1, p_bc2, stat, xbfl, sq = (
        bufs["p_sum1"], bufs["p_sum2"], bufs["p_bc1"], bufs["p_bc2"],
        bufs["stat"], bufs["xbf"], bufs["sq"])
    xbf = xbfl[0]
    sums = []
    m1 = None
    m2 = None
    sqop = None
    for i in range(PT):
        cast = p.op("vector", lambda v, i=i: v.tensor_copy(xbf[:, 0:Tt], x_tiles[i][:, 0:Tt]),
                    after=x_ready + ((m1,) if m1 is not None else ())
                    + ((sqop,) if sqop is not None else ()))
        sqop = p.op("scalar", lambda s, i=i: s.activation(sq[:, 0:Tt], xbf[:, 0:Tt], AF.Square),
                    after=(cast,) + ((m2,) if m2 is not None else ()))
        m1 = p.op("tensor", lambda te, i=i: te.matmul(p_sum1[0:1, 0:Tt], ones_c[:, 0:1], xbf[:, 0:Tt],
                                                      start=(i == 0), stop=(i == PT - 1)),
                  after=(cast,))
        m2 = p.op("tensor", lambda te, i=i: te.matmul(p_sum2[0:1, 0:Tt], ones_c[:, 0:1], sq[:, 0:Tt],
                                                      start=(i == 0), stop=(i == PT - 1)),
                  after=(sqop,))
        sums += [m1, m2]
    mean_ap = stat[0:1, 0:Tt]
    ex2_ap = stat[0:1, Tt:2 * Tt]
    sc1_ap = stat[0:1, 2 * Tt:3 * Tt]
    a1 = p.op("vector", lambda v: v.tensor_scalar_mul(mean_ap, p_sum1[0:1, 0:Tt], 1.0 / D),
              after=(sums[-2],))
    a2 = p.op("vector", lambda v: v.tensor_scalar_mul(ex2_ap, p_sum2[0:1, 0:Tt], 1.0 / D),
              after=(sums[-1],))
    a3 = p.op("vector", lambda v: v.tensor_tensor(sc1_ap, mean_ap, mean_ap, ALU.mult),
              after=(a1,))
    a4 = p.op("vector", lambda v: v.tensor_tensor(ex2_ap, ex2_ap, sc1_ap, ALU.subtract),
              after=(a2, a3))
    st1 = p.op("vector", lambda v: v.tensor_scalar_add(ex2_ap, ex2_ap, eps),
               after=(a4,))
    r1 = p.op("scalar", lambda s: s.activation(sc1_ap, ex2_ap, AF.Ln), after=(st1, a3))
    st2 = p.op("scalar", lambda s: s.activation(ex2_ap, sc1_ap, AF.Exp, scale=-0.5),
               after=(r1,))
    bc1 = p.op("tensor", lambda te: te.matmul(p_bc1[:, 0:Tt], ones_r[0:1, :], stat[0:1, 0:Tt],
                                              start=True, stop=True), after=(st1,))
    bc2 = p.op("tensor", lambda te: te.matmul(p_bc2[:, 0:Tt], ones_r[0:1, :], stat[0:1, Tt:2 * Tt],
                                              start=True, stop=True), after=(st2,))
    outs = []
    nsc = bufs["nsc"][0]
    for i in range(PT):
        n1 = p.op("vector", lambda v, i=i: v.tensor_tensor(
            nsc[:, 0:Tt], x_tiles[i][:, 0:Tt], p_bc1[:, 0:Tt], ALU.subtract),
            after=(bc1,) + tuple(x_ready) + tuple(outs[-1:]))
        if xn32_tiles is None:
            n2 = p.op("vector", lambda v, i=i: v.tensor_tensor(
                xn_tiles[i][:, 0:Tt], nsc[:, 0:Tt], p_bc2[:, 0:Tt], ALU.mult),
                after=(n1, bc2))
        else:
            n2a = p.op("vector", lambda v, i=i: v.tensor_tensor(
                xn32_tiles[i][:, 0:Tt], nsc[:, 0:Tt], p_bc2[:, 0:Tt], ALU.mult),
                after=(n1, bc2) + tuple(x_ready))
            n2 = p.op("vector", lambda v, i=i: v.tensor_copy(
                xn_tiles[i][:, 0:Tt], xn32_tiles[i][:, 0:Tt]), after=(n2a,))
        outs.append(n2)
    if out_extra is not None:
        mean_row, rstd_row = out_extra
        sv1 = p.op("vector", lambda v: v.tensor_copy(mean_row, mean_ap), after=(st2,))
        sv2 = p.op("vector", lambda v: v.tensor_copy(rstd_row, ex2_ap), after=(st2,))
        outs += [sv1, sv2]
    return outs


def emit_l0_attn(p, nc, ctx, T, xb, qkvT, projT, pbias, attn_acc, xnb_out, entry=()):
    H, HD = 32, 32
    TT = 512
    NS = T // TT
    NB = T // 128
    EPS = 1e-15

    import ml_dtypes
    ones_c_h = nc.inline_tensor(np.ones((128, 1), ml_dtypes.bfloat16), name=_nm("onc"))
    ones_r_h = nc.inline_tensor(np.ones((1, 128), np.float32), name=_nm("onr"))
    ident_h = nc.inline_tensor(np.eye(128, dtype=ml_dtypes.bfloat16), name=_nm("idn"))

    with ExitStack() as c1:
        q_fm = [SB(nc, c1, [128, T], BF16, "qfm") for _ in range(PT)]
        vk_bf = SB(nc, c1, [128, 8 * 129], BF16, "vkbf")
        vk_rep = [SB(nc, c1, [128, 33], BF16, "vkrep") for _ in range(8)]

        with ExitStack() as cp:
            qkv_sb = SB(nc, cp, [128, PT * 3 * D], BF16, "qkvw")
            x_f = [[SB(nc, cp, [128, TT], F32, "x") for _ in range(PT)] for _ in range(2)]
            xn_bf = [[SB(nc, cp, [128, TT], BF16, "xn") for _ in range(PT)] for _ in range(2)]
            kv_tm = [SB(nc, cp, [128, 2048], BF16, "kvtm") for _ in range(2)]
            vk_acc = SB(nc, cp, [128, 8 * 129], F32, "vkacc")
            ones_c = SB(nc, cp, [128, 1], BF16, "oc")
            ones_r = SB(nc, cp, [1, 128], F32, "orr")
            lnbufs = dict(
                p_sum1=None, p_sum2=None, p_bc1=None, p_bc2=None,
                stat=SB(nc, cp, [1, 3 * TT], F32, "st"),
                xbf=[SB(nc, cp, [128, TT], BF16, "xc")],
                sq=SB(nc, cp, [128, TT], BF16, "sqq"),
                nsc=[SB(nc, cp, [128, TT], F32, "nsc")],
            )
            lnbufs["p_sum1"] = PS(nc, cp, [1, TT], F32, "ps1")
            lnbufs["p_sum2"] = PS(nc, cp, [1, TT], F32, "ps2")
            lnbufs["p_bc1"] = PS(nc, cp, [128, TT], F32, "pb1")
            lnbufs["p_bc2"] = PS(nc, cp, [128, TT], F32, "pb2")
            p_mm = [PS(nc, cp, [128, TT], F32, "pmm") for _ in range(2)]
            p_vk = [PS(nc, cp, [128, 129], F32, "pvk") for _ in range(2)]

            lw = load_weight_bf16(p, qkv_sb, qkvT, D, 3 * D, after=entry)
            lo = p.op("sync", lambda e: [
                e.dma_start(out=ones_c[:], in_=ones_c_h.ap()[:, :]),
                e.dma_start(out=ones_r[:], in_=ones_r_h.ap()[:, :])],
                units=2, dma=True, stream="cst", after=entry)
            lo2 = lo
            vz = p.op("vector", lambda v: v.memset(vk_acc[:], 0.0), after=entry)

            prev_ln = []
            prev_x_cons = [None, None]
            prev_xn_cons = [(), ()]
            prev_kv_cons = [None, None]
            vk_adds = []
            q_evs_all = []
            xnb_dmas = []
            for s in range(NS):
                par = s % 2
                xl = p.op("sync", lambda e, s=s, par=par: [
                    e.dma_start(out=x_f[par][i][:], in_=xb[128 * i:128 * (i + 1), TT * s:TT * (s + 1)])
                    for i in range(PT)], units=PT, dma=True,
                    stream=f"ax{par}",
                    after=((prev_x_cons[par],) if prev_x_cons[par] else ()) + entry)
                xn_view = [xn_bf[par][i][:] for i in range(PT)]
                x_view = [x_f[par][i] for i in range(PT)]
                louts = emit_ln(p, nc, ctx, x_view, xn_view, TT, ones_c, ones_r,
                                lnbufs, x_ready=(xl, lo, lo2) + tuple(prev_ln)
                                + tuple(prev_xn_cons[par]))
                prev_ln = louts
                prev_x_cons[par] = louts[-1]
                xnb_dmas.append(p.op("sync", lambda e, s=s, par=par: [
                    e.dma_start(out=xnb_out[128 * i:128 * (i + 1), TT * s:TT * (s + 1)],
                                in_=xn_bf[par][i][:]) for i in range(PT)],
                    units=PT, dma=True, stream="xnbo", after=tuple(louts)))
                for o in range(PT):
                    pb = p_mm[o % 2]
                    mm = mm_chain(p, pb[:, 0:TT],
                                  [wtile(qkv_sb, i, o, 3 * D) for i in range(PT)],
                                  [xn_bf[par][i][:] for i in range(PT)],
                                  after=tuple(louts) + (lw,)
                                  + ((q_evs_all[-2],) if len(q_evs_all) >= 2 else ())
                                  + ((vk_adds[-1],) if vk_adds else ()))
                    ev = p.op("scalar", lambda sc, o=o, s=s, pb=pb: sc.activation(
                        q_fm[o][:, TT * s:TT * (s + 1)], pb[:, 0:TT], AF.Relu),
                        after=(mm,))
                    q_evs_all.append(ev)
                for bb in range(4):
                    b = 4 * s + bb
                    kpar = b % 2
                    kv_evs = []
                    for ot in range(4):
                        pb = p_mm[ot % 2]
                        def kvmm_fn(te, bb=bb, s=s, ot=ot, pb=pb, par=par):
                            ins = None
                            for i in range(PT):
                                ins = te.matmul(
                                    pb[:, 0:TT],
                                    xn_bf[par][i][:, 128 * bb: 128 * bb + 128],
                                    qkv_sb[:, i * 3 * D + 1024 + 512 * ot: i * 3 * D + 1024 + 512 * ot + 512],
                                    start=(i == 0), stop=(i == PT - 1))
                            return ins
                        war = (q_evs_all[-1],) if (bb == 0 and ot < 2) else ()
                        war2 = (kv_evs[-2],) if len(kv_evs) >= 2 else ()
                        mm = p.op("tensor", kvmm_fn,
                                  after=tuple(louts) + war + war2 + (lw,) +
                                  ((prev_kv_cons[kpar],) if prev_kv_cons[kpar] else ()))
                        ev = p.op("scalar", lambda sc, ot=ot, kpar=kpar, pb=pb: sc.activation(
                            kv_tm[kpar][:, 512 * ot:512 * ot + 512], pb[:, 0:TT], AF.Relu),
                            after=(mm,))
                        kv_evs.append(ev)
                    pack_adds = []
                    for pk in range(8):
                        pvk = p_vk[pk % 2]
                        def vk_fn(te, pk=pk, kpar=kpar, pvk=pvk):
                            i1 = te.matmul(pvk[:, 0:128], kv_tm[kpar][:, 128 * pk:128 * pk + 128],
                                           kv_tm[kpar][:, 1024 + 128 * pk:1024 + 128 * pk + 128],
                                           start=True, stop=False)
                            return te.matmul(pvk[:, 128:129], kv_tm[kpar][:, 128 * pk:128 * pk + 128],
                                             ones_c[:, 0:1], start=False, stop=True)
                        war = (vk_adds[-2],) if len(vk_adds) >= 2 else ()
                        mm = p.op("tensor", vk_fn, after=tuple(kv_evs) + war + (lo, vz))
                        ad = p.op("vector", lambda v, pk=pk, pvk=pvk: v.tensor_tensor(
                            vk_acc[:, 129 * pk:129 * (pk + 1)], vk_acc[:, 129 * pk:129 * (pk + 1)],
                            pvk[:, 0:129], ALU.add), after=(mm,))
                        vk_adds.append(ad)
                        pack_adds.append(ad)
                    prev_kv_cons[kpar] = pack_adds[-1]
                prev_xn_cons[par] = (xnb_dmas[-1], vk_adds[-1], q_evs_all[-1])

            vkc = p.op("vector", lambda v: v.tensor_copy(vk_bf[:], vk_acc[:]),
                       after=tuple(vk_adds[-8:]))
            def rep_fn(e):
                outl = []
                for pk in range(8):
                    for j in range(4):
                        outl.append(e.dma_start(
                            out=vk_rep[pk][32 * j:32 * j + 32, 0:32],
                            in_=vk_bf[32 * j:32 * j + 32, 129 * pk + 32 * j:129 * pk + 32 * j + 32]))
                        outl.append(e.dma_start(
                            out=vk_rep[pk][32 * j:32 * j + 32, 32:33],
                            in_=vk_bf[32 * j:32 * j + 32, 129 * pk + 128:129 * pk + 129]))
                return outl
            rep_all = p.op("sync", rep_fn, units=64, dma=True, stream="vkrep",
                           after=(vkc,))
            rep_dmas = [rep_all] * 8

        xnb_pass1_done = (q_evs_all[-1], xnb_dmas[-1])
        with ExitStack() as cp:
            projw = SB(nc, cp, [128, PT * D], BF16, "projw")
            pb_sb = SB(nc, cp, [128, PT], F32, "pbias")
            pi_tm = SB(nc, cp, [128, 4 * D], BF16, "pitm")
            pi_fm = [SB(nc, cp, [128, TT], BF16, "pifm") for _ in range(PT)]
            rcol = SB(nc, cp, [128, 8], F32, "rcol")
            ident = SB(nc, cp, [128, 128], BF16, "ident")
            xo_f = [SB(nc, cp, [128, TT], F32, "xo") for _ in range(2)]
            ao_f = [SB(nc, cp, [128, TT], F32, "ao") for _ in range(2)]

            p_h = [PS(nc, cp, [128, 33], F32, "ph") for _ in range(4)]
            p_tp = PS(nc, cp, [128, 128], BF16, "ptp")
            p_pr = [PS(nc, cp, [128, TT], F32, "ppr") for _ in range(2)]

            lw2 = load_weight_bf16(p, projw, projT, D, D,
                                   after=tuple(vk_adds[-1:]) + tuple(rep_dmas) + xnb_pass1_done)
            lpb = p.op("sync", lambda e: [
                e.dma_start(out=pb_sb[:], in_=pbias[:, :]),
                e.dma_start(out=ident[:], in_=ident_h.ap()[:, :])],
                units=2, dma=True, stream="cst", after=(vkc, rep_dmas[-1]))
            lid = lpb

            prev_pi_cons = None
            last_ad = [None, None]
            last_bi = [None, None]
            last_st = [None, None]
            last_proj = None
            hist = []
            for s in range(NS):
                pi_ops = []
                for bb in range(4):
                    b = 4 * s + bb
                    for pk in range(8):
                        deps = [rep_dmas[pk]]
                        if prev_pi_cons is not None:
                            deps.append(prev_pi_cons)
                        mm4 = []
                        for j in range(4):
                            def h_fn2(te, pk=pk, j=j, bb=bb, s=s):
                                lhsT = q_fm[pk][32 * j:32 * j + 32,
                                                TT * s + 128 * bb: TT * s + 128 * bb + 128]
                                return te.matmul(p_h[j][:, 0:33], lhsT,
                                                 vk_rep[pk][32 * j:32 * j + 32, 0:33],
                                                 start=True, stop=True,
                                                 tile_position=(32 * j, 0))
                            mm4.append(p.op("tensor", h_fn2,
                                            after=tuple(deps) + tuple(hist[-8:])))
                        for j in range(4):
                            h = 4 * pk + j
                            c1o = p.op("vector", lambda v, j=j: v.tensor_scalar_add(
                                rcol[:, j:j + 1], p_h[j][:, 32:33], EPS),
                                after=(mm4[j],) + tuple(hist[-4:]))
                            c2o = p.op("vector", lambda v, j=j: v.reciprocal(
                                rcol[:, 4 + j:5 + j], rcol[:, j:j + 1]), after=(c1o,))
                            c3o = p.op("vector", lambda v, j=j, h=h, bb=bb: v.tensor_scalar(
                                pi_tm[:, 1024 * bb + 32 * h:1024 * bb + 32 * h + 32],
                                p_h[j][:, 0:32], rcol[:, 4 + j:5 + j], None, ALU.mult),
                                after=(c2o,))
                            pi_ops.append(c3o)
                            hist.append(c3o)
                tps = []
                for bb in range(4):
                    for o in range(PT):
                        tp = p.op("tensor", lambda te, bb=bb, o=o: te.transpose(
                            p_tp[:, 0:128], pi_tm[:, 1024 * bb + 128 * o:1024 * bb + 128 * o + 128],
                            ident[:]), after=tuple(pi_ops) + ((tps[-1],) if tps else ()) + (lid,))
                        ev = p.op("vector", lambda v, bb=bb, o=o: v.tensor_copy(
                            pi_fm[o][:, 128 * bb:128 * bb + 128], p_tp[:, 0:128]),
                            after=(tp,) + ((last_proj,) if (bb == 0 and o == 0 and last_proj) else ()))
                        tps.append(ev)
                prev_pi_cons = tps[-1]
                for o in range(PT):
                    par = o % 2
                    ppr = p_pr[par]
                    xlo = p.op("sync", lambda e, s=s, o=o, par=par: [
                        e.dma_start(out=xo_f[par][:, :],
                                    in_=xb[128 * o:128 * (o + 1), TT * s:TT * (s + 1)])],
                        units=1, dma=True, stream=f"xo{par}",
                        after=(last_ad[par],) if last_ad[par] else ())
                    mm = mm_chain(p, ppr[:, 0:TT],
                                  [wtile(projw, i, o, D) for i in range(PT)],
                                  [pi_fm[i][:, 0:TT] for i in range(PT)],
                                  after=tuple(tps) + (lw2,)
                                  + ((last_bi[par],) if last_bi[par] else ()))
                    bi = p.op("scalar", lambda sc, o=o, par=par, ppr=ppr: sc.activation(
                        ao_f[par][:, 0:TT], ppr[:, 0:TT], AF.Identity, bias=pb_sb[:, o:o + 1]),
                        after=(mm, lpb) + ((last_st[par],) if last_st[par] else ()))
                    ad = p.op("vector", lambda v, par=par: v.tensor_tensor(
                        ao_f[par][:, 0:TT], ao_f[par][:, 0:TT], xo_f[par][:, 0:TT], ALU.add),
                        after=(bi, xlo))
                    st = p.op("sync", lambda e, o=o, s=s, par=par: [
                        e.dma_start(out=attn_acc[128 * o:128 * (o + 1), TT * s:TT * (s + 1)],
                                    in_=ao_f[par][:, 0:TT])], units=1, dma=True,
                        stream="aout", after=(ad,))
                    last_ad[par] = ad
                    last_bi[par] = bi
                    last_st[par] = st
                last_proj = st
            return [st] + xnb_dmas


def emit_l0_ffn(p, nc, ctx, R, NSTEP, xr, wT, bcol, cuo_out, entry=()):
    import ml_dtypes
    ones_c_h = nc.inline_tensor(np.ones((128, 1), ml_dtypes.bfloat16), name=_nm("onc"))
    ones_r_h = nc.inline_tensor(np.ones((1, 128), np.float32), name=_nm("onr"))

    with ExitStack() as c1:
        w_sb = {k: SB(nc, c1, [128, PT * D], BF16, k) for k in
                ["wfT", "wiT", "wcT", "ufT", "uiT", "ucT", "uoT"]}
        b_sb = SB(nc, c1, [128, 4 * PT], F32, "bc")
        xr_f = [[SB(nc, c1, [128, R], F32, "xr") for _ in range(PT)]]
        xr_f.append(xr_f[0])
        xn_bf = [[SB(nc, c1, [128, R], BF16, "xn") for _ in range(PT)] for _ in range(2)]
        c_f = [SB(nc, c1, [128, R], F32, "c") for _ in range(PT)]
        c_bf = [SB(nc, c1, [128, R], BF16, "cb") for _ in range(PT)]
        g_f = [SB(nc, c1, [128, R], F32, "gf") for _ in range(2)]
        g_i = [SB(nc, c1, [128, R], F32, "gi") for _ in range(2)]
        g_c = [SB(nc, c1, [128, R], F32, "gc") for _ in range(2)]
        cuo_sb = SB(nc, c1, [128, R], F32, "cuo")
        ones_c = SB(nc, c1, [128, 1], BF16, "oc")
        ones_r = SB(nc, c1, [1, 128], F32, "orr")
        lnbufs = dict(
            stat=SB(nc, c1, [1, 3 * R], F32, "st"),
            xbf=[SB(nc, c1, [128, R], BF16, "xc")],
            sq=SB(nc, c1, [128, R], BF16, "sqq"),
            nsc=[SB(nc, c1, [128, R], F32, "nsc")],
        )
        lnbufs["p_sum1"] = PS(nc, c1, [1, R], F32, "ps1")
        lnbufs["p_sum2"] = PS(nc, c1, [1, R], F32, "ps2")
        lnbufs["p_bc1"] = PS(nc, c1, [128, R], F32, "pb1")
        lnbufs["p_bc2"] = PS(nc, c1, [128, R], F32, "pb2")
        p_g = [PS(nc, c1, [128, R], F32, "pg") for _ in range(3)]

        wkeys = list(w_sb.keys())
        def lw_all(e):
            outl = []
            for k in wkeys:
                src = wT[k].rearrange("(i p) o -> p i o", p=128)
                dst = w_sb[k][:].rearrange("p (i o) -> p i o", o=D)
                outl.append(e.dma_start(out=dst, in_=src))
            return outl
        lw_op = p.op("gpsimd", lw_all, units=len(wkeys), dma=True, stream="wld",
                     after=entry)
        lws = {k: lw_op for k in wkeys}
        lb = p.dma("gpsimd", b_sb[:], None, stream="cst") if False else None
        def lb_fn(e):
            return [e.dma_start(out=b_sb[:, PT * j:PT * (j + 1)], in_=bcol[k][:, :])
                    for j, k in enumerate(["bfc", "bic", "bcc", "incc"])]
        lb = p.op("gpsimd", lb_fn, units=4, dma=True, stream="cst", after=entry)
        lo = p.op("sync", lambda e: [
            e.dma_start(out=ones_c[:], in_=ones_c_h.ap()[:, :]),
            e.dma_start(out=ones_r[:], in_=ones_r_h.ap()[:, :])],
            units=2, dma=True, stream="cst", after=entry)
        lo2 = lo

        initc = []
        for m in range(PT):
            z = p.op("vector", lambda v, m=m: v.memset(c_f[m][:], 0.0), after=entry)
            a = p.op("scalar", lambda sc, m=m: sc.activation(
                c_f[m][:], c_f[m][:], AF.Identity, bias=b_sb[:, 3 * PT + m:3 * PT + m + 1]),
                after=(z, lb))
            cb = p.op("vector", lambda v, m=m: v.tensor_copy(c_bf[m][:], c_f[m][:]),
                      after=(a,))
            initc.append(cb)

        prev_ln = []
        prev_x_cons = [None, None]
        last_xn_cons = [None, None]
        last_cb = {m: initc[m] for m in range(PT)}
        last_gev = {}
        gate_names = [("wfT", "ufT", 0), ("wiT", "uiT", 1), ("wcT", "ucT", 2)]
        for t in range(NSTEP):
            par = t % 2
            xl = p.op("sync", lambda e, t=t, par=par: [
                e.dma_start(out=xr_f[par][i][:], in_=xr[t, 128 * i:128 * (i + 1), :])
                for i in range(PT)], units=PT, dma=True, stream="xr",
                after=tuple(c for c in prev_x_cons if c) + entry)
            xr_dep = (xl, lo, lo2) + tuple(prev_ln)
            if last_xn_cons[par] is not None:
                xr_dep = xr_dep + (last_xn_cons[par],)
            louts = emit_ln(p, nc, ctx, xr_f[par], [xn_bf[par][i][:] for i in range(PT)],
                            R, ones_c, ones_r, lnbufs, x_ready=xr_dep)
            prev_ln = louts
            prev_x_cons[par] = louts[-1]
            u3s = {}
            step_mms = []
            for m in range(PT):
                gevs = []
                for (wname, uname, gi) in gate_names:
                    pg = p_g[gi]
                    def gate_mm(te, m=m, wname=wname, uname=uname, par=par, pg=pg):
                        ins = None
                        for i in range(PT):
                            ins = te.matmul(pg[:, 0:R], wtile(w_sb[wname], i, m, D),
                                            xn_bf[par][i][:], start=(i == 0), stop=False)
                        for i in range(PT):
                            ins = te.matmul(pg[:, 0:R], wtile(w_sb[uname], i, m, D),
                                            c_bf[i][:], start=False, stop=(i == PT - 1))
                        return ins
                    war = (last_gev[gi],) if gi in last_gev else ()
                    mm = p.op("tensor", gate_mm,
                              after=tuple(louts) + tuple(last_cb.values()) + war
                              + (lws[wname], lws[uname]))
                    step_mms.append(mm)
                    dst = (g_f, g_i, g_c)[gi]
                    func = (AF.Sigmoid, AF.Sigmoid, AF.Tanh)[gi]
                    ev = p.op("scalar", lambda sc, m=m, gi=gi, dst=dst, func=func, pg=pg: sc.activation(
                        dst[m % 2][:], pg[:, 0:R], func,
                        bias=b_sb[:, gi * PT + m:gi * PT + m + 1]),
                        after=(mm, lb) + ((u3s[m - 2],) if m - 2 in u3s else ()))
                    gevs.append(ev)
                    last_gev[gi] = ev
                u1 = p.op("vector", lambda v, m=m: v.tensor_tensor(
                    g_i[m % 2][:], g_i[m % 2][:], g_c[m % 2][:], ALU.mult),
                    after=tuple(gevs))
                u2 = p.op("vector", lambda v, m=m: v.tensor_tensor(
                    c_f[m][:], g_f[m % 2][:], c_f[m][:], ALU.mult),
                    after=(gevs[0], gevs[1]))
                u3 = p.op("vector", lambda v, m=m: v.tensor_tensor(
                    c_f[m][:], c_f[m][:], g_i[m % 2][:], ALU.add), after=(u1, u2))
                u3s[m] = u3
            last_xn_cons[par] = step_mms[-1]
            new_cb = {}
            for m in range(PT):
                cb = p.op("vector", lambda v, m=m: v.tensor_copy(c_bf[m][:], c_f[m][:]),
                          after=(u3s[m], step_mms[-1]))
                new_cb[m] = cb
            last_cb = new_cb
        sts = []
        last_ev = None
        for m in range(PT):
            mm = mm_chain(p, p_g[0][:, 0:R],
                          [wtile(w_sb["uoT"], i, m, D) for i in range(PT)],
                          [c_bf[i][:] for i in range(PT)],
                          after=tuple(last_cb.values()) + (lws["uoT"],)
                          + ((last_ev,) if last_ev else ()) + (last_gev[0],))
            ev = p.op("vector", lambda v, m=m: v.tensor_copy(cuo_sb[:], p_g[0][:, 0:R]),
                      after=(mm,) + ((sts[-1],) if sts else ()))
            last_ev = ev
            st = p.op("sync", lambda e, m=m: [
                e.dma_start(out=cuo_out[128 * m:128 * (m + 1), :], in_=cuo_sb[:])],
                units=1, dma=True, stream="cuoo", after=(ev,))
            sts.append(st)
        return sts


def build_L0A(T=4096, R=512, NSTEP=8):
    nc = bass.Bass("TRN2", target_bir_lowering=False)
    xb = nc.dram_tensor("xb", [D, T], F32, kind="ExternalInput").ap()
    xr = nc.dram_tensor("xr", [NSTEP, D, R], F32, kind="ExternalInput").ap()
    qkvT = nc.dram_tensor("qkvT", [D, 3 * D], F32, kind="ExternalInput").ap()
    projT = nc.dram_tensor("projT", [D, D], F32, kind="ExternalInput").ap()
    pbias = nc.dram_tensor("pbias", [128, PT], F32, kind="ExternalInput").ap()
    wT = {k: nc.dram_tensor(k, [D, D], F32, kind="ExternalInput").ap()
          for k in ["wfT", "wiT", "wcT", "ufT", "uiT", "ucT", "uoT"]}
    bcol = {k: nc.dram_tensor(k, [128, PT], F32, kind="ExternalInput").ap()
            for k in ["bfc", "bic", "bcc", "incc"]}
    attn_acc = nc.dram_tensor("attn_acc", [D, T], F32, kind="ExternalOutput").ap()
    xnb_out = nc.dram_tensor("xnb", [D, T], BF16, kind="ExternalOutput").ap()
    cuo_out = nc.dram_tensor("cuo", [D, R], F32, kind="ExternalOutput").ap()

    with ExitStack() as ctx:
        p = Prog(nc, ctx)
        o1 = emit_l0_ffn(p, nc, ctx, R, NSTEP, xr, wT, bcol, cuo_out)
        o2 = emit_l0_attn(p, nc, ctx, T, xb, qkvT, projT, pbias, attn_acc, xnb_out,
                          entry=tuple(o1))
        p.op("gpsimd", lambda g: g.nop(), after=tuple(o1) + tuple(o2))
        p.emit()
    return nc


def emit_l0b(p, nc, ctx, T, attn_acc, xnb, cuo, woT, bocol, x1_out, entry=()):
    TT = 512
    NS = T // TT
    with ExitStack() as c1:
        wo_sb = SB(nc, c1, [128, PT * D], BF16, "wow")
        bo_sb = SB(nc, c1, [128, PT], F32, "boc")
        xn_t = [[SB(nc, c1, [128, TT], BF16, "xnt") for _ in range(PT)] for _ in range(2)]
        cuo_t = [SB(nc, c1, [128, TT], F32, "cuot") for _ in range(2)]
        acc_t = [SB(nc, c1, [128, TT], F32, "acct") for _ in range(2)]
        h_t = [SB(nc, c1, [128, TT], F32, "ht") for _ in range(2)]
        p_mm = [PS(nc, c1, [128, TT], F32, "pmm") for _ in range(2)]

        lw = load_weight_bf16(p, wo_sb, woT, D, D, after=entry)
        lb = p.dma("sync", bo_sb[:], bocol[:, :], stream="cst", after=entry)

        last_xn_cons = [None, None]
        last_h = [None, None]
        last_st = None
        for s in range(NS):
            par = s % 2
            xl = p.op("sync", lambda e, s=s, par=par: [
                e.dma_start(out=xn_t[par][i][:], in_=xnb[128 * i:128 * (i + 1), TT * s:TT * (s + 1)])
                for i in range(PT)], units=PT, dma=True, stream=f"bx{par}",
                after=((last_xn_cons[par],) if last_xn_cons[par] else ()) + entry)
            for o in range(PT):
                opar = o % 2
                pb = p_mm[opar]
                cl = p.op("gpsimd", lambda e, s=s, o=o, opar=opar: [
                    e.dma_start(out=cuo_t[opar][:], in_=cuo[128 * o:128 * (o + 1), TT * s:TT * (s + 1)]),
                    e.dma_start(out=acc_t[opar][:], in_=attn_acc[128 * o:128 * (o + 1), TT * s:TT * (s + 1)])],
                    units=2, dma=True, stream=f"bc{opar}",
                    after=((last_h[opar],) if last_h[opar] else ()) + entry)
                mm = mm_chain(p, pb[:, 0:TT],
                              [wtile(wo_sb, i, o, D) for i in range(PT)],
                              [xn_t[par][i][:] for i in range(PT)],
                              after=(xl, lw) + ((last_h[opar],) if last_h[opar] else ()))
                if o == PT - 1:
                    last_xn_cons[par] = mm
                a1 = p.op("vector", lambda v, opar=opar, pb=pb: v.tensor_tensor(
                    h_t[opar][:], pb[:, 0:TT], cuo_t[opar][:], ALU.add), after=(mm, cl))
                a2 = p.op("scalar", lambda sc, o=o, opar=opar: sc.activation(
                    h_t[opar][:], h_t[opar][:], AF.Tanh, bias=bo_sb[:, o:o + 1]),
                    after=(a1, lb))
                a3 = p.op("vector", lambda v, opar=opar: v.tensor_tensor(
                    h_t[opar][:], h_t[opar][:], acc_t[opar][:], ALU.add), after=(a2, cl))
                st = p.op("sync", lambda e, o=o, s=s, opar=opar: [
                    e.dma_start(out=x1_out[128 * o:128 * (o + 1), TT * s:TT * (s + 1)],
                                in_=h_t[opar][:])], units=1, dma=True,
                    stream="x1o", after=(a3,))
                last_h[opar] = st
                last_st = st
        return [last_st]


def build_L0B(T=4096):
    nc = bass.Bass("TRN2", target_bir_lowering=False)
    attn_acc = nc.dram_tensor("attn_acc", [D, T], F32, kind="ExternalInput").ap()
    xnb = nc.dram_tensor("xnb", [D, T], BF16, kind="ExternalInput").ap()
    cuo = nc.dram_tensor("cuo", [D, T], F32, kind="ExternalInput").ap()
    woT = nc.dram_tensor("woT", [D, D], F32, kind="ExternalInput").ap()
    bocol = nc.dram_tensor("bocol", [128, PT], F32, kind="ExternalInput").ap()
    x1 = nc.dram_tensor("x1", [D, T], F32, kind="ExternalOutput").ap()
    with ExitStack() as ctx:
        p = Prog(nc, ctx)
        o = emit_l0b(p, nc, ctx, T, attn_acc, xnb, cuo, woT, bocol, x1)
        p.op("gpsimd", lambda g: g.nop(), after=tuple(o))
        p.emit()
    return nc


def _l3_host_consts(T):
    import math
    NUM_BUCKETS, MAX_DISTANCE = 32, 128
    d = np.arange(-(T - 1), T)
    nb = NUM_BUCKETS // 2
    rel = (d > 0).astype(np.int64) * nb
    rp = np.abs(d)
    max_exact = nb // 2
    is_small = rp < max_exact
    rpl = max_exact + (np.log(np.maximum(rp, 1).astype(np.float64) / max_exact)
                       / math.log(MAX_DISTANCE / max_exact) * (nb - max_exact)).astype(np.int64)
    rpl = np.minimum(rpl, nb - 1)
    buckets = rel + np.where(is_small, rp, rpl)
    onehot = np.zeros((2 * T - 1, 32), np.float32)
    onehot[np.arange(2 * T - 1), buckets] = 1.0
    return onehot


def emit_l3_phase1(p, nc, ctx, T, x3, wqkvT, relbias, wgate, bgate, ohp_rows,
                   entry=()):
    H, HD = 16, 64
    TT = 512
    NS = T // TT
    NB = T // 128
    ND = 2 * NB - 1
    DFF = 4096
    import ml_dtypes

    onehot_np = _l3_host_consts(T)
    npad = ceil_div(2 * T - 1, 128) * 128
    oh_pad = np.zeros((npad, 32), np.float32)
    oh_pad[:2 * T - 1] = onehot_np
    oh_h = nc.inline_tensor(oh_pad.astype(ml_dtypes.bfloat16), name=_nm("l3oh"))
    ones_c_h = nc.inline_tensor(np.ones((128, 1), ml_dtypes.bfloat16), name=_nm("onc"))
    ones_r_h = nc.inline_tensor(np.ones((1, 128), np.float32), name=_nm("onr"))
    nident_h = nc.inline_tensor((-np.eye(128)).astype(ml_dtypes.bfloat16), name=_nm("nid"))
    ident_h = nc.inline_tensor(np.eye(128, dtype=ml_dtypes.bfloat16), name=_nm("idn"))

    qfm_d = nc.dram_tensor(_nm("l3q"), [D, T], BF16, kind="Internal").ap()
    kfm_d = nc.dram_tensor(_nm("l3k"), [D, T], BF16, kind="Internal").ap()
    vtm_d = nc.dram_tensor(_nm("l3v"), [T, D], BF16, kind="Internal").ap()
    xn_d = nc.dram_tensor(_nm("l3xn"), [D, T], BF16, kind="Internal").ap()
    th_d = nc.dram_tensor(_nm("l3th"), [H, npad], F32, kind="Internal").ap()
    aacc_d = nc.dram_tensor(_nm("l3aa"), [D, T], F32, kind="Internal").ap()

    with ExitStack() as c1:
        qkv_sb = SB(nc, c1, [128, PT * 3 * D], BF16, "qkvw")
        x_f = [[SB(nc, c1, [128, TT], F32, "x") for _ in range(PT)] for _ in range(2)]
        xn_bf = [[SB(nc, c1, [128, TT], BF16, "xn") for _ in range(PT)] for _ in range(2)]
        ones_c = SB(nc, c1, [128, 1], BF16, "oc")
        ones_r = SB(nc, c1, [1, 128], F32, "orr")
        lnbufs = dict(
            stat=SB(nc, c1, [1, 3 * TT], F32, "st"),
            xbf=[SB(nc, c1, [128, TT], BF16, "xc")],
            sq=SB(nc, c1, [128, TT], BF16, "sqq"),
            nsc=[SB(nc, c1, [128, TT], F32, "nsc")],
        )
        oh_sb = SB(nc, c1, [128, (npad // 128) * 32], BF16, "ohs")
        ev_sb = [SB(nc, c1, [128, TT], BF16, "evb") for _ in range(2)]
        xn32 = [SB(nc, c1, [128, TT], F32, "xn32") for _ in range(PT)]
        wg_sb = SB(nc, c1, [128, PT * 8], F32, "wg")
        bg_sb = SB(nc, c1, [1, 8], F32, "bg")
        idf_sb = SB(nc, c1, [128, 128], F32, "idf")
        rbufs = dict(p_lg=None, p_tp=None, lg=SB(nc, c1, [128, 18], F32, "lg"))
        idf_h = nc.inline_tensor(np.eye(128, dtype=np.float32), name=_nm("idf"))
        rb_sb = SB(nc, c1, [32, H], BF16, "rbs")

        lnbufs["p_sum1"] = PS(nc, c1, [1, TT], F32, "ps1")
        lnbufs["p_sum2"] = PS(nc, c1, [1, TT], F32, "ps2")
        lnbufs["p_bc1"] = PS(nc, c1, [128, TT], F32, "pb1")
        lnbufs["p_bc2"] = PS(nc, c1, [128, TT], F32, "pb2")
        p_mm = [PS(nc, c1, [128, TT], F32, "pmm") for _ in range(2)]
        rbufs["p_lg"] = PS(nc, c1, [128, 8], F32, "plg")
        rbufs["p_tp"] = PS(nc, c1, [8, 128], F32, "ptp")

        lw = load_weight_bf16(p, qkv_sb, wqkvT, D, 3 * D, after=entry)
        lwg = p.op("sync", lambda e: [
            e.dma_start(out=wg_sb[:].rearrange("p (i o) -> p i o", o=8),
                        in_=wgate.rearrange("(i p) o -> p i o", p=128)),
            e.dma_start(out=bg_sb[:], in_=bgate[:, :]),
            e.dma_start(out=idf_sb[:], in_=idf_h.ap()[:, :])],
            units=3, dma=True, stream="cst", after=entry)
        lo = p.op("sync", lambda e: [
            e.dma_start(out=ones_c[:], in_=ones_c_h.ap()[:, :]),
            e.dma_start(out=ones_r[:], in_=ones_r_h.ap()[:, :]),
            e.dma_start(out=oh_sb[:].rearrange("p (n c) -> p n c", c=32),
                        in_=oh_h.ap().rearrange("(n p) c -> p n c", p=128)),
            ], units=3, dma=True, stream="cst", after=entry)
        lrb = p.dma("gpsimd", rb_sb[:], relbias.rearrange("b h -> b h"),
                    stream="cst", after=entry)

        thdmas = []
        with ExitStack() as c2:
            p_oht = p_mm[0][0:32, 0:64].bitcast(BF16)
            p_thh = p_mm[1][0:16, 0:128]
            ohT_sb = SB(nc, c2, [32, 128], BF16, "ohT")
            th_all = SB(nc, c2, [16, npad], BF16, "thall")
            idb = SB(nc, c2, [128, 128], BF16, "idb")
            lid = p.dma("sync", idb[:], ident_h.ap()[:, :], stream="cst", after=entry)
            prev_ev = None
            prev_mm = None
            for ch in range(npad // 128):
                tp = p.op("tensor", lambda te, ch=ch: te.transpose(
                    p_oht, oh_sb[:, 32 * ch:32 * ch + 32], idb[:]),
                    after=(lo, lid) + ((prev_ev,) if prev_ev else ()))
                ev = p.op("vector", lambda v: v.tensor_copy(ohT_sb[:], p_oht),
                          after=(tp,) + ((prev_mm,) if prev_mm else ()))
                mm = p.op("tensor", lambda te, ch=ch: te.matmul(
                    p_thh, rb_sb[:, :], ohT_sb[:], start=True, stop=True),
                    after=(ev, lrb))
                cp = p.op("vector", lambda v, ch=ch: v.tensor_copy(
                    th_all[:, 128 * ch:128 * ch + 128], p_thh), after=(mm,))
                prev_ev = cp
                prev_mm = cp
            thd = p.op("gpsimd", lambda e: [e.dma_start(out=th_d[:, :], in_=th_all[:, :])],
                       units=1, dma=True, stream="thd", after=(prev_ev,))
            thdmas.append(thd)

        prev_ln = [thd]
        prev_rt = []
        prev_x_cons = [None, None]
        prev_xn_cons = [(), ()]
        evs = []
        outd = []
        for s in range(NS):
            par = s % 2
            xl = p.op("sync", lambda e, s=s, par=par: [
                e.dma_start(out=x_f[par][i][:], in_=x3[128 * i:128 * (i + 1), TT * s:TT * (s + 1)])
                for i in range(PT)], units=PT, dma=True, stream=f"ax{par}",
                after=((prev_x_cons[par],) if prev_x_cons[par] else ()) + entry)
            xn_view = [xn_bf[par][i][:] for i in range(PT)]
            louts = emit_ln(p, nc, ctx, [x_f[par][i] for i in range(PT)], xn_view, TT,
                            ones_c, ones_r, lnbufs, x_ready=(xl, lo) + tuple(prev_ln)
                            + tuple(prev_xn_cons[par]),
                            xn32_tiles=xn32)
            prev_ln = list(louts)
            prev_x_cons[par] = louts[-1]
            for bb in range(4):
                rv = emit_router_tm(
                    p, nc, ctx, 8, 128,
                    [xn32[i][:, 128 * bb:128 * bb + 128] for i in range(PT)],
                    wg_sb, bg_sb, ones_r,
                    ohp_rows[0:8, TT * s + 128 * bb: TT * s + 128 * bb + 128],
                    rbufs, deps=tuple(louts) + (lwg, lo) + tuple(prev_rt), ident_f32=idf_sb)
                prev_rt = [rv]
                prev_ln.append(rv)
            outd.append(p.op("sync", lambda e, s=s, par=par: [
                e.dma_start(out=xn_d[128 * i:128 * (i + 1), TT * s:TT * (s + 1)],
                            in_=xn_bf[par][i][:]) for i in range(PT)],
                units=PT, dma=True, stream="xnd", after=tuple(louts)))
            for idx, (lo_, hi_, dst, scale) in enumerate(
                    [(0, D, qfm_d, 0.125), (D, 2 * D, kfm_d, 1.0)]):
                for o in range(PT):
                    pb = p_mm[(len(evs)) % 2]
                    mm = mm_chain(p, pb[:, 0:TT],
                                  [wtile(qkv_sb, i, 0, 3 * D, width=3 * D)[:, lo_ + 128 * o: lo_ + 128 * o + 128]
                                   for i in range(PT)],
                                  [xn_bf[par][i][:] for i in range(PT)],
                                  after=tuple(louts) + (lw,)
                                  + ((evs[-2],) if len(evs) >= 2 else ()))
                    epar = len(evs) % 2
                    ev = p.op("scalar", lambda sc, pb=pb, epar=epar, scale=scale: sc.activation(
                        ev_sb[epar][:, 0:TT], pb[:, 0:TT], AF.Identity, scale=scale),
                        after=(mm,) + ((evs[-2],) if len(evs) >= 2 else ()))
                    st = p.op("sync", lambda e, o=o, s=s, dst=dst, epar=epar: [
                        e.dma_start(out=dst[128 * o:128 * (o + 1), TT * s:TT * (s + 1)],
                                    in_=ev_sb[epar][:, 0:TT])], units=1, dma=True,
                        stream="qkout", after=(ev,))
                    evs.append(st)
            for bb in range(4):
                for half in range(2):
                    pb = p_mm[(2 * bb + half) % 2]
                    def vmm2(te, bb=bb, s=s, half=half, pb=pb, par=par):
                        ins = None
                        for i in range(PT):
                            ins = te.matmul(
                                pb[:, 0:TT],
                                xn_bf[par][i][:, 128 * bb: 128 * bb + 128],
                                qkv_sb[:, i * 3 * D + 2 * D + 512 * half: i * 3 * D + 2 * D + 512 * half + 512],
                                start=(i == 0), stop=(i == PT - 1))
                        return ins
                    mm = p.op("tensor", vmm2, after=tuple(louts) + (lw,)
                              + ((evs[-2],) if len(evs) >= 2 else ()))
                    epar = len(evs) % 2
                    ev = p.op("scalar", lambda sc, pb=pb, epar=epar: sc.activation(
                        ev_sb[epar][:, 0:TT], pb[:, 0:TT], AF.Identity),
                        after=(mm,) + ((evs[-2],) if len(evs) >= 2 else ()))
                    st = p.op("sync", lambda e, bb=bb, s=s, half=half, epar=epar: [
                        e.dma_start(out=vtm_d[TT * s + 128 * bb:TT * s + 128 * bb + 128,
                                              512 * half:512 * half + 512],
                                    in_=ev_sb[epar][:, 0:TT])], units=1, dma=True,
                        stream="qkout", after=(ev,))
                    evs.append(st)
            prev_xn_cons[par] = (outd[-1], evs[-1])
        phase1_done = tuple(evs[-4:]) + (outd[-1], thdmas[-1])
    return phase1_done, (qfm_d, kfm_d, vtm_d, xn_d, th_d, aacc_d)


def emit_l3_attn_heads(p, nc, ctx, T, scratch, wprojT, pbcol, x3, entry=()):
    qfm_d, kfm_d, vtm_d, xn_d, th_d, aacc_d = scratch
    H, HD = 16, 64
    TT = 512
    NS = T // TT
    NB = T // 128
    ND = 2 * NB - 1
    import ml_dtypes
    prev_h = nc.inline_tensor(np.eye(128)[::-1].copy().astype(ml_dtypes.bfloat16),
                              name=_nm("prev"))
    nid_h = nc.inline_tensor(np.eye(128).astype(np.float32), name=_nm("nid"))
    ones_cb_h = nc.inline_tensor(np.ones((128, 1), ml_dtypes.bfloat16), name=_nm("onc"))
    ones_r64_h = nc.inline_tensor(np.ones((1, 64), np.float32), name=_nm("on64"))
    onesrw_h = nc.inline_tensor(np.ones((1, 128), np.float32), name=_nm("onrw"))

    with ExitStack() as c2:
        q_sb = SB(nc, c2, [64, T], BF16, "qh")
        k_aug = SB(nc, c2, [65, T], BF16, "kaug")
        v_sb = SB(nc, c2, [128, NB * 65], BF16, "vh")
        th_row = SB(nc, c2, [1, 2 * T - 1], F32, "throw")
        ebt = SB(nc, c2, [128, ND * 128], BF16, "ebt")
        t2b = [SB(nc, c2, [128, 128], BF16, "t2b") for _ in range(2)]
        prev_sb = SB(nc, c2, [128, 128], BF16, "prevs")
        nid_sb = SB(nc, c2, [128, 128], F32, "nids")
        ones_cb = SB(nc, c2, [128, 1], BF16, "ocb")
        ones_r64 = SB(nc, c2, [1, 64], F32, "o64")
        ones_rw = SB(nc, c2, [1, 128], F32, "orw")
        mc8 = SB(nc, c2, [128, 8], F32, "mc8")
        m4 = SB(nc, c2, [128, 32], F32, "m4c")
        negm = SB(nc, c2, [4, 128], F32, "negm")
        bmax = SB(nc, c2, [1, 2], F32, "bmax")
        bmax_c = SB(nc, c2, [128, 1], F32, "bmaxc")
        q_aug = SB(nc, c2, [65, TT], BF16, "qaug")
        pb_sb = [SB(nc, c2, [128, TT], BF16, "pbs") for _ in range(4)]
        rl_row = SB(nc, c2, [1, TT], F32, "rlrow")
        oev = SB(nc, c2, [64, TT], BF16, "oev")
        ao_sb = [SB(nc, c2, [128, T], BF16, "ao") for _ in range(PT)]

        p_sc = [PS(nc, c2, [128, TT], F32, "psc") for _ in range(2)]
        p_eb = PS(nc, c2, [128, 128], F32, "peb")
        p_nm = PS(nc, c2, [128, 128], F32, "pnm")
        p_l = PS(nc, c2, [1, TT], F32, "pl")
        p_o = PS(nc, c2, [65, TT], F32, "po")
        p_bc = PS(nc, c2, [64, TT], F32, "pbc")

        lc = p.op("sync", lambda e: [
            e.dma_start(out=prev_sb[:], in_=prev_h.ap()[:, :]),
            e.dma_start(out=nid_sb[:], in_=nid_h.ap()[:, :]),
            e.dma_start(out=ones_cb[:], in_=ones_cb_h.ap()[:, :]),
            e.dma_start(out=ones_r64[:], in_=ones_r64_h.ap()[:, :]),
            e.dma_start(out=ones_rw[:], in_=onesrw_h.ap()[:, :])],
            units=5, dma=True, stream="cst", after=entry)

        hprev_done = entry
        vones = p.op("vector", lambda v: v.memset(
            v_sb[:].rearrange("p (b c) -> p b c", c=65)[:, :, 64:65], 1.0),
            after=entry)
        for h in range(H):
            ld = p.op("sync", lambda e, h=h: [
                e.dma_start(out=q_sb[:, :], in_=qfm_d[64 * h:64 * h + 64, :]),
                e.dma_start(out=k_aug[0:64, :], in_=kfm_d[64 * h:64 * h + 64, :]),
                e.dma_start(out=v_sb[:].rearrange("p (b c) -> p b c", c=65)[:, :, 0:64],
                            in_=vtm_d.rearrange("(b p) c -> p b c", p=128)[:, :, 64 * h:64 * h + 64]),
                e.dma_start(out=th_row[:, :], in_=th_d[h:h + 1, 0:2 * T - 1])],
                units=4, dma=True, stream="hld", after=tuple(hprev_done) + entry)
            onr = p.op("vector", lambda v: v.memset(k_aug[64:65, :], 1.0),
                       after=(ld,) + tuple(hprev_done))
            bm1 = p.op("vector", lambda v: v.reduce_max(
                bmax[0:1, 0:1], th_row[0:1, :], AX.X, apply_absolute_value=True),
                after=(ld,) + tuple(hprev_done))
            bmm = p.op("tensor", lambda te: te.matmul(
                p_nm[0:128, 0:1] if p_nm.shape[0] >= 128 else p_nm[:, 0:1],
                ones_r64[0:1, 0:1].to_broadcast([1, 128]) if False else ones_rw[0:1, :],
                bmax[0:1, 0:1], start=True, stop=True), after=(bm1, lc))
            bmc = p.op("vector", lambda v: v.tensor_copy(
                bmax_c[:, 0:1], p_nm[0:128, 0:1] if p_nm.shape[0] >= 128 else p_nm[:, 0:1]),
                after=(bmm,))
            ebt_ops = []
            for dd in range(ND):
                par = dd % 2
                t2 = p.op("gpsimd", lambda e, dd=dd, par=par, h=h: [
                    e.dma_start(out=t2b[par][:],
                                in_=bass.AP(th_d.tensor, h * (th_d.shape[1]) + 128 * dd,
                                            [[1, 128], [1, 128]]))],
                    units=1, dma=True, stream=f"t2{par}",
                    after=(ld,) + ((ebt_ops[-2],) if len(ebt_ops) >= 2 else ()))
                rev = ND - 1 - dd
                mm = p.op("tensor", lambda te, par=par: te.matmul(
                    p_eb[:, 0:128], t2b[par][:], prev_sb[:], start=True, stop=True),
                    after=(t2, lc) + ((ebt_ops[-1],) if ebt_ops else ()))
                ex = p.op("scalar", lambda sc, rev=rev: sc.activation(
                    ebt[:, 128 * rev:128 * rev + 128], p_eb[:, 0:128], AF.Exp),
                    after=(mm,))
                ebt_ops.append(ex)
            mhat = {}
            for a in range(NB):
                rms = []
                for c in range(NS):
                    pb = p_sc[c % 2]
                    mm = p.op("tensor", lambda te, a=a, c=c, pb=pb: te.matmul(
                        pb[:, 0:TT], q_sb[:, 128 * a:128 * a + 128],
                        k_aug[0:64, TT * c:TT * (c + 1)], start=True, stop=True),
                        after=(ld, onr) + ((rms[-2],) if len(rms) >= 2 else ())
                        + ((mhat[a - 1],) if a - 1 in mhat else ()))
                    rm = p.op("vector", lambda v, c=c, pb=pb: v.reduce_max(
                        mc8[:, c:c + 1], pb[:, 0:TT], AX.X), after=(mm,))
                    rms.append(rm)
                r2 = p.op("vector", lambda v, a=a: v.reduce_max(
                    m4[:, a:a + 1], mc8[:, 0:NS], AX.X), after=tuple(rms))
                mh = p.op("vector", lambda v, a=a: v.tensor_tensor(
                    m4[:, a:a + 1], m4[:, a:a + 1], bmax_c[:, 0:1],
                    ALU.add), after=(r2, bmc))
                mhat[a] = mh
            exs = []
            prev_fin = None
            prev_nme = None
            prev_rc = None
            prev_oe = None
            last_mm_s = None
            for s in range(NS):
                tps = p.op("tensor", lambda te, s=s: te.transpose(
                    p_nm[0:4, 0:128], m4[:, 4 * s:4 * s + 4], nid_sb[:]),
                    after=(mhat[4 * s], mhat[4 * s + 1], mhat[4 * s + 2],
                           mhat[4 * s + 3], lc, bmc)
                    + ((prev_nme,) if prev_nme else ()))
                nme = p.op("vector", lambda v: v.tensor_scalar_mul(
                    negm[:], p_nm[0:4, 0:128], -1.0),
                    after=(tps,) + ((qa2,) if s > 0 else ()))
                prev_nme = nme
                qa1 = p.op("vector", lambda v, s=s: v.tensor_copy(
                    q_aug[0:64, :], q_sb[:, TT * s:TT * (s + 1)]),
                    after=(ld,) + ((last_mm_s,) if last_mm_s else ()))
                qa2 = p.op("gpsimd", lambda e: [
                    e.dma_start(out=q_aug[64:65, 0:TT], in_=negm[:, :])],
                    units=1, dma=True, stream="qaug", after=(nme, qa1))
                lmm = None
                omm = None
                tts = []
                lmms = []
                omms = []

                def flush_unit(kko):
                    nonlocal lmm, omm, last_mm_s
                    par = kko % 4
                    omm2 = p.op("tensor", lambda te, kko=kko, par=par: te.matmul(
                        p_o[0:65, 0:TT], v_sb[:, 65 * kko:65 * kko + 65],
                        pb_sb[par][:, 0:TT],
                        start=(kko == 0), stop=(kko == NB - 1)),
                        after=(tts[kko], ld, vones)
                        + ((prev_oe,) if (kko == 0 and prev_oe) else ())
                        + ((prev_rc,) if (kko == 0 and prev_rc) else ()))
                    lmms.append(omm2)
                    omms.append(omm2)
                    lmm = omm2
                    omm = omm2
                    last_mm_s = omm2

                for kk in range(NB):
                    par = kk % 4
                    pb = p_sc[kk % 2]
                    mm = p.op("tensor", lambda te, kk=kk, pb=pb: te.matmul(
                        pb[:, 0:TT], k_aug[:, 128 * kk:128 * kk + 128],
                        q_aug[:, 0:TT], start=True, stop=True),
                        after=(qa2, onr) + ((exs[-2],) if len(exs) >= 2 else ()))
                    ex = p.op("scalar", lambda sc, kk=kk, par=par, pb=pb: sc.activation(
                        pb_sb[par][:, 0:TT], pb[:, 0:TT], AF.Exp),
                        after=(mm,) + ((omms[kk - 4],) if kk >= 4 else ()))
                    exs.append(ex)
                    rev0 = 4 * s - kk + NB - 1
                    tt = p.op("vector", lambda v, par=par, rev0=rev0: v.tensor_tensor(
                        pb_sb[par][:, 0:TT], pb_sb[par][:, 0:TT],
                        ebt[:, 128 * rev0:128 * rev0 + 512], ALU.mult),
                        after=(ex,) + tuple(ebt_ops[rev0:rev0 + 4])
                        + ((lmms[kk - 4],) if kk >= 4 else ()))
                    tts.append(tt)
                    if kk >= 2:
                        flush_unit(kk - 2)
                flush_unit(NB - 2)
                flush_unit(NB - 1)
                rc = p.op("vector", lambda v: v.reciprocal(rl_row[0:1, :], p_o[64:65, 0:TT]),
                          after=(lmm,) + ((prev_fin,) if prev_fin else ()))
                prev_rc = rc
                bc = p.op("tensor", lambda te: te.matmul(
                    p_bc[:, 0:TT], ones_r64[0:1, 0:64], rl_row[0:1, :],
                    start=True, stop=True), after=(rc, lc)
                    + ((prev_fin,) if prev_fin else ()))
                oe = p.op("vector", lambda v: v.tensor_copy(oev[:, 0:TT], p_o[0:64, 0:TT]),
                          after=(omm,) + ((prev_fin,) if prev_fin else ()))
                prev_oe = oe
                fin = p.op("vector", lambda v, h=h, s=s: v.tensor_tensor(
                    ao_sb[h // 2][64 * (h % 2):64 * (h % 2) + 64, TT * s:TT * (s + 1)],
                    oev[:, 0:TT], p_bc[:, 0:TT], ALU.mult), after=(oe, bc))
                prev_fin = fin
            hprev_done = (fin, exs[-1], exs[-2])
        with ExitStack() as c3:
            projw = SB(nc, c3, [128, PT * D], BF16, "projw")
            pb2 = SB(nc, c3, [128, PT], F32, "pbias")
            xo_f = [SB(nc, c3, [128, TT], F32, "xo") for _ in range(2)]
            ao_f = [SB(nc, c3, [128, TT], F32, "aof") for _ in range(2)]
            p_pr = p_sc
            lw2 = load_weight_bf16(p, projw, wprojT, D, D, after=tuple(hprev_done))
            lpb = p.dma("sync", pb2[:], pbcol[:, :], stream="cst", after=tuple(hprev_done))
            last_ad = [None, None]
            last_bi = [None, None]
            last_st2 = None
            for s in range(NS):
                for o in range(PT):
                    par = o % 2
                    ppr = p_pr[par]
                    xlo = p.op("sync", lambda e, s=s, o=o, par=par: [
                        e.dma_start(out=xo_f[par][:, :],
                                    in_=x3[128 * o:128 * (o + 1), TT * s:TT * (s + 1)])],
                        units=1, dma=True, stream=f"xo{par}",
                        after=((last_ad[par],) if last_ad[par] else ()) + tuple(hprev_done))
                    mm = mm_chain(p, ppr[:, 0:TT],
                                  [wtile(projw, i, o, D) for i in range(PT)],
                                  [ao_sb[i][:, TT * s:TT * (s + 1)] for i in range(PT)],
                                  after=(lw2,) + tuple(hprev_done)
                                  + ((last_bi[par],) if last_bi[par] else ()))
                    bi = p.op("scalar", lambda sc, o=o, par=par, ppr=ppr: sc.activation(
                        ao_f[par][:, 0:TT], ppr[:, 0:TT], AF.Identity, bias=pb2[:, o:o + 1]),
                        after=(mm, lpb) + ((last_ad[par],) if last_ad[par] else ()))
                    ad = p.op("vector", lambda v, par=par: v.tensor_tensor(
                        ao_f[par][:, 0:TT], ao_f[par][:, 0:TT], xo_f[par][:, 0:TT],
                        ALU.add), after=(bi, xlo))
                    st = p.op("sync", lambda e, o=o, s=s, par=par: [
                        e.dma_start(out=aacc_d[128 * o:128 * (o + 1), TT * s:TT * (s + 1)],
                                    in_=ao_f[par][:, 0:TT])], units=1, dma=True,
                        stream="aout", after=(ad,))
                    last_ad[par] = st
                    last_bi[par] = bi
                    last_st2 = st
            return (last_st2,)



def emit_router_tm(p, nc, ctx, NE, Tt, xn32_view, wg_sb, bg_sb, ones_r,
                   ohp_rows_dst, bufs, deps, scale_by_invsum=True,
                   ident_f32=None):
    p_lg, p_tp, lg = bufs["p_lg"], bufs["p_tp"], bufs["lg"]
    NEc = NE

    def mm_fn2(te):
        ins = None
        for i in range(PT):
            ins = te.matmul(p_lg[:, 0:NEc], xn32_view[i],
                            wg_sb[:, i * NEc:(i + 1) * NEc], start=(i == 0), stop=False)
        return te.matmul(p_lg[:, 0:NEc], ones_r[0:1, 0:128], bg_sb[0:1, 0:NEc],
                         start=False, stop=True)
    mm = p.op("tensor", mm_fn2, after=tuple(deps))
    c0 = p.op("vector", lambda v: v.tensor_copy(lg[:, 0:NEc], p_lg[:, 0:NEc]),
              after=(mm,))
    mx = p.op("vector", lambda v: v.reduce_max(lg[:, NEc:NEc + 1], lg[:, 0:NEc], AX.X),
              after=(c0,))
    oh = p.op("vector", lambda v: v.tensor_scalar(
        lg[:, NEc + 2:2 * NEc + 2], lg[:, 0:NEc], lg[:, NEc:NEc + 1], None, ALU.is_ge),
        after=(mx,))
    ops = [mm, c0, mx, oh]
    if scale_by_invsum:
        nm = p.op("vector", lambda v: v.tensor_scalar_mul(
            lg[:, NEc + 1:NEc + 2], lg[:, NEc:NEc + 1], -1.0), after=(mx,))
        ex = p.op("scalar", lambda sc: sc.activation(
            lg[:, 0:NEc], lg[:, 0:NEc], AF.Exp, bias=lg[:, NEc + 1:NEc + 2]),
            after=(nm, oh))
        sm = p.op("vector", lambda v: v.reduce_sum(
            lg[:, NEc:NEc + 1], lg[:, 0:NEc], AX.X), after=(ex,))
        rc = p.op("vector", lambda v: v.reciprocal(
            lg[:, NEc + 1:NEc + 2], lg[:, NEc:NEc + 1]), after=(sm,))
        sc2 = p.op("vector", lambda v: v.tensor_scalar(
            lg[:, NEc + 2:2 * NEc + 2], lg[:, NEc + 2:2 * NEc + 2],
            lg[:, NEc + 1:NEc + 2], None, ALU.mult), after=(rc,))
        ops += [nm, ex, sm, rc, sc2]
    tp = p.op("tensor", lambda te: te.transpose(
        p_tp[:, 0:128], lg[:, NEc + 2:2 * NEc + 2], ident_f32[:]),
        after=(ops[-1],))
    ev = p.op("vector", lambda v: v.tensor_copy(ohp_rows_dst, p_tp[0:NEc, 0:128]),
              after=(tp,))
    return ev


def emit_l3_moe(p, nc, ctx, T, scratch, wgate, bgate, W1T, b1col, W2T, b2col,
                x4_out, entry=()):
    qfm_d, kfm_d, vtm_d, xn_d, th_d, aacc_d = scratch
    NE = 8
    DFF = 4096
    TT = 512
    NS = T // TT
    NB = T // 128
    import ml_dtypes
    ones_r_h = nc.inline_tensor(np.ones((1, 128), np.float32), name=_nm("onr"))
    idf_h = nc.inline_tensor(np.eye(128, dtype=np.float32), name=_nm("idf"))

    moe_d = nc.dram_tensor(_nm("l3moe"), [D, T], F32, kind="Internal").ap()
    return moe_d


def emit_l3_moe_main(p, nc, ctx, T, xn_d, aacc_d, ohp_rows, W1T, b1col, W2T,
                     b2col, x4_out, entry=()):
    NE, DFF, TT = 8, 4096, 512
    NS = T // TT
    MH = DFF // 128
    import ml_dtypes
    ones_r_h = nc.inline_tensor(np.ones((1, 128), np.float32), name=_nm("onr"))
    moe_d = nc.dram_tensor(_nm("l3moe"), [D, T], F32, kind="Internal").ap()

    with ExitStack() as c1:
        w1_sb = SB(nc, c1, [128, PT * DFF], BF16, "w1e")
        w2_sb = SB(nc, c1, [128, MH * D], BF16, "w2e")
        b1_sb = SB(nc, c1, [128, MH], F32, "b1e")
        b2_sb = SB(nc, c1, [128, PT], F32, "b2e")
        xn_t = [[SB(nc, c1, [128, TT], BF16, "xnt") for _ in range(PT)] for _ in range(2)]
        h_sb = [SB(nc, c1, [128, TT], BF16, "hsb") for _ in range(MH)]
        y_sb = [SB(nc, c1, [128, TT], F32, "ysb") for _ in range(2)]
        ones_r = SB(nc, c1, [1, 128], F32, "orr")
        ohp_row = [SB(nc, c1, [1, TT], F32, "ohprow") for _ in range(2)]
        p_h = [PS(nc, c1, [128, TT], F32, "phh") for _ in range(2)]
        p_y = [PS(nc, c1, [128, TT], F32, "pyy") for _ in range(2)]
        p_bc = PS(nc, c1, [128, TT], F32, "pbc")

        lor = p.dma("sync", ones_r[:], ones_r_h.ap()[:, :], stream="cst", after=entry)

        last_xn_cons = [None, None]
        last_h_cons = {}
        last_y = [None, None]
        last_bc_cons = None
        last_st = None
        w1q_ops = {}
        w2q_ops = {}
        last_w1_reader = {}
        last_w2_reader = {}

        def issue_w1q(e2, q):
            if e2 >= NE or (e2, q) in w1q_ops:
                return
            war = last_w1_reader.get((e2 - 1, q))
            w1q_ops[(e2, q)] = p.op("gpsimd", lambda g, e2=e2, q=q: [
                g.dma_start(
                    out=w1_sb[:].rearrange("p (i o) -> p i o", o=DFF)[:, :, 1024 * q:1024 * (q + 1)],
                    in_=W1T[e2].rearrange("(i p) o -> p i o", p=128)[:, :, 1024 * q:1024 * (q + 1)])],
                units=1, dma=True, stream="wmoe",
                after=((war,) if war else tuple(entry)))

        def issue_w2q(e2, q):
            if e2 >= NE or (e2, q) in w2q_ops:
                return
            war = last_w2_reader.get((e2 - 1, q))
            w2q_ops[(e2, q)] = p.op("gpsimd", lambda g, e2=e2, q=q: [
                g.dma_start(
                    out=w2_sb[:].rearrange("p (i o) -> p i o", o=D)[:, :, 256 * q:256 * (q + 1)],
                    in_=W2T[e2].rearrange("(i p) o -> p i o", p=128)[:, :, 256 * q:256 * (q + 1)])],
                units=1, dma=True, stream="wmoe",
                after=((war,) if war else tuple(entry)))

        b1_ops = {}
        b2_ops = {}

        def issue_b1(e2, war=None):
            if e2 >= NE or e2 in b1_ops:
                return
            b1_ops[e2] = p.op("gpsimd", lambda g, e2=e2: [
                g.dma_start(out=b1_sb[:], in_=b1col[e2])],
                units=1, dma=True, stream="wmoe",
                after=((war,) if war else tuple(entry)))

        def issue_b2(e2, war=None):
            if e2 >= NE or e2 in b2_ops:
                return
            b2_ops[e2] = p.op("gpsimd", lambda g, e2=e2: [
                g.dma_start(out=b2_sb[:], in_=b2col[e2])],
                units=1, dma=True, stream="wmoe",
                after=((war,) if war else tuple(entry)))

        for q in range(4):
            issue_w1q(0, q)
        for q in range(4):
            issue_w2q(0, q)
        issue_b1(0)
        issue_b2(0)
        for e in range(NE):
            e_cons = []
            for s in range(NS):
                par = s % 2
                xl = p.op("sync", lambda eng, e=e, s=s, par=par: [
                    eng.dma_start(out=xn_t[par][i][:],
                                  in_=xn_d[128 * i:128 * (i + 1), TT * s:TT * (s + 1)])
                    for i in range(PT)], units=PT, dma=True, stream=f"mx{par}",
                    after=((last_xn_cons[par],) if last_xn_cons[par] else ()) + entry)
                h_evs = []
                for m in range(MH):
                    pb = p_h[m % 2]
                    mm = mm_chain(p, pb[:, 0:TT],
                                  [wtile(w1_sb, i, m, DFF) for i in range(PT)],
                                  [xn_t[par][i][:] for i in range(PT)],
                                  after=(xl, w1q_ops[(e, m // 8)],
                                         w1q_ops[(e, 0)])
                                  + ((h_evs[-2],) if len(h_evs) >= 2 else ())
                                  + ((last_h_cons.get(m),) if m in last_h_cons else ()))
                    ev = p.op("scalar", lambda sc, m=m, pb=pb: sc.activation(
                        h_sb[m][:, 0:TT], pb[:, 0:TT], AF.Relu, bias=b1_sb[:, m:m + 1]),
                        after=(mm, b1_ops[e]))
                    h_evs.append(ev)
                    if s == NS - 1 and m % 8 == 7:
                        last_w1_reader[(e, m // 8)] = mm
                        issue_w1q(e + 1, m // 8)
                    if s == NS - 1 and m == MH - 1:
                        issue_b1(e + 1, war=ev)
                last_xn_cons[par] = h_evs[-1]
                opar2 = s % 2
                ohl = p.op("gpsimd", lambda eng, e=e, s=s, opar2=opar2: [
                    eng.dma_start(out=ohp_row[opar2][0:1, :],
                                  in_=ohp_rows[e:e + 1, TT * s:TT * (s + 1)])],
                    units=1, dma=True, stream=f"ohl{opar2}",
                    after=((last_bc_cons,) if last_bc_cons else ()) + entry)
                bc = p.op("tensor", lambda te, e=e, s=s, opar2=opar2: te.matmul(
                    p_bc[:, 0:TT], ones_r[0:1, :], ohp_row[opar2][0:1, 0:TT],
                    start=True, stop=True),
                    after=(lor, ohl) + ((last_bc_cons,) if last_bc_cons else ()))
                for o in range(PT):
                    opar = o % 2
                    py = p_y[opar]
                    mm = mm_chain(p, py[:, 0:TT],
                                  [wtile(w2_sb, mi, o, D) for mi in range(MH)],
                                  [h_sb[mi][:, 0:TT] for mi in range(MH)],
                                  after=tuple(h_evs) + (w2q_ops[(e, o // 2)],)
                                  + ((last_y[opar],) if last_y[opar] else ()))
                    for mi in range(MH):
                        last_h_cons[mi] = mm
                    if s == NS - 1 and o % 2 == 1:
                        last_w2_reader[(e, o // 2)] = mm
                        issue_w2q(e + 1, o // 2)
                    bi = p.op("scalar", lambda sc, o=o, opar=opar, py=py: sc.activation(
                        y_sb[opar][:, 0:TT], py[:, 0:TT], AF.Identity,
                        bias=b2_sb[:, o:o + 1]), after=(mm, b2_ops[e])
                        + ((last_y[opar],) if last_y[opar] else ()))
                    if s == NS - 1 and o == PT - 1:
                        issue_b2(e + 1, war=bi)
                    tt = p.op("vector", lambda v, opar=opar: v.tensor_tensor(
                        y_sb[opar][:, 0:TT], y_sb[opar][:, 0:TT], p_bc[:, 0:TT],
                        ALU.mult), after=(bi, bc))
                    st = p.op("gpsimd", lambda g, e=e, o=o, s=s, opar=opar: [
                        g.dma_start(out=moe_d[128 * o:128 * (o + 1), TT * s:TT * (s + 1)],
                                    in_=y_sb[opar][:, 0:TT],
                                    **({} if e == 0 else {"accum_op": ALU.add}))],
                        units=1, dma=True, stream=f"macc{o % 4}", after=(tt,))
                    last_y[opar] = st
                    e_cons.append(st)
                last_bc_cons = e_cons[-1]
        a_t = [SB(nc, c1, [128, TT], F32, "at4") for _ in range(2)]
        m_t = [SB(nc, c1, [128, TT], F32, "mt4") for _ in range(2)]
        last_fin = [None, None]
        fins = []
        for s in range(NS):
            for o in range(PT):
                par = (s * PT + o) % 2
                ld = p.op("sync", lambda eng, s=s, o=o, par=par: [
                    eng.dma_start(out=a_t[par][:], in_=aacc_d[128 * o:128 * (o + 1), TT * s:TT * (s + 1)]),
                    eng.dma_start(out=m_t[par][:], in_=moe_d[128 * o:128 * (o + 1), TT * s:TT * (s + 1)])],
                    units=2, dma=True, stream=f"cmb{par}",
                    after=tuple(e_cons[-8:]) + ((last_fin[par],) if last_fin[par] else ()))
                ad = p.op("vector", lambda v, par=par: v.tensor_tensor(
                    a_t[par][:], a_t[par][:], m_t[par][:], ALU.add), after=(ld,))
                st = p.op("sync", lambda eng, s=s, o=o, par=par: [
                    eng.dma_start(out=x4_out[128 * o:128 * (o + 1), TT * s:TT * (s + 1)],
                                  in_=a_t[par][:])], units=1, dma=True,
                    stream="x4o", after=(ad,))
                last_fin[par] = st
                fins.append(st)
        return (fins[-1],)


def build_L3(T=4096):
    nc = bass.Bass("TRN2", target_bir_lowering=False)
    x3 = nc.dram_tensor("x3", [D, T], F32, kind="ExternalInput").ap()
    wqkvT = nc.dram_tensor("wqkvT", [D, 3 * D], F32, kind="ExternalInput").ap()
    wprojT = nc.dram_tensor("wprojT", [D, D], F32, kind="ExternalInput").ap()
    pbcol = nc.dram_tensor("pbcol", [128, PT], F32, kind="ExternalInput").ap()
    relbias = nc.dram_tensor("relbias", [32, 16], F32, kind="ExternalInput").ap()
    wgate = nc.dram_tensor("wgate", [D, 8], F32, kind="ExternalInput").ap()
    bgate = nc.dram_tensor("bgate", [1, 8], F32, kind="ExternalInput").ap()
    W1T = nc.dram_tensor("W1T", [8, D, 4096], F32, kind="ExternalInput").ap()
    b1col = nc.dram_tensor("b1col", [8, 128, 32], F32, kind="ExternalInput").ap()
    W2T = nc.dram_tensor("W2T", [8, 4096, D], F32, kind="ExternalInput").ap()
    b2col = nc.dram_tensor("b2col", [8, 128, PT], F32, kind="ExternalInput").ap()
    x4 = nc.dram_tensor("x4", [D, T], F32, kind="ExternalOutput").ap()

    with ExitStack() as ctx:
        p = Prog(nc, ctx)
        ohp_rows = SB(nc, ctx, [8, T], BF16, "ohp")
        ph1, scratch = emit_l3_phase1(p, nc, ctx, T, x3, wqkvT, relbias,
                                      wgate, bgate, ohp_rows)
        ph2 = emit_l3_attn_heads(p, nc, ctx, T, scratch, wprojT, pbcol, x3,
                                 entry=tuple(ph1))
        xn_d = scratch[3]
        aacc_d = scratch[5]
        fins = emit_l3_moe_main(p, nc, ctx, T, xn_d, aacc_d, ohp_rows,
                                W1T, b1col, W2T, b2col, x4, entry=tuple(ph2))
        p.op("gpsimd", lambda g: g.nop(), after=tuple(fins))
        p.emit()
    return nc


def emit_l1(p, nc, ctx, T, x1, WqT, WkT, WvT, WgT, bg, WdlT, Lpar_row, WoT,
            lngcol, lnbcol, rwT, rb_row, w1T, b1col, w2T, b2col, x2_out, entry=()):
    H, S = 8, 128
    LV = 12
    TT = 512
    NS = T // TT
    NB = T // 128
    import ml_dtypes
    ones_c_h = nc.inline_tensor(np.ones((128, 1), ml_dtypes.bfloat16), name=_nm("onc"))
    ones_r_h = nc.inline_tensor(np.ones((1, 128), np.float32), name=_nm("onr"))
    idf_h = nc.inline_tensor(np.eye(128, dtype=np.float32), name=_nm("idf"))
    mask_np = np.where(np.arange(128)[None, :] >= np.arange(128)[:, None],
                       0.0, -1e30).astype(np.float32)
    mask_h = nc.inline_tensor(mask_np, name=_nm("msk"))

    xn_d = nc.dram_tensor(_nm("l1xn"), [D, T], BF16, kind="Internal").ap()
    qfm_d = nc.dram_tensor(_nm("l1q"), [D, T], BF16, kind="Internal").ap()
    kfm_d = nc.dram_tensor(_nm("l1k"), [D, T], BF16, kind="Internal").ap()
    vtm_d = nc.dram_tensor(_nm("l1v"), [T, D], BF16, kind="Internal").ap()
    aacc_d = nc.dram_tensor(_nm("l1aa"), [D, T], F32, kind="Internal").ap()
    lam_d = nc.dram_tensor(_nm("l1lam"), [96, T], F32, kind="Internal").ap()
    g_d = nc.dram_tensor(_nm("l1g"), [H, T], F32, kind="Internal").ap()

    ohp_rows = SB(nc, ctx, [32, T], BF16, "ohp1")

    with ExitStack() as c1:
        qw = SB(nc, c1, [128, PT * D], BF16, "qw")
        kw = SB(nc, c1, [128, PT * D], BF16, "kw")
        vw = SB(nc, c1, [128, PT * D], BF16, "vw")
        gw = SB(nc, c1, [128, PT * 8], BF16, "gw")
        dlw = SB(nc, c1, [128, PT * 96], BF16, "dlw")
        rw = SB(nc, c1, [128, PT * 32], F32, "rw")
        bg_sb = SB(nc, c1, [1, 8], F32, "bgs")
        lp_sb = SB(nc, c1, [1, 96], F32, "lps")
        rb_sb = SB(nc, c1, [1, 32], F32, "rbs")
        x_f = [[SB(nc, c1, [128, TT], F32, "x") for _ in range(PT)]]
        x_f.append(x_f[0])
        xn_bf = [[SB(nc, c1, [128, TT], BF16, "xn") for _ in range(PT)] for _ in range(2)]
        xn32 = [SB(nc, c1, [128, TT], F32, "xn32") for _ in range(PT)]
        ones_c = SB(nc, c1, [128, 1], BF16, "oc")
        ones_r = SB(nc, c1, [1, 128], F32, "orr")
        idf_sb = SB(nc, c1, [128, 128], F32, "idf")
        onesf = SB(nc, c1, [128, 1], F32, "onesf")
        gz = SB(nc, c1, [8, T], F32, "gz")
        g2_pm = SB(nc, c1, [8, T], F32, "g2pm")
        ev_sb = [SB(nc, c1, [128, TT], BF16, "evb") for _ in range(2)]
        gl_tm = SB(nc, c1, [128, 8], F32, "gltm")
        dl_tm = SB(nc, c1, [128, 96], F32, "dltm")
        sp_sc = SB(nc, c1, [128, 96], F32, "spsc")
        g_pm = SB(nc, c1, [8, T], F32, "gpm")
        lam_pm = SB(nc, c1, [96, T], F32, "lampm")
        lnbufs = dict(
            stat=SB(nc, c1, [1, 3 * TT], F32, "st"),
            xbf=[SB(nc, c1, [128, TT], BF16, "xc")],
            sq=SB(nc, c1, [128, TT], BF16, "sqq"),
            nsc=[SB(nc, c1, [128, TT], F32, "nsc")],
        )
        rbufs = dict(lg=SB(nc, c1, [128, 66], F32, "lg"))
        lnbufs["p_sum1"] = PS(nc, c1, [1, TT], F32, "ps1")
        lnbufs["p_sum2"] = PS(nc, c1, [1, TT], F32, "ps2")
        lnbufs["p_bc1"] = PS(nc, c1, [128, TT], F32, "pb1")
        lnbufs["p_bc2"] = PS(nc, c1, [128, TT], F32, "pb2")
        p_mm = [PS(nc, c1, [128, TT], F32, "pmm") for _ in range(2)]
        p_sm = PS(nc, c1, [128, 128], F32, "psm")
        rbufs["p_lg"] = p_sm[:, 0:32]
        rbufs["p_tp"] = PS(nc, c1, [32, 128], F32, "ptp")

        lw = p.op("gpsimd", lambda g: [
            g.dma_start(out=qw[:].rearrange("p (i o) -> p i o", o=D),
                        in_=WqT.rearrange("(i p) o -> p i o", p=128)),
            g.dma_start(out=kw[:].rearrange("p (i o) -> p i o", o=D),
                        in_=WkT.rearrange("(i p) o -> p i o", p=128)),
            g.dma_start(out=vw[:].rearrange("p (i o) -> p i o", o=D),
                        in_=WvT.rearrange("(i p) o -> p i o", p=128)),
            g.dma_start(out=gw[:].rearrange("p (i o) -> p i o", o=8),
                        in_=WgT.rearrange("(i p) o -> p i o", p=128)),
            g.dma_start(out=dlw[:].rearrange("p (i o) -> p i o", o=96),
                        in_=WdlT.rearrange("(i p) o -> p i o", p=128))],
            units=5, dma=True, stream="wld", after=entry)
        lw2 = p.op("sync", lambda e: [
            e.dma_start(out=rw[:].rearrange("p (i o) -> p i o", o=32),
                        in_=rwT.rearrange("(i p) o -> p i o", p=128)),
            e.dma_start(out=bg_sb[:], in_=bg[:, :]),
            e.dma_start(out=lp_sb[:], in_=Lpar_row[:, :]),
            e.dma_start(out=rb_sb[:], in_=rb_row[:, :]),
            e.dma_start(out=ones_c[:], in_=ones_c_h.ap()[:, :]),
            e.dma_start(out=ones_r[:], in_=ones_r_h.ap()[:, :]),
            e.dma_start(out=idf_sb[:], in_=idf_h.ap()[:, :])],
            units=7, dma=True, stream="cst", after=entry)
        oz = p.op("vector", lambda v: v.memset(onesf[:], 1.0), after=entry)
        oz2 = p.op("vector", lambda v: v.memset(gz[0:8, :], 0.0), after=entry)

        prev_ln = []
        prev_rt = []
        prev_x_cons = [None, None]
        prev_xn_cons = [(), ()]
        evs = []
        gl_ops = []
        prev_small = []
        for s in range(NS):
            par = s % 2
            xl = p.op("sync", lambda e, s=s, par=par: [
                e.dma_start(out=x_f[par][i][:], in_=x1[128 * i:128 * (i + 1), TT * s:TT * (s + 1)])
                for i in range(PT)], units=PT, dma=True, stream="ax",
                after=tuple(c for c in prev_x_cons if c) + entry)
            xn_view = [xn_bf[par][i][:] for i in range(PT)]
            louts = emit_ln(p, nc, ctx, [x_f[par][i] for i in range(PT)], xn_view, TT,
                            ones_c, ones_r, lnbufs, x_ready=(xl, lw2) + tuple(prev_ln)
                            + tuple(prev_xn_cons[par]),
                            xn32_tiles=xn32)
            prev_ln = list(louts)
            prev_x_cons[par] = louts[-1]
            xnst = p.op("sync", lambda e, s=s, par=par: [
                e.dma_start(out=xn_d[128 * i:128 * (i + 1), TT * s:TT * (s + 1)],
                            in_=xn_bf[par][i][:]) for i in range(PT)],
                units=PT, dma=True, stream="xnd", after=tuple(louts))
            for bb in range(4):
                rv = emit_router_tm(
                    p, nc, ctx, 32, 128,
                    [xn32[i][:, 128 * bb:128 * bb + 128] for i in range(PT)],
                    rw, rb_sb, ones_r,
                    ohp_rows[0:32, TT * s + 128 * bb: TT * s + 128 * bb + 128],
                    rbufs, deps=tuple(louts) + (lw2,) + tuple(prev_rt) + tuple(prev_small),
                    scale_by_invsum=False, ident_f32=idf_sb)
                prev_rt = [rv]
                prev_ln.append(rv)
            for (wsb, dst, scale) in [(qw, qfm_d, S ** -0.5), (kw, kfm_d, 1.0)]:
                for o in range(PT):
                    pb = p_mm[len(evs) % 2]
                    mm = mm_chain(p, pb[:, 0:TT],
                                  [wtile(wsb, i, o, D) for i in range(PT)],
                                  [xn_bf[par][i][:] for i in range(PT)],
                                  after=tuple(louts) + (lw,) + tuple(prev_small)
                                  + ((evs[-2],) if len(evs) >= 2 else ()))
                    epar = len(evs) % 2
                    ev = p.op("scalar", lambda sc, pb=pb, epar=epar, scale=scale: sc.activation(
                        ev_sb[epar][:, 0:TT], pb[:, 0:TT], AF.Identity, scale=scale),
                        after=(mm,) + ((evs[-2],) if len(evs) >= 2 else ()))
                    st = p.op("sync", lambda e, o=o, s=s, dst=dst, epar=epar: [
                        e.dma_start(out=dst[128 * o:128 * (o + 1), TT * s:TT * (s + 1)],
                                    in_=ev_sb[epar][:, 0:TT])], units=1, dma=True,
                        stream="qkout", after=(ev,))
                    evs.append(st)
            for bb in range(4):
                for half in range(2):
                    pb = p_mm[len(evs) % 2]
                    def vmm(te, bb=bb, s=s, half=half, pb=pb, par=par):
                        ins = None
                        for i in range(PT):
                            ins = te.matmul(
                                pb[:, 0:TT],
                                xn_bf[par][i][:, 128 * bb: 128 * bb + 128],
                                vw[:, i * D + 512 * half: i * D + 512 * half + 512],
                                start=(i == 0), stop=(i == PT - 1))
                        return ins
                    mm = p.op("tensor", vmm, after=tuple(louts) + (lw,)
                              + ((evs[-2],) if len(evs) >= 2 else ()))
                    epar = len(evs) % 2
                    ev = p.op("scalar", lambda sc, pb=pb, epar=epar: sc.activation(
                        ev_sb[epar][:, 0:TT], pb[:, 0:TT], AF.Identity),
                        after=(mm,) + ((evs[-2],) if len(evs) >= 2 else ()))
                    st = p.op("sync", lambda e, bb=bb, s=s, half=half, epar=epar: [
                        e.dma_start(out=vtm_d[TT * s + 128 * bb:TT * s + 128 * bb + 128,
                                              512 * half:512 * half + 512],
                                    in_=ev_sb[epar][:, 0:TT])], units=1, dma=True,
                        stream="qkout", after=(ev,))
                    evs.append(st)
            for bb in range(4):
                def gmm(te, s=s, bb=bb, par=par):
                    ins = None
                    for i in range(PT):
                        ins = te.matmul(p_sm[:, 32:40],
                                        xn_bf[par][i][:, 128 * bb:128 * bb + 128],
                                        gw[:, i * 8:(i + 1) * 8], start=(i == 0), stop=False)
                    return te.matmul(p_sm[:, 32:40], ones_r[0:1, :], bg_sb[0:1, :],
                                     start=False, stop=True)
                g1 = p.op("tensor", gmm, after=tuple(louts) + (lw, lw2)
                          + tuple(prev_small) + tuple(prev_rt))
                def dmm(te, s=s, bb=bb, par=par):
                    ins = None
                    for i in range(PT):
                        ins = te.matmul(p_mm[0][:, 0:96],
                                        xn_bf[par][i][:, 128 * bb:128 * bb + 128],
                                        dlw[:, i * 96:(i + 1) * 96], start=(i == 0), stop=False)
                    return te.matmul(p_mm[0][:, 0:96], ones_r[0:1, :], lp_sb[0:1, :],
                                     start=False, stop=True)
                d1 = p.op("tensor", dmm, after=tuple(louts) + (lw, lw2)
                          + ((evs[-2],) if len(evs) >= 2 else ()) + tuple(prev_small))
                sp1 = p.op("scalar", lambda sc: sc.activation(
                    gl_tm[:, 0:8], p_sm[:, 32:40], AF.Abs), after=(g1,))
                sp2 = p.op("scalar", lambda sc: sc.activation(
                    gl_tm[:, 0:8], gl_tm[:, 0:8], AF.Exp, scale=-1.0), after=(sp1,))
                sp3 = p.op("scalar", lambda sc: sc.activation(
                    gl_tm[:, 0:8], gl_tm[:, 0:8], AF.Ln, bias=onesf[:, 0:1]),
                    after=(sp2, oz))
                sp4 = p.op("vector", lambda v: v.tensor_scalar_max(
                    sp_sc[:, 0:8], p_sm[:, 32:40], 0.0), after=(g1, sp1))
                sp5 = p.op("vector", lambda v: v.tensor_tensor(
                    gl_tm[:, 0:8], gl_tm[:, 0:8], sp_sc[:, 0:8], ALU.add), after=(sp3, sp4))
                sp6 = p.op("vector", lambda v: v.tensor_scalar_mul(
                    gl_tm[:, 0:8], gl_tm[:, 0:8], -1.0), after=(sp5,))
                l1o = p.op("scalar", lambda sc: sc.activation(
                    dl_tm[:, 0:96], p_mm[0][:, 0:96], AF.Abs), after=(d1,))
                l2o = p.op("scalar", lambda sc: sc.activation(
                    dl_tm[:, 0:96], dl_tm[:, 0:96], AF.Exp, scale=-1.0), after=(l1o,))
                l3o = p.op("scalar", lambda sc: sc.activation(
                    dl_tm[:, 0:96], dl_tm[:, 0:96], AF.Ln, bias=onesf[:, 0:1]),
                    after=(l2o, oz))
                l4o = p.op("vector", lambda v: v.tensor_scalar_max(
                    sp_sc[:, 0:96], p_mm[0][:, 0:96], 0.0), after=(d1, l1o))
                l5o = p.op("vector", lambda v: v.tensor_tensor(
                    dl_tm[:, 0:96], dl_tm[:, 0:96], sp_sc[:, 0:96], ALU.add),
                    after=(l3o, l4o))
                l6o = p.op("scalar", lambda sc: sc.activation(
                    dl_tm[:, 0:96], dl_tm[:, 0:96], AF.Ln), after=(l5o,))
                tg = p.op("tensor", lambda te: te.transpose(
                    rbufs["p_tp"][0:8, 0:128],
                    gl_tm[:, 0:8], idf_sb[:]),
                    after=(sp6,) + tuple(prev_small) + tuple(prev_rt))
                ge = p.op("vector", lambda v, s=s, bb=bb: v.tensor_copy(
                    g_pm[:, TT * s + 128 * bb:TT * s + 128 * bb + 128],
                    rbufs["p_tp"][0:8, 0:128]), after=(tg,))
                tl = p.op("tensor", lambda te: te.transpose(
                    rbufs["p_tp"][0:32, 0:128], dl_tm[:, 0:32], idf_sb[:]),
                    after=(l6o, ge))
                le1 = p.op("vector", lambda v, s=s, bb=bb: v.tensor_copy(
                    lam_pm[0:32, TT * s + 128 * bb:TT * s + 128 * bb + 128],
                    rbufs["p_tp"][0:32, 0:128]), after=(tl,))
                tl2 = p.op("tensor", lambda te: te.transpose(
                    rbufs["p_tp"][0:32, 0:128], dl_tm[:, 32:64], idf_sb[:]),
                    after=(l6o, le1))
                le2 = p.op("vector", lambda v, s=s, bb=bb: v.tensor_copy(
                    lam_pm[32:64, TT * s + 128 * bb:TT * s + 128 * bb + 128],
                    rbufs["p_tp"][0:32, 0:128]), after=(tl2,))
                tl3 = p.op("tensor", lambda te: te.transpose(
                    rbufs["p_tp"][0:32, 0:128], dl_tm[:, 64:96], idf_sb[:]),
                    after=(l6o, le2))
                le3 = p.op("vector", lambda v, s=s, bb=bb: v.tensor_copy(
                    lam_pm[64:96, TT * s + 128 * bb:TT * s + 128 * bb + 128],
                    rbufs["p_tp"][0:32, 0:128]), after=(tl3,))
                prev_small = [sp6, le1, le2, le3, ge]
                gl_ops += [sp6, le3]
            prev_xn_cons[par] = (xnst, evs[-1], prev_small[-1])
        gcs = p.op("vector", lambda v: v.tensor_tensor_scan(
            g2_pm[0:8, :], gz[0:8, :], g_pm[0:8, :], 0.0, ALU.add, ALU.add),
            after=tuple(gl_ops) + (oz2,))
        gst = p.op("sync", lambda e: [e.dma_start(out=g_d[:, :], in_=g2_pm[0:8, :])],
                   units=1, dma=True, stream="gst", after=(gcs,))
        lst = p.op("sync", lambda e: [e.dma_start(out=lam_d[:, :], in_=lam_pm[:, :])],
                   units=1, dma=True, stream="gst", after=tuple(gl_ops))
        ph1 = (gst, lst, evs[-1], evs[-2], xnst)
    return ph1, (xn_d, qfm_d, kfm_d, vtm_d, aacc_d, lam_d, g_d), ohp_rows


def _l1_level_vectors(T):
    LV = 12
    d = np.arange(-(T - 1), T)
    lvl = np.where(d <= 0, 0, np.floor(np.log2(np.maximum(d, 1))).astype(np.int64) + 1)
    lvl = np.minimum(lvl, LV - 1)
    av = np.full((LV, 2 * T - 1), -1e30, np.float32)
    for l in range(LV):
        av[l, (lvl == l) & (d >= 0)] = 0.0
    return av


def emit_l1_attn_heads(p, nc, ctx, T, scratch, WoT, lngcol, lnbcol, x1,
                       aacc_d, entry=()):
    xn_d, qfm_d, kfm_d, vtm_d, _aacc, lam_d, g_d = scratch
    H, S, LV = 8, 128, 12
    TT = 512
    NS = T // TT
    NB = T // 128
    import ml_dtypes
    av_np = _l1_level_vectors(T)
    npad = ceil_div(2 * T - 1 + 128, 128) * 128
    import ml_dtypes as _mld
    av_pad = np.full((LV, npad), -1e30, np.float32)
    av_pad[:, :2 * T - 1] = av_np
    av_h = nc.inline_tensor(av_pad.reshape(-1).astype(_mld.bfloat16), name=_nm("l1av"))
    av_int = nc.dram_tensor(_nm("l1avi"), [LV * npad], BF16, kind="Internal")
    av_d = av_int.ap()
    prev_h = nc.inline_tensor(np.eye(128)[::-1].copy().astype(_mld.bfloat16),
                              name=_nm("prev"))
    def levels_for(doff):
        lo, hi = 128 * doff - 127, 128 * doff + 511
        out = []
        for l in range(LV):
            if l == 0:
                blo, bhi = 0, 0
            elif l < LV - 1:
                blo, bhi = 2 ** (l - 1), 2 ** l - 1
            else:
                blo, bhi = 2 ** (l - 1), 10 ** 9
            if bhi >= lo and blo <= hi:
                pure = blo <= max(lo, 0) and bhi >= hi
                out.append((l, pure))
        return out

    with ExitStack() as c2:
        ao_sb = [SB(nc, c2, [128, T], BF16, "ao1") for _ in range(PT)]
        p_sc = [PS(nc, c2, [128, TT], F32, "psc") for _ in range(2)]
        p_ar = [PS(nc, c2, [128, TT], F32, "par") for _ in range(2)]
        p_o = PS(nc, c2, [128, TT], F32, "po1")
        c2b = ctx.enter_context(ExitStack()) if False else None
        import contextlib
        c2b = c2.enter_context(contextlib.ExitStack())
        q_sb = SB(nc, c2b, [128, T], BF16, "qh1")
        k_sb = SB(nc, c2b, [128, T], BF16, "kh1")
        v_sb = SB(nc, c2b, [128, T], BF16, "vh1")
        ga_sb = SB(nc, c2b, [3, T], F32, "ga")
        garhs = SB(nc, c2b, [3, LV * TT], F32, "garhs")
        t2c = [SB(nc, c2b, [128, TT], BF16, "t2c") for _ in range(4)]
        prev_sb = SB(nc, c2b, [128, 128], BF16, "prevs")
        e_sb = [SB(nc, c2b, [128, TT], F32, "esb") for _ in range(4)]
        a_sb = [SB(nc, c2b, [128, TT], BF16, "asb") for _ in range(4)]
        oev = SB(nc, c2b, [128, TT], BF16, "oev1")

        neg1_h = nc.inline_tensor(np.full((1, LV * TT), -1.0, np.float32),
                                  name=_nm("neg1"))
        lc = p.op("sync", lambda e: [
            e.dma_start(out=prev_sb[:], in_=prev_h.ap()[:, :]),
            e.dma_start(out=garhs[1:2, :], in_=neg1_h.ap()[:, :]),
            e.dma_start(out=av_int.ap()[:], in_=av_h.ap()[:])],
            units=3, dma=True, stream="cst", after=entry)

        hdone = entry
        t2_reader = {}
        for h in range(H):
            ld = p.op("sync", lambda e, h=h: [
                e.dma_start(out=q_sb[:, :], in_=qfm_d[128 * h:128 * h + 128, :]),
                e.dma_start(out=k_sb[:, :], in_=kfm_d[128 * h:128 * h + 128, :]),
                e.dma_start(out=v_sb[:].rearrange("p (b c) -> p b c", c=128),
                            in_=vtm_d.rearrange("(b p) c -> p b c", p=128)[:, :, 128 * h:128 * h + 128])],
                units=3, dma=True, stream="hld", after=tuple(hdone) + entry)
            gz1 = p.op("vector", lambda v: v.memset(ga_sb[0:3, :], 1.0),
                       after=tuple(hdone) + entry)
            gz2 = gz1
            gld = p.op("gpsimd", lambda e, h=h: [
                e.dma_start(out=ga_sb[1:2, :], in_=g_d[h:h + 1, :])],
                units=1, dma=True, stream="gld", after=tuple(hdone) + (gz1,) + entry)
            a_ops = []
            ex_ops = []
            t2ops = []
            prev_garhs_cons = None
            fin = None
            for tc in range(NS):
                gz3 = lc
                grh = p.op("gpsimd", lambda e, h=h, tc=tc: [
                    e.dma_start(out=garhs[0:1, l * TT:(l + 1) * TT],
                                in_=g_d[h:h + 1, TT * tc:TT * (tc + 1)])
                    for l in range(LV)] + [
                    e.dma_start(out=garhs[2:3, l * TT:(l + 1) * TT],
                                in_=lam_d[12 * h + l:12 * h + l + 1, TT * tc:TT * (tc + 1)])
                    for l in range(LV)],
                    units=2 * LV, dma=True, stream="gld",
                    after=((prev_garhs_cons,) if prev_garhs_cons else tuple(hdone)))
                units = []
                for kk in range(4 * tc + 4):
                    doff = 4 * tc - kk
                    for (l, pure) in levels_for(doff):
                        units.append((kk, l, pure))
                NU = len(units)
                t2_of = {}
                mi_of = {}
                mcount = 0
                for ui, (kk, l, pure) in enumerate(units):
                    if not pure:
                        mi_of[ui] = mcount
                        mcount += 1

                def issue_t2(ui2):
                    kk2, l2, pure2 = units[ui2]
                    if pure2 or ui2 in t2_of:
                        return
                    mi = mi_of[ui2]
                    slot = mi % 4
                    doff2 = 4 * tc - kk2
                    c0 = 128 * doff2 + (T - 1) - 127
                    war = t2_reader.get(slot)
                    t2 = p.op("gpsimd", lambda e, l2=l2, c0=c0, slot=slot: [
                        e.dma_start(out=t2c[slot][:],
                                    in_=bass.AP(av_d.tensor, l2 * npad + c0,
                                                [[1, 128], [1, TT]]))],
                        units=1, dma=True, stream=f"t2{slot % 2}",
                        after=((war,) if war else ()) + (lc,) + tuple(t2ops[-2:]))
                    t2ops.append(t2)
                    t2_of[ui2] = (t2, slot)

                smm_bykk = {}
                oacc = []
                upars = {}
                at_tc = []
                for ui, (kk, l, pure) in enumerate(units):
                    for ahead in range(3):
                        if ui + ahead < NU:
                            issue_t2(ui + ahead)
                    if kk not in smm_bykk:
                        psc = p_sc[len(smm_bykk) % 2]
                        smm = p.op("tensor", lambda te, kk=kk, tc=tc, psc=psc: te.matmul(
                            psc[:, 0:TT], k_sb[:, 128 * kk:128 * kk + 128],
                            q_sb[:, TT * tc:TT * (tc + 1)], start=True, stop=True),
                            after=(ld,) + tuple(a_ops[-4:]))
                        smm_bykk[kk] = (smm, psc)
                    smm, psc = smm_bykk[kk]
                    par = len(ex_ops) % 4
                    upars[ui] = par
                    pa = p_ar[len(ex_ops) % 2]
                    t2slot = t2_of[ui][1] if not pure else None
                    def argmm(te, l=l, kk=kk, tc=tc, pure=pure, par=par, pa=pa,
                              t2slot=t2slot):
                        ins = None
                        if not pure:
                            ins = te.matmul(pa[:, 0:TT], prev_sb[:],
                                            t2c[t2slot][:],
                                            start=True, stop=False)
                        ins = te.matmul(pa[:, 0:TT], ga_sb[:, 128 * kk:128 * kk + 128],
                                        garhs[:, l * TT:(l + 1) * TT],
                                        start=pure, stop=True)
                        return ins
                    t2dep = (t2_of[ui][0],) if not pure else ()
                    am = p.op("tensor", argmm,
                              after=(gld, grh, gz1, gz2, gz3, lc) + t2dep
                              + ((ex_ops[-2],) if len(ex_ops) >= 2 else ()))
                    if not pure:
                        t2_reader[t2_of[ui][1]] = am
                    ex = p.op("scalar", lambda sc, par=par, pa=pa: sc.activation(
                        e_sb[par][:, 0:TT], pa[:, 0:TT], AF.Exp),
                        after=(am,) + ((a_ops[-4],) if len(a_ops) >= 4 else ()))
                    ex_ops.append(ex)
                    at = p.op("vector", lambda v, par=par, psc=psc: v.tensor_tensor(
                        a_sb[par][:, 0:TT], psc[:, 0:TT], e_sb[par][:, 0:TT],
                        ALU.mult), after=(ex, smm)
                        + ((a_ops[-4],) if len(a_ops) >= 4 else ())
                        + ((oacc[ui - 3],) if ui >= 3 else ()))
                    a_ops.append(at)
                    at_tc.append(at)
                    for uo in ([ui - 2] if ui >= 2 else []) + ([NU - 2, NU - 1] if ui == NU - 1 else []):
                        kko = units[uo][0]
                        om = p.op("tensor", lambda te, kko=kko, uo=uo, NU=NU, sl=upars[uo]: te.matmul(
                            p_o[:, 0:TT], v_sb[:, 128 * kko:128 * kko + 128],
                            a_sb[sl][:, 0:TT], start=(uo == 0), stop=(uo == NU - 1)),
                            after=(at_tc[uo], ld) + (((fin,)) if (uo == 0 and fin) else ()))
                        oacc.append(om)
                oe = p.op("vector", lambda v, tc=tc: v.tensor_copy(
                    oev[:, 0:TT], p_o[:, 0:TT]), after=(oacc[-1],))
                fin = p.op("vector", lambda v, h=h, tc=tc: v.tensor_copy(
                    ao_sb[h][:, TT * tc:TT * (tc + 1)], oev[:, 0:TT]), after=(oe,))
                prev_garhs_cons = am
            hdone = (fin, ex_ops[-1], ex_ops[-2] if len(ex_ops) >= 2 else ex_ops[-1])
        c2b.close()
        with ExitStack() as c3:
            import ml_dtypes as mld
            ones_c2_h = nc.inline_tensor(np.ones((128, 1), mld.bfloat16), name=_nm("onc"))
            ones_r2_h = nc.inline_tensor(np.ones((1, 128), np.float32), name=_nm("onr"))
            wow = SB(nc, c3, [128, PT * D], BF16, "wow")
            lng = SB(nc, c3, [128, PT], F32, "lng")
            lnb = SB(nc, c3, [128, PT], F32, "lnb")
            y2 = [SB(nc, c3, [128, TT], F32, "y2") for _ in range(PT)]
            y2n = [SB(nc, c3, [128, TT], F32, "y2n") for _ in range(PT)]
            xn_t = [SB(nc, c3, [128, TT], BF16, "xnt") for _ in range(PT)]
            x1_t = [SB(nc, c3, [128, TT], F32, "x1t") for _ in range(2)]
            ones_c2 = SB(nc, c3, [128, 1], BF16, "oc2")
            ones_r2 = SB(nc, c3, [1, 128], F32, "or2")
            lnbufs2 = dict(
                stat=SB(nc, c3, [1, 3 * TT], F32, "st2"),
                xbf=[SB(nc, c3, [128, TT], BF16, "xc2")],
                sq=SB(nc, c3, [128, TT], BF16, "sq2"),
                nsc=[SB(nc, c3, [128, TT], F32, "ns2")],
            )
            lnbufs2["p_sum1"] = PS(nc, c3, [1, TT], F32, "ps1b")
            lnbufs2["p_sum2"] = PS(nc, c3, [1, TT], F32, "ps2b")
            lnbufs2["p_bc1"] = p_sc[0]
            lnbufs2["p_bc2"] = p_sc[1]
            p_pr = p_ar
            lw2 = load_weight_bf16(p, wow, WoT, D, D, after=tuple(hdone))
            lcs = p.op("sync", lambda e: [
                e.dma_start(out=lng[:], in_=lngcol[:, :]),
                e.dma_start(out=lnb[:], in_=lnbcol[:, :]),
                e.dma_start(out=ones_c2[:], in_=ones_c2_h.ap()[:, :]),
                e.dma_start(out=ones_r2[:], in_=ones_r2_h.ap()[:, :])],
                units=4, dma=True, stream="cst", after=tuple(hdone))
            prev_ln2 = list(hdone)
            prev_st = None
            for s in range(NS):
                xld = p.op("sync", lambda e, s=s: [
                    e.dma_start(out=xn_t[i][:], in_=xn_d[128 * i:128 * (i + 1), TT * s:TT * (s + 1)])
                    for i in range(PT)], units=PT, dma=True, stream="pl",
                    after=tuple(prev_ln2))
                mms = []
                for o in range(PT):
                    ppr = p_pr[o % 2]
                    mm = mm_chain(p, ppr[:, 0:TT],
                                  [wtile(wow, i, o, D) for i in range(PT)],
                                  [ao_sb[i][:, TT * s:TT * (s + 1)] for i in range(PT)],
                                  after=(lw2, xld) + tuple(prev_ln2)
                                  + ((mms[-2],) if len(mms) >= 2 else ()))
                    ad = p.op("vector", lambda v, o=o, ppr=ppr: v.tensor_tensor(
                        y2[o][:, 0:TT], ppr[:, 0:TT], xn_t[o][:], ALU.add),
                        after=(mm, xld))
                    mms.append(ad)
                louts = emit_ln(p, nc, ctx, y2, y2n, TT, ones_c2, ones_r2, lnbufs2,
                                x_ready=tuple(mms) + (lcs,))
                sts = []
                for o in range(PT):
                    xpar = o % 2
                    x1l = p.op("gpsimd", lambda e, s=s, o=o, xpar=xpar: [
                        e.dma_start(out=x1_t[xpar][:],
                                    in_=x1[128 * o:128 * (o + 1), TT * s:TT * (s + 1)])],
                        units=1, dma=True, stream=f"x1l{xpar}",
                        after=((sts[-2],) if len(sts) >= 2 else tuple(prev_ln2)))
                    gb = p.op("vector", lambda v, o=o: v.tensor_scalar(
                        y2n[o][:, 0:TT], y2n[o][:, 0:TT], lng[:, o:o + 1],
                        lnb[:, o:o + 1], ALU.mult, ALU.add), after=tuple(louts))
                    ax = p.op("vector", lambda v, o=o, xpar=xpar: v.tensor_tensor(
                        y2n[o][:, 0:TT], y2n[o][:, 0:TT], x1_t[xpar][:], ALU.add),
                        after=(gb, x1l))
                    st = p.op("sync", lambda e, o=o, s=s: [
                        e.dma_start(out=aacc_d[128 * o:128 * (o + 1), TT * s:TT * (s + 1)],
                                    in_=y2n[o][:, 0:TT])], units=1, dma=True,
                        stream="aout", after=(ax,))
                    sts.append(st)
                prev_ln2 = sts
                prev_st = st
            return (prev_st,)


def emit_l1_moe(p, nc, ctx, T, xn_d, aacc_d, ohp_rows, w1T, b1col, w2T, b2col,
                x2_out, entry=()):
    NE, HH, TT = 32, 256, 512
    NS = T // TT
    MH = HH // 128
    import ml_dtypes
    ones_r_h = nc.inline_tensor(np.ones((1, 128), np.float32), name=_nm("onr"))
    moe_d = nc.dram_tensor(_nm("l1moe"), [D, T], F32, kind="Internal").ap()

    with ExitStack() as c1:
        w1_sb = [SB(nc, c1, [128, PT * HH], BF16, "w1e") for _ in range(2)]
        w2_sb = [SB(nc, c1, [128, MH * D], BF16, "w2e") for _ in range(2)]
        b1_sb = [SB(nc, c1, [128, MH], F32, "b1e") for _ in range(2)]
        b2_sb = [SB(nc, c1, [128, PT], F32, "b2e") for _ in range(2)]
        xn_t = [SB(nc, c1, [128, T], BF16, "xnt") for _ in range(PT)]
        h_sb = [SB(nc, c1, [128, TT], BF16, "hsb") for _ in range(MH)]
        y_sb = [SB(nc, c1, [128, TT], BF16, "ysb") for _ in range(2)]
        bc_sb = SB(nc, c1, [128, TT], BF16, "bcs")
        ohp_row = SB(nc, c1, [1, T], F32, "ohpr")
        a_t = [SB(nc, c1, [128, TT], F32, "at2") for _ in range(2)]
        o_t = [SB(nc, c1, [128, TT], F32, "ot2") for _ in range(2)]
        ones_r = SB(nc, c1, [1, 128], F32, "orr")
        p_h = [PS(nc, c1, [128, TT], F32, "phh") for _ in range(2)]
        p_y = [PS(nc, c1, [128, TT], F32, "pyy") for _ in range(2)]
        p_bc = PS(nc, c1, [128, TT], F32, "pbc")

        lor = p.dma("sync", ones_r[:], ones_r_h.ap()[:, :], stream="cst", after=entry)
        xlall = p.op("sync", lambda e: [
            e.dma_start(out=xn_t[i][:], in_=xn_d[128 * i:128 * (i + 1), :])
            for i in range(PT)], units=PT, dma=True, stream="mx", after=entry)

        last_st = {}
        prev_bc = None
        last_tt = None
        prev_y = [None, None]
        lws = {}
        for e in range(NE):
            wpar = e % 2
            lw = p.op("gpsimd", lambda g, e=e, wpar=wpar: [
                g.dma_start(out=w1_sb[wpar][:].rearrange("p (i o) -> p i o", o=HH),
                            in_=w1T[e].rearrange("(i p) o -> p i o", p=128)),
                g.dma_start(out=w2_sb[wpar][:].rearrange("p (i o) -> p i o", o=D),
                            in_=w2T[e].rearrange("(i p) o -> p i o", p=128)),
                g.dma_start(out=b1_sb[wpar][:], in_=b1col[e]),
                g.dma_start(out=b2_sb[wpar][:], in_=b2col[e])],
                units=4, dma=True, stream=f"wmoe{wpar}",
                after=(lws[e - 2] if e - 2 in lws else tuple(entry)) if False else
                (tuple(lws[e - 2]) if e - 2 in lws else tuple(entry)))
            ohl = p.op("gpsimd", lambda eng, e=e: [
                eng.dma_start(out=ohp_row[0:1, :], in_=ohp_rows[e:e + 1, :])],
                units=1, dma=True, stream="ohl", after=((prev_bc,) if prev_bc else entry))
            e_read_w = []
            for s in range(NS):
                bc = p.op("tensor", lambda te, s=s: te.matmul(
                    p_bc[:, 0:TT], ones_r[0:1, :], ohp_row[0:1, TT * s:TT * (s + 1)],
                    start=True, stop=True),
                    after=(lor, ohl) + ((prev_bc,) if prev_bc else ()))
                bce = p.op("vector", lambda v: v.tensor_copy(bc_sb[:, 0:TT], p_bc[:, 0:TT]),
                           after=(bc,) + ((last_tt,) if last_tt else ()))
                prev_bc = bce
                h_evs = []
                for m in range(MH):
                    pb = p_h[m % 2]
                    mm = mm_chain(p, pb[:, 0:TT],
                                  [wtile(w1_sb[wpar], i, m, HH) for i in range(PT)],
                                  [xn_t[i][:, TT * s:TT * (s + 1)] for i in range(PT)],
                                  after=(xlall, lw)
                                  + ((h_evs[-2],) if len(h_evs) >= 2 else ())
                                  + ((last_st.get((s, 0)),) if (s, 0) in last_st else ()))
                    ev = p.op("scalar", lambda sc, m=m, pb=pb, wpar=wpar: sc.activation(
                        h_sb[m][:, 0:TT], pb[:, 0:TT], AF.Relu,
                        bias=b1_sb[wpar][:, m:m + 1]), after=(mm,))
                    h_evs.append(ev)
                    e_read_w.append(mm)
                    e_read_w.append(ev)
                for o in range(PT):
                    opar = o % 2
                    py = p_y[opar]
                    mm = mm_chain(p, py[:, 0:TT],
                                  [wtile(w2_sb[wpar], mi, o, D) for mi in range(MH)],
                                  [h_sb[mi][:, 0:TT] for mi in range(MH)],
                                  after=tuple(h_evs) + (lw,)
                                  + ((prev_y[opar],) if prev_y[opar] else ()))
                    e_read_w.append(mm)
                    bi = p.op("scalar", lambda sc, o=o, opar=opar, py=py, wpar=wpar: sc.activation(
                        y_sb[opar][:, 0:TT], py[:, 0:TT], AF.Relu,
                        bias=b2_sb[wpar][:, o:o + 1]), after=(mm,)
                        + ((prev_y[opar],) if prev_y[opar] else ()))
                    e_read_w.append(bi)
                    tt = p.op("vector", lambda v, opar=opar: v.tensor_tensor(
                        y_sb[opar][:, 0:TT], y_sb[opar][:, 0:TT], bc_sb[:, 0:TT],
                        ALU.mult), after=(bi, bce))
                    last_tt = tt
                    ac = p.op("gpsimd", lambda g, e=e, o=o, s=s, opar=opar: [
                        g.dma_start(out=moe_d[128 * o:128 * (o + 1), TT * s:TT * (s + 1)],
                                    in_=y_sb[opar][:, 0:TT],
                                    **({} if e == 0 else {"accum_op": ALU.add}))],
                        units=1, dma=True, stream=f"macc{o % 4}", after=(tt,))
                    prev_y[opar] = ac
                    last_st[(s, o)] = ac
            lws[e] = tuple(e_read_w[-2:])
        fins = []
        last_fin = [None, None]
        for s in range(NS):
            for o in range(PT):
                par = (s * PT + o) % 2
                ld = p.op("sync", lambda eng, s=s, o=o, par=par: [
                    eng.dma_start(out=a_t[par][:],
                                  in_=aacc_d[128 * o:128 * (o + 1), TT * s:TT * (s + 1)]),
                    eng.dma_start(out=o_t[par][:],
                                  in_=moe_d[128 * o:128 * (o + 1), TT * s:TT * (s + 1)])],
                    units=2, dma=True, stream=f"cmb{par}",
                    after=((last_fin[par],) if last_fin[par] else ())
                    + (last_st[(s, o)],))
                ad = p.op("vector", lambda v, s=s, o=o, par=par: v.tensor_tensor(
                    o_t[par][:], a_t[par][:], o_t[par][:], ALU.add), after=(ld,))
                st = p.op("sync", lambda eng, s=s, o=o, par=par: [
                    eng.dma_start(out=x2_out[128 * o:128 * (o + 1), TT * s:TT * (s + 1)],
                                  in_=o_t[par][:])], units=1, dma=True,
                    stream="x2o", after=(ad,))
                last_fin[par] = st
                fins.append(st)
        return (fins[-1],)


def build_L1(T=4096):
    nc = bass.Bass("TRN2", target_bir_lowering=False)
    x1 = nc.dram_tensor("x1", [D, T], F32, kind="ExternalInput").ap()
    WqT = nc.dram_tensor("WqT", [D, D], F32, kind="ExternalInput").ap()
    WkT = nc.dram_tensor("WkT", [D, D], F32, kind="ExternalInput").ap()
    WvT = nc.dram_tensor("WvT", [D, D], F32, kind="ExternalInput").ap()
    WgT = nc.dram_tensor("WgT", [D, 8], F32, kind="ExternalInput").ap()
    bg = nc.dram_tensor("bg", [1, 8], F32, kind="ExternalInput").ap()
    WdlT = nc.dram_tensor("WdlT", [D, 96], F32, kind="ExternalInput").ap()
    Lpar = nc.dram_tensor("Lpar", [1, 96], F32, kind="ExternalInput").ap()
    WoT = nc.dram_tensor("WoT", [D, D], F32, kind="ExternalInput").ap()
    lngcol = nc.dram_tensor("lngcol", [128, PT], F32, kind="ExternalInput").ap()
    lnbcol = nc.dram_tensor("lnbcol", [128, PT], F32, kind="ExternalInput").ap()
    rwT = nc.dram_tensor("rwT", [D, 32], F32, kind="ExternalInput").ap()
    rb = nc.dram_tensor("rb", [1, 32], F32, kind="ExternalInput").ap()
    w1T = nc.dram_tensor("w1T", [32, D, 256], F32, kind="ExternalInput").ap()
    b1col = nc.dram_tensor("b1col", [32, 128, 2], F32, kind="ExternalInput").ap()
    w2T = nc.dram_tensor("w2T", [32, 256, D], F32, kind="ExternalInput").ap()
    b2col = nc.dram_tensor("b2col", [32, 128, PT], F32, kind="ExternalInput").ap()
    x2 = nc.dram_tensor("x2", [D, T], F32, kind="ExternalOutput").ap()
    with ExitStack() as ctx:
        p = Prog(nc, ctx)
        ph1, scratch, ohp_rows = emit_l1(p, nc, ctx, T, x1, WqT, WkT, WvT, WgT,
                                         bg, WdlT, Lpar, WoT, lngcol, lnbcol,
                                         rwT, rb, w1T, b1col, w2T, b2col, x2)
        ph2 = emit_l1_attn_heads(p, nc, ctx, T, scratch, WoT, lngcol, lnbcol,
                                 x1, scratch[4], entry=tuple(ph1))
        fins = emit_l1_moe(p, nc, ctx, T, scratch[0], scratch[4], ohp_rows,
                           w1T, b1col, w2T, b2col, x2, entry=tuple(ph2))
        p.op("gpsimd", lambda g: g.nop(), after=tuple(fins))
        p.emit()
    return nc


def emit_l2(p, nc, ctx, T, x2, WinT, wconvc, bconvc, WxT, WdtT, bdtc, AlogC,
            lngcol, lnbcol, WoutT, boutc, WrT, br_row, W1T, b1col, W2T, b2col,
            x3_out, entry=()):
    TT = 512
    NS = T // TT
    NN = 16
    DFF = 4096
    import ml_dtypes
    ones_c_h = nc.inline_tensor(np.ones((128, 1), ml_dtypes.bfloat16), name=_nm("onc"))
    ones_r_h = nc.inline_tensor(np.ones((1, 128), np.float32), name=_nm("onr"))
    idf_h = nc.inline_tensor(np.eye(128, dtype=np.float32), name=_nm("idf"))

    xn_d = nc.dram_tensor(_nm("l2xn"), [D, T], BF16, kind="Internal").ap()
    gate_d = nc.dram_tensor(_nm("l2gt"), [D, T], BF16, kind="Internal").ap()
    delta_d = nc.dram_tensor(_nm("l2dl"), [D, T], F32, kind="Internal").ap()
    bc_d = nc.dram_tensor(_nm("l2bc"), [32, T], F32, kind="Internal").ap()
    y_d = nc.dram_tensor(_nm("l2y"), [D, T], F32, kind="Internal").ap()
    aacc_d = nc.dram_tensor(_nm("l2aa"), [D, T], F32, kind="Internal").ap()

    oh_sb = SB(nc, ctx, [1, 32], F32, "l2oh")
    idx_sb = SB(nc, ctx, [1, 1], I32, "l2idx")
    iota8_np = np.arange(8, dtype=np.float32)[None, :]
    iota8_h = nc.inline_tensor(iota8_np, name=_nm("iota8"))
    iota8_sb = SB(nc, ctx, [1, 8], F32, "iota8")
    with ExitStack() as c1:
        winw = SB(nc, c1, [128, PT * 2 * D], BF16, "winw")
        wdtw = SB(nc, c1, [128, PT * D], BF16, "wdtw")
        wxw = SB(nc, c1, [128, PT * 32], BF16, "wxw")
        wrw = SB(nc, c1, [128, PT * 8], F32, "wrw")
        br_sb = SB(nc, c1, [1, 8], F32, "brs")
        cc_sb = SB(nc, c1, [128, 4 * PT], F32, "ccs")
        cb_sb = SB(nc, c1, [128, PT], F32, "cbs")
        dt_sb = SB(nc, c1, [128, PT], F32, "dts")
        x_f = [[SB(nc, c1, [128, TT], F32, "x") for _ in range(PT)] for _ in range(2)]
        xn_bf = [[SB(nc, c1, [128, TT], BF16, "xn") for _ in range(PT)] for _ in range(2)]
        xn32 = [SB(nc, c1, [128, TT], F32, "xn32") for _ in range(PT)]
        xin_sb = [SB(nc, c1, [128, TT + 4], BF16, "xin") for _ in range(PT)]
        z_sb = [SB(nc, c1, [128, TT], F32, "z") for _ in range(2)]
        gt_sb = [SB(nc, c1, [128, TT], BF16, "gt") for _ in range(PT)]
        cv_sb = [SB(nc, c1, [128, TT], F32, "cv") for _ in range(2)]
        sp_sb = [SB(nc, c1, [128, TT], F32, "sp") for _ in range(2)]
        xmean = SB(nc, c1, [128, PT + 2], F32, "xmean")
        ones_c = SB(nc, c1, [128, 1], BF16, "oc")
        ones_r = SB(nc, c1, [1, 128], F32, "orr")
        idf_sb = SB(nc, c1, [128, 128], F32, "idf")
        onesf = SB(nc, c1, [128, 1], F32, "onesf")
        lnbufs = dict(
            stat=SB(nc, c1, [1, 3 * TT], F32, "st"),
            xbf=[SB(nc, c1, [128, TT], BF16, "xc")],
            sq=SB(nc, c1, [128, TT], BF16, "sqq"),
            nsc=[SB(nc, c1, [128, TT], F32, "nsc")],
        )
        lnbufs["p_sum1"] = PS(nc, c1, [1, TT], F32, "ps1")
        lnbufs["p_sum2"] = PS(nc, c1, [1, TT], F32, "ps2")
        lnbufs["p_bc1"] = PS(nc, c1, [128, TT], F32, "pb1")
        lnbufs["p_bc2"] = PS(nc, c1, [128, TT], F32, "pb2")
        p_mm = [PS(nc, c1, [128, TT], F32, "pmm") for _ in range(2)]
        p_sm = PS(nc, c1, [128, 64], F32, "psm")

        lw = p.op("gpsimd", lambda g: [
            g.dma_start(out=winw[:].rearrange("p (i o) -> p i o", o=2 * D),
                        in_=WinT.rearrange("(i p) o -> p i o", p=128)),
            g.dma_start(out=wdtw[:].rearrange("p (i o) -> p i o", o=D),
                        in_=WdtT.rearrange("(i p) o -> p i o", p=128)),
            g.dma_start(out=wxw[:].rearrange("p (i o) -> p i o", o=32),
                        in_=WxT.rearrange("(i p) o -> p i o", p=128))],
            units=3, dma=True, stream="wld", after=entry)
        lw2 = p.op("sync", lambda e: [
            e.dma_start(out=wrw[:].rearrange("p (i o) -> p i o", o=8),
                        in_=WrT.rearrange("(i p) o -> p i o", p=128)),
            e.dma_start(out=br_sb[:], in_=br_row[:, :]),
            e.dma_start(out=cc_sb[:], in_=wconvc[:, :]),
            e.dma_start(out=cb_sb[:], in_=bconvc[:, :]),
            e.dma_start(out=dt_sb[:], in_=bdtc[:, :]),
            e.dma_start(out=ones_c[:], in_=ones_c_h.ap()[:, :]),
            e.dma_start(out=ones_r[:], in_=ones_r_h.ap()[:, :]),
            e.dma_start(out=idf_sb[:], in_=idf_h.ap()[:, :]),
            e.dma_start(out=iota8_sb[:], in_=iota8_h.ap()[:, :])],
            units=9, dma=True, stream="cst", after=entry)
        lio = lw2
        zpad = [p.op("vector", lambda v, i=i: v.memset(xin_sb[i][:, 0:4], 0.0),
                     after=entry) for i in range(PT)]
        oz = p.op("vector", lambda v: v.memset(onesf[:], 1.0), after=entry)
        mz = p.op("vector", lambda v: v.memset(xmean[:], 0.0), after=entry)

        prev_ln = []
        prev_x_cons = [None, None]
        prev_xn_cons = [(), ()]
        evs = []
        mean_ops = []
        xz_done = []
        last_conv = {}
        for s in range(NS):
            par = s % 2
            xl = p.op("sync", lambda e, s=s, par=par: [
                e.dma_start(out=x_f[par][i][:], in_=x2[128 * i:128 * (i + 1), TT * s:TT * (s + 1)])
                for i in range(PT)], units=PT, dma=True, stream=f"ax{par}",
                after=((prev_x_cons[par],) if prev_x_cons[par] else ()) + entry)
            xn_view = [xn_bf[par][i][:] for i in range(PT)]
            louts = emit_ln(p, nc, ctx, [x_f[par][i] for i in range(PT)], xn_view, TT,
                            ones_c, ones_r, lnbufs, x_ready=(xl, lw2) + tuple(prev_ln)
                            + tuple(prev_xn_cons[par]),
                            xn32_tiles=xn32)
            prev_ln = list(louts)
            prev_x_cons[par] = louts[-1]
            xnst = p.op("sync", lambda e, s=s, par=par: [
                e.dma_start(out=xn_d[128 * i:128 * (i + 1), TT * s:TT * (s + 1)],
                            in_=xn_bf[par][i][:]) for i in range(PT)],
                units=PT, dma=True, stream="xnd", after=tuple(louts))
            for i in range(PT):
                mo = p.op("vector", lambda v, i=i, s=s: v.tensor_scalar(
                    xmean[:, PT:PT + 1], xn32[i][:], 1.0, None, ALU.mult,
                    accum_out=None) if False else v.reduce_sum(
                    xmean[:, PT + 1:PT + 2], xn32[i][:], AX.X),
                    after=tuple(louts) + tuple(mean_ops[-1:]))
                ac = p.op("vector", lambda v, i=i, s=s: v.tensor_tensor(
                    xmean[:, i:i + 1], xmean[:, i:i + 1], xmean[:, PT + 1:PT + 2],
                    ALU.add), after=(mo, mz) + ((mean_ops[-1],) if mean_ops else ()))
                mean_ops.append(ac)
            xz_mm_last = None
            for o in range(2 * PT):
                pb = p_mm[o % 2]
                mm = mm_chain(p, pb[:, 0:TT],
                              [wtile(winw, i, o, 2 * D) for i in range(PT)],
                              [xn_bf[par][i][:] for i in range(PT)],
                              after=tuple(louts) + (lw,)
                              + ((evs[-2],) if len(evs) >= 2 else ()))
                xz_mm_last = mm
                if o < PT:
                    if s == 0:
                        carry = zpad[o]
                    else:
                        carry = p.op("vector", lambda v, o=o: v.tensor_copy(
                            xin_sb[o][:, 0:4], xin_sb[o][:, TT:TT + 4]),
                            after=(zpad[o],) + tuple(last_conv.get(o, ())))
                    ev = p.op("vector", lambda v, o=o, s=s, pb=pb: v.tensor_copy(
                        xin_sb[o][:, 4:4 + TT], pb[:, 0:TT]),
                        after=(mm, carry) + ((evs[-2],) if len(evs) >= 2 else ()))
                else:
                    ev = p.op("vector", lambda v, o=o, s=s, pb=pb: v.tensor_copy(
                        z_sb[s % 2][:] if False else z_sb[(o + s) % 2][:], pb[:, 0:TT]),
                        after=(mm,) + ((evs[-2],) if len(evs) >= 2 else ()))
                    oo = o - PT
                    def conv_fn(v, oo=oo, s=s, zpar=(o + s) % 2):
                        r = v.tensor_scalar(cv_sb[s % 2][:, 0:TT],
                                            xin_sb[oo][:, 1: 1 + TT],
                                            cc_sb[:, 4 * oo + 0:4 * oo + 1], None,
                                            ALU.mult)
                        return r
                    cv1 = p.op("vector", conv_fn, after=(ev, evs[-1] if evs else ev, lw2)
                               + tuple(xz_done[-1:]))
                    cvs = [cv1]
                    for j in range(1, 4):
                        cvj = p.op("vector", lambda v, oo=oo, s=s, j=j: v.scalar_tensor_tensor(
                            cv_sb[s % 2][:, 0:TT],
                            xin_sb[oo][:, 1 + j: 1 + j + TT],
                            cc_sb[:, 4 * oo + j:4 * oo + j + 1],
                            cv_sb[s % 2][:, 0:TT], ALU.mult, ALU.add),
                            after=(cvs[-1],))
                        cvs.append(cvj)
                    si = p.op("scalar", lambda sc, oo=oo, s=s: sc.activation(
                        sp_sb[s % 2][:, 0:TT], cv_sb[s % 2][:, 0:TT], AF.Silu,
                        bias=cb_sb[:, oo:oo + 1]), after=(cvs[-1], lw2))
                    gt = p.op("vector", lambda v, oo=oo, s=s, zpar=(o + s) % 2: v.tensor_tensor(
                        gt_sb[oo][:, 0:TT], sp_sb[s % 2][:, 0:TT], z_sb[zpar][:],
                        ALU.mult), after=(si, ev))
                    gst = p.op("sync", lambda e, oo=oo, s=s: [
                        e.dma_start(out=gate_d[128 * oo:128 * (oo + 1), TT * s:TT * (s + 1)],
                                    in_=gt_sb[oo][:, 0:TT])], units=1, dma=True,
                        stream="gout", after=(gt,))
                    xz_done.append(gst)
                    last_conv[oo] = (cvs[-1], si)
                evs.append(ev)
            gates_ready = tuple(xz_done[-PT:])
            for o in range(PT):
                pb = p_mm[o % 2]
                mm = mm_chain(p, pb[:, 0:TT],
                              [wtile(wdtw, i, o, D) for i in range(PT)],
                              [gt_sb[i][:, 0:TT] for i in range(PT)],
                              after=gates_ready + (lw,)
                              + ((evs[-2],) if len(evs) >= 2 else ()))
                par2 = o % 2
                d0 = p.op("scalar", lambda sc, o=o, par2=par2, pb=pb: sc.activation(
                    cv_sb[par2][:, 0:TT], pb[:, 0:TT], AF.Identity,
                    bias=dt_sb[:, o:o + 1]), after=(mm, lw2)
                    + ((evs[-2],) if len(evs) >= 2 else ()))
                s1 = p.op("scalar", lambda sc, par2=par2: sc.activation(
                    sp_sb[par2][:, 0:TT], cv_sb[par2][:, 0:TT], AF.Abs), after=(d0,))
                s2 = p.op("scalar", lambda sc, par2=par2: sc.activation(
                    sp_sb[par2][:, 0:TT], sp_sb[par2][:, 0:TT], AF.Exp, scale=-1.0),
                    after=(s1,))
                s3 = p.op("scalar", lambda sc, par2=par2: sc.activation(
                    sp_sb[par2][:, 0:TT], sp_sb[par2][:, 0:TT], AF.Ln,
                    bias=onesf[:, 0:1]), after=(s2, oz))
                s4 = p.op("vector", lambda v, par2=par2: v.tensor_scalar_max(
                    cv_sb[par2][:, 0:TT], cv_sb[par2][:, 0:TT], 0.0), after=(s3,))
                s5 = p.op("vector", lambda v, par2=par2: v.tensor_tensor(
                    cv_sb[par2][:, 0:TT], cv_sb[par2][:, 0:TT], sp_sb[par2][:, 0:TT],
                    ALU.add), after=(s4,))
                dst_op = p.op("sync", lambda e, o=o, s=s, par2=par2: [
                    e.dma_start(out=delta_d[128 * o:128 * (o + 1), TT * s:TT * (s + 1)],
                                in_=cv_sb[par2][:, 0:TT])], units=1, dma=True,
                    stream="dout", after=(s5,))
                evs.append(dst_op)
            pbx = p_mm[0]
            def bcmm_fn(te, s=s, pbx=pbx):
                ins = None
                for i in range(PT):
                    ins = te.matmul(pbx[0:32, 0:TT], wtile(wxw, i, 0, 32, width=32),
                                    gt_sb[i][:, 0:TT], start=(i == 0),
                                    stop=(i == PT - 1))
                return ins
            mmbc = p.op("tensor", bcmm_fn,
                        after=gates_ready + (lw,)
                        + ((evs[-2],) if len(evs) >= 2 else ()))
            bce = p.op("vector", lambda v, pbx=pbx: v.tensor_copy(
                sp_sb[0][0:32, 0:TT], pbx[0:32, 0:TT]),
                after=(mmbc,) + ((evs[-2],) if len(evs) >= 2 else ()))
            bst = p.op("sync", lambda e, s=s: [
                e.dma_start(out=bc_d[:, TT * s:TT * (s + 1)], in_=sp_sb[0][0:32, 0:TT])],
                units=1, dma=True, stream="bcout", after=(bce,))
            evs.append(bst)
            evs.append(bst)
            prev_xn_cons[par] = (xnst, xz_mm_last)
        mdone = mean_ops[-1]
        msc = p.op("vector", lambda v: v.tensor_scalar_mul(
            xmean[:, 0:PT], xmean[:, 0:PT], 1.0 / T), after=(mdone,))
        def rmm(te):
            ins = None
            for i in range(PT):
                ins = te.matmul(p_sm[0:1, 0:8], xmean[:, i:i + 1],
                                wrw[:, i * 8:(i + 1) * 8], start=(i == 0), stop=False)
            return te.matmul(p_sm[0:1, 0:8], onesf[0:1, 0:1], br_sb[0:1, :],
                             start=False, stop=True)
        rmo = p.op("tensor", rmm, after=(msc, lw2, oz)
                   + ((evs[-2],) if len(evs) >= 2 else ()))
        rme = p.op("vector", lambda v: v.tensor_copy(oh_sb[0:1, 0:8], p_sm[0:1, 0:8]),
                   after=(rmo,))
        rmx = p.op("vector", lambda v: v.reduce_max(
            oh_sb[0:1, 8:9], oh_sb[0:1, 0:8], AX.X), after=(rme,))
        roh = p.op("vector", lambda v: v.tensor_scalar(
            oh_sb[0:1, 16:24], oh_sb[0:1, 0:8], oh_sb[0:1, 8:9], None,
            ALU.is_ge), after=(rmx,))
        iot = p.op("vector", lambda v: v.tensor_tensor(
            oh_sb[0:1, 24:32], oh_sb[0:1, 16:24], iota8_sb[0:1, 0:8], ALU.mult),
            after=(roh, lio))
        isum = p.op("vector", lambda v: v.reduce_sum(
            oh_sb[0:1, 9:10], oh_sb[0:1, 24:32], AX.X), after=(iot,))
        icast = p.op("vector", lambda v: v.tensor_copy(
            idx_sb[0:1, 0:1], oh_sb[0:1, 9:10]), after=(isum,))
        ph1 = tuple(evs[-3:]) + (xnst, icast)
    return ph1, (xn_d, gate_d, delta_d, bc_d, y_d, aacc_d), (oh_sb, idx_sb, icast)


def emit_l2_scan(p, nc, ctx, T, scratch, AlogC, entry=()):
    xn_d, gate_d, delta_d, bc_d, y_d, aacc_d = scratch
    TT = 512
    NS = T // TT
    NN = 16
    import ml_dtypes
    ones_r_h = nc.inline_tensor(np.ones((1, 128), np.float32), name=_nm("onr"))
    with ExitStack() as c2:
        dl_sb = [SB(nc, c2, [128, T], F32, "dl2") for _ in range(2)]
        gt_sb = [SB(nc, c2, [128, T], BF16, "gt2") for _ in range(2)]
        u_sb = SB(nc, c2, [128, T], F32, "u2")
        da_sb = SB(nc, c2, [128, T], F32, "da2")
        dbx_sb = SB(nc, c2, [128, T], F32, "dbx2")
        h_sb = SB(nc, c2, [128, T], F32, "h2")
        y_sb = SB(nc, c2, [128, T], F32, "y2acc")
        bcrow = [SB(nc, c2, [1, 2 * T], BF16, "bcr") for _ in range(2)]
        ones_rb = SB(nc, c2, [1, 128], BF16, "orb")
        acol = SB(nc, c2, [128, 2 * NN * PT // 16 + 2 * NN] if False else [128, 16 * PT + 2], F32, "acol")
        ones_r = SB(nc, c2, [1, 128], F32, "orr")
        tmp_sb = SB(nc, c2, [128, TT], F32, "tmp2")
        p_bc = [PS(nc, c2, [128, TT], F32, "pbc") for _ in range(2)]

        onesb_h = nc.inline_tensor(np.ones((1, 128), ml_dtypes.bfloat16),
                                   name=_nm("onb"))
        lc = p.op("sync", lambda e: [
            e.dma_start(out=ones_r[:], in_=ones_r_h.ap()[:, :]),
            e.dma_start(out=ones_rb[:], in_=onesb_h.ap()[:, :]),
            e.dma_start(out=acol[:, 0:16 * PT], in_=AlogC[:, :])],
            units=3, dma=True, stream="cst", after=entry)
        a1 = p.op("scalar", lambda sc: sc.activation(
            acol[:, 0:16 * PT], acol[:, 0:16 * PT], AF.Exp), after=(lc,))
        a2 = p.op("vector", lambda v: v.tensor_scalar_mul(
            acol[:, 0:16 * PT], acol[:, 0:16 * PT], -1.0), after=(a1,))

        prev_y_st = None
        prev_o_ops = entry
        lds = {}
        last_dao = {}
        last_bc_cons = {}

        def issue_ld(o2):
            if o2 >= PT or o2 in lds:
                return
            ld = p.op("sync", lambda e, o2=o2: [
                e.dma_start(out=dl_sb[o2 % 2][:], in_=delta_d[128 * o2:128 * (o2 + 1), :]),
                e.dma_start(out=gt_sb[o2 % 2][:], in_=gate_d[128 * o2:128 * (o2 + 1), :])],
                units=2, dma=True, stream="sld",
                after=(tuple(entry) if o2 == 0 else (last_dao[o2 - 2],) if o2 >= 2 else ()))
            lds[o2] = ld

        def issue_bl(o, n):
            npar = n % 2
            war = last_bc_cons.get((o, n - 2)) or last_bc_cons.get((o - 1, NN + n - 2))
            return p.op("gpsimd", lambda e, n=n, npar=npar: [
                e.dma_start(out=bcrow[npar][0:1, 0:T], in_=bc_d[n:n + 1, :]),
                e.dma_start(out=bcrow[npar][0:1, T:2 * T], in_=bc_d[16 + n:17 + n, :])],
                units=2, dma=True, stream=f"bl{npar}",
                after=((war,) if war else tuple(entry)))

        bls = {}
        for o in range(PT):
            issue_ld(o)
            issue_ld(o + 1)
            ld = lds[o]
            uo = p.op("vector", lambda v, o=o: v.tensor_tensor(
                u_sb[:], dl_sb[o % 2][:], gt_sb[o % 2][:], ALU.mult),
                after=(ld,) + tuple(prev_o_ops))
            yz = p.op("vector", lambda v: v.memset(y_sb[:], 0.0),
                      after=((prev_y_st,) if prev_y_st else ()))
            nn_ops = [uo, yz]
            for n in range(NN):
                npar = n % 2
                if (o, n) not in bls:
                    bls[(o, n)] = issue_bl(o, n)
                bl = bls[(o, n)]
                dao = p.op("scalar", lambda sc, o=o, n=n: sc.activation(
                    da_sb[:], dl_sb[o % 2][:], AF.Exp,
                    scale=acol[:, 16 * o + n:16 * o + n + 1]),
                    after=(a2, ld) + tuple(nn_ops[-2:]))
                last_dao[o] = dao
                dbs = []
                for scnk in range(NS):
                    pb = p_bc[scnk % 2]
                    bcm = p.op("tensor", lambda te, scnk=scnk, npar=npar, pb=pb: te.matmul(
                        pb[:, 0:TT], ones_rb[0:1, :],
                        bcrow[npar][0:1, TT * scnk:TT * (scnk + 1)],
                        start=True, stop=True), after=(bl, lc) + tuple(dbs[-2:])
                        + tuple(nn_ops[-2:]))
                    db = p.op("vector", lambda v, scnk=scnk, pb=pb: v.tensor_tensor(
                        dbx_sb[:, TT * scnk:TT * (scnk + 1)], u_sb[:, TT * scnk:TT * (scnk + 1)],
                        pb[:, 0:TT], ALU.mult), after=(bcm, uo))
                    dbs.append(db)
                sc_op = p.op("vector", lambda v: v.tensor_tensor_scan(
                    h_sb[:], da_sb[:], dbx_sb[:], 0.0, ALU.mult, ALU.add),
                    after=tuple(dbs) + (dao,))
                yops = []
                for scnk in range(NS):
                    pb = p_bc[scnk % 2]
                    ccm = p.op("tensor", lambda te, scnk=scnk, npar=npar, pb=pb: te.matmul(
                        pb[:, 0:TT], ones_rb[0:1, :],
                        bcrow[npar][0:1, T + TT * scnk:T + TT * (scnk + 1)],
                        start=True, stop=True), after=(bl, sc_op) + tuple(yops[-2:]))
                    y1 = p.op("vector", lambda v, scnk=scnk, pb=pb: v.tensor_tensor(
                        tmp_sb[:, 0:TT], h_sb[:, TT * scnk:TT * (scnk + 1)], pb[:, 0:TT],
                        ALU.mult), after=(ccm, sc_op) + tuple(yops[-1:]))
                    y2 = p.op("vector", lambda v, scnk=scnk: v.tensor_tensor(
                        y_sb[:, TT * scnk:TT * (scnk + 1)], y_sb[:, TT * scnk:TT * (scnk + 1)],
                        tmp_sb[:, 0:TT], ALU.add), after=(y1, yz))
                    yops.append(y2)
                last_bc_cons[(o, n)] = yops[-1]
                nn_ops += [sc_op, yops[-1]]
                if n + 2 < NN:
                    bls[(o, n + 2)] = issue_bl(o, n + 2)
                elif o + 1 < PT:
                    bls[(o + 1, n + 2 - NN)] = issue_bl(o + 1, n + 2 - NN)
            yst = p.op("sync", lambda e, o=o: [
                e.dma_start(out=y_d[128 * o:128 * (o + 1), :], in_=y_sb[:])],
                units=1, dma=True, stream="yout", after=tuple(nn_ops[-2:]))
            prev_y_st = yst
            prev_o_ops = nn_ops[-2:] + [yst]
        return (yst,)


def emit_l2_out(p, nc, ctx, T, x2, scratch, lngcol, lnbcol, WoutT, boutc, entry=()):
    xn_d, gate_d, delta_d, bc_d, y_d, aacc_d = scratch
    TT = 512
    NS = T // TT
    import ml_dtypes
    ones_c2_h = nc.inline_tensor(np.ones((128, 1), ml_dtypes.bfloat16), name=_nm("onc"))
    ones_r2_h = nc.inline_tensor(np.ones((1, 128), np.float32), name=_nm("onr"))
    with ExitStack() as c3:
        wow = SB(nc, c3, [128, PT * D], BF16, "wout")
        lng = SB(nc, c3, [128, PT], F32, "lng")
        lnb = SB(nc, c3, [128, PT], F32, "lnb")
        bo = SB(nc, c3, [128, PT], F32, "bo")
        y2 = [SB(nc, c3, [128, TT], F32, "y3") for _ in range(PT)]
        y2n = [SB(nc, c3, [128, TT], BF16, "y3n") for _ in range(PT)]
        x2_t = [SB(nc, c3, [128, TT], F32, "x2t") for _ in range(PT)]
        ao_f = [SB(nc, c3, [128, TT], F32, "aof") for _ in range(2)]
        ones_c2 = SB(nc, c3, [128, 1], BF16, "oc2")
        ones_r2 = SB(nc, c3, [1, 128], F32, "or2")
        lnbufs2 = dict(
            stat=SB(nc, c3, [1, 3 * TT], F32, "st2"),
            xbf=[SB(nc, c3, [128, TT], BF16, "xc2") for _ in range(PT)],
            sq=SB(nc, c3, [128, TT], BF16, "sq2"),
            nsc=[SB(nc, c3, [128, TT], F32, "ns2")],
        )
        lnbufs2["p_sum1"] = PS(nc, c3, [1, TT], F32, "ps1b")
        lnbufs2["p_sum2"] = PS(nc, c3, [1, TT], F32, "ps2b")
        lnbufs2["p_bc1"] = PS(nc, c3, [128, TT], F32, "pb1b")
        lnbufs2["p_bc2"] = PS(nc, c3, [128, TT], F32, "pb2b")
        p_pr = [PS(nc, c3, [128, TT], F32, "ppr") for _ in range(2)]
        lw2 = load_weight_bf16(p, wow, WoutT, D, D, after=entry)
        lcs = p.op("sync", lambda e: [
            e.dma_start(out=lng[:], in_=lngcol[:, :]),
            e.dma_start(out=lnb[:], in_=lnbcol[:, :]),
            e.dma_start(out=bo[:], in_=boutc[:, :]),
            e.dma_start(out=ones_c2[:], in_=ones_c2_h.ap()[:, :]),
            e.dma_start(out=ones_r2[:], in_=ones_r2_h.ap()[:, :])],
            units=5, dma=True, stream="cst", after=entry)
        prev_ln2 = list(entry)
        last_st = None
        for s in range(NS):
            xld = p.op("sync", lambda e, s=s: [
                e.dma_start(out=y2[i][:], in_=y_d[128 * i:128 * (i + 1), TT * s:TT * (s + 1)])
                for i in range(PT)] + [
                e.dma_start(out=x2_t[i][:], in_=x2[128 * i:128 * (i + 1), TT * s:TT * (s + 1)])
                for i in range(PT)], units=2 * PT, dma=True, stream="pl",
                after=tuple(prev_ln2) + entry)
            xnl = p.op("gpsimd", lambda e, s=s: [
                e.dma_start(out=lnbufs2["xbf"][i][:],
                            in_=xn_d[128 * i:128 * (i + 1), TT * s:TT * (s + 1)])
                for i in range(PT)], units=PT, dma=True, stream="pl2",
                after=tuple(prev_ln2) + entry)
            ads = []
            for i in range(PT):
                ad = p.op("vector", lambda v, i=i: v.tensor_tensor(
                    y2[i][:], y2[i][:], lnbufs2["xbf"][i][:], ALU.add),
                    after=(xld, xnl))
                ads.append(ad)
            louts = emit_ln(p, nc, ctx, y2, y2n, TT, ones_c2, ones_r2, lnbufs2,
                            x_ready=tuple(ads) + (lcs,))
            sts = []
            gbs = []
            for o in range(PT):
                gb = p.op("vector", lambda v, o=o: v.tensor_scalar(
                    y2n[o][:, 0:TT], y2n[o][:, 0:TT], lng[:, o:o + 1],
                    lnb[:, o:o + 1], ALU.mult, ALU.add), after=tuple(louts))
                gbs.append(gb)
            for o in range(PT):
                ppr = p_pr[o % 2]
                mm = mm_chain(p, ppr[:, 0:TT],
                              [wtile(wow, i, o, D) for i in range(PT)],
                              [y2n[i][:, 0:TT] for i in range(PT)],
                              after=tuple(gbs) + (lw2,)
                              + ((sts[-2],) if len(sts) >= 2 else ()))
                bi = p.op("scalar", lambda sc, o=o, ppr=ppr: sc.activation(
                    ao_f[o % 2][:, 0:TT], ppr[:, 0:TT], AF.Identity, bias=bo[:, o:o + 1]),
                    after=(mm,) + ((sts[-2],) if len(sts) >= 2 else ()))
                ax = p.op("vector", lambda v, o=o: v.tensor_tensor(
                    ao_f[o % 2][:, 0:TT], ao_f[o % 2][:, 0:TT], x2_t[o][:], ALU.add),
                    after=(bi, xld))
                st = p.op("sync", lambda e, o=o, s=s: [
                    e.dma_start(out=aacc_d[128 * o:128 * (o + 1), TT * s:TT * (s + 1)],
                                in_=ao_f[o % 2][:, 0:TT])], units=1, dma=True,
                    stream="aout", after=(ax,))
                sts.append(st)
            prev_ln2 = sts
            last_st = st
        return (last_st,)


def emit_l2_moe(p, nc, ctx, T, scratch, l2r, W1T, b1col, W2T, b2col, x3_out,
                entry=()):
    from concourse.bass_core_rust import IndirectOffsetOnAxis
    xn_d, gate_d, delta_d, bc_d, y_d, aacc_d = scratch
    oh_sb, idx_sb, icast = l2r
    TT = 512
    NS = T // TT
    DFF = 4096
    MH = DFF // 128
    import ml_dtypes
    iota_h = nc.inline_tensor(np.arange(128, dtype=np.float32)[:, None], name=_nm("io128"))
    ones_r_h = nc.inline_tensor(np.ones((1, 128), np.float32), name=_nm("onr"))
    with ExitStack() as c4:
        w1_sb = SB(nc, c4, [128, PT * DFF], BF16, "w1s")
        w2_sb = SB(nc, c4, [128, MH * D], BF16, "w2s")
        b1_sb = SB(nc, c4, [128, MH], F32, "b1s")
        b2_sb = SB(nc, c4, [128, PT], F32, "b2s")
        iota_sb = SB(nc, c4, [128, 1], F32, "iot")
        idxf = SB(nc, c4, [128, 4], F32, "idxf")
        idxi = SB(nc, c4, [128, 2], I32, "idxi")
        ones_r = SB(nc, c4, [1, 128], F32, "orr")
        xn_t = [[SB(nc, c4, [128, TT], BF16, "xnt") for _ in range(PT)] for _ in range(2)]
        h_sb = [SB(nc, c4, [128, TT], BF16, "hsb") for _ in range(MH)]
        y_sb = [SB(nc, c4, [128, TT], F32, "ysb") for _ in range(2)]
        a_t = [SB(nc, c4, [128, TT], F32, "at3") for _ in range(2)]
        p_h = [PS(nc, c4, [128, TT], F32, "phh") for _ in range(2)]
        p_y = [PS(nc, c4, [128, TT], F32, "pyy") for _ in range(2)]
        p_b = PS(nc, c4, [128, 1], F32, "pbb")

        lc = p.op("sync", lambda e: [
            e.dma_start(out=iota_sb[:], in_=iota_h.ap()[:, :]),
            e.dma_start(out=ones_r[:], in_=ones_r_h.ap()[:, :])],
            units=2, dma=True, stream="cst", after=entry)
        bcm = p.op("tensor", lambda te: te.matmul(
            p_b[:, 0:1], ones_r[0:1, :], oh_sb[0:1, 9:10], start=True, stop=True),
            after=(lc, icast) + entry)
        bce = p.op("vector", lambda v: v.tensor_copy(idxf[:, 0:1], p_b[:, 0:1]),
                   after=(bcm,))
        i1 = p.op("vector", lambda v: v.tensor_scalar(
            idxf[:, 1:2], idxf[:, 0:1], 1024.0, None, ALU.mult), after=(bce,))
        i2 = p.op("vector", lambda v: v.tensor_tensor(
            idxf[:, 1:2], idxf[:, 1:2], iota_sb[:, 0:1], ALU.add), after=(i1,))
        i3 = p.op("vector", lambda v: v.tensor_scalar(
            idxf[:, 2:3], idxf[:, 0:1], 128.0, None, ALU.mult), after=(bce,))
        i4 = p.op("vector", lambda v: v.tensor_tensor(
            idxf[:, 2:3], idxf[:, 2:3], iota_sb[:, 0:1], ALU.add), after=(i3,))
        w1flat = bass.AP(W1T.tensor, 0, [[4096, 8 * 1024], [1, 4096]])
        w2flat = bass.AP(W2T.tensor, 0, [[D, 8 * 4096], [1, D]])
        b1flat = bass.AP(b1col.tensor, 0, [[32, 8 * 128], [1, 32]])
        b2flat = bass.AP(b2col.tensor, 0, [[PT, 8 * 128], [1, PT]])
        gops = []
        icasts = []
        for b in range(PT):
            ic = p.op("vector", lambda v, b=b: v.tensor_scalar(
                idxf[:, 3:4],
                idxf[:, 1:2], float(128 * b), None, ALU.add), after=(i2,)
                + tuple(icasts[-1:]) + tuple(gops[-1:]))
            ic2 = p.op("vector", lambda v: v.tensor_copy(idxi[:, 0:1], idxf[:, 3:4]),
                       after=(ic,))
            icasts.append(ic2)
            g1 = p.op("gpsimd", lambda e, b=b: [
                e.indirect_dma_start(
                    out=w1_sb[:].rearrange("p (i o) -> p i o", o=DFF)[:, b, :],
                    out_offset=None,
                    in_=w1flat,
                    in_offset=IndirectOffsetOnAxis(idxi[:, 0:1], 0))],
                units=1, dma=True, stream="gw", after=(ic2,) + entry)
            gops.append(g1)
        for b in range(MH):
            ic = p.op("vector", lambda v, b=b: v.tensor_scalar(
                idxf[:, 3:4], idxf[:, 0:1], 4096.0, None, ALU.mult),
                after=(bce,) + tuple(icasts[-1:]) + tuple(gops[-1:]))
            ic1b = p.op("vector", lambda v, b=b: v.tensor_scalar(
                idxf[:, 3:4], idxf[:, 3:4], float(128 * b), None, ALU.add),
                after=(ic,))
            ic1c = p.op("vector", lambda v: v.tensor_tensor(
                idxf[:, 3:4], idxf[:, 3:4], iota_sb[:, 0:1], ALU.add), after=(ic1b,))
            ic2 = p.op("vector", lambda v: v.tensor_copy(idxi[:, 0:1], idxf[:, 3:4]),
                       after=(ic1c,))
            icasts.append(ic2)
            g2 = p.op("gpsimd", lambda e, b=b: [
                e.indirect_dma_start(
                    out=w2_sb[:].rearrange("p (i o) -> p i o", o=D)[:, b, :],
                    out_offset=None,
                    in_=w2flat,
                    in_offset=IndirectOffsetOnAxis(idxi[:, 0:1], 0))],
                units=1, dma=True, stream="gw", after=(ic2,) + entry)
            gops.append(g2)
        icb1 = p.op("vector", lambda v: v.tensor_copy(idxi[:, 1:2], idxf[:, 2:3]),
                    after=(i4,) + tuple(icasts[-1:]) + tuple(gops[-1:]))
        gb1 = p.op("gpsimd", lambda e: [
            e.indirect_dma_start(out=b1_sb[:, :], out_offset=None,
                                 in_=b1flat,
                                 in_offset=IndirectOffsetOnAxis(idxi[:, 1:2], 0)),
            e.indirect_dma_start(out=b2_sb[:, :], out_offset=None,
                                 in_=b2flat,
                                 in_offset=IndirectOffsetOnAxis(idxi[:, 1:2], 0))],
            units=2, dma=True, stream="gw", after=(icb1,) + entry)
        last_xn_cons = [None, None]
        last_h_cons = {}
        last_y = [None, None]
        fins = []
        for s in range(NS):
            par = s % 2
            xl = p.op("sync", lambda eng, s=s, par=par: [
                eng.dma_start(out=xn_t[par][i][:],
                              in_=xn_d[128 * i:128 * (i + 1), TT * s:TT * (s + 1)])
                for i in range(PT)], units=PT, dma=True, stream=f"mx{par}",
                after=((last_xn_cons[par],) if last_xn_cons[par] else ()) + entry)
            h_evs = []
            for m in range(MH):
                pb = p_h[m % 2]
                mm = mm_chain(p, pb[:, 0:TT],
                              [wtile(w1_sb, i, m, DFF) for i in range(PT)],
                              [xn_t[par][i][:] for i in range(PT)],
                              after=(xl,) + tuple(gops)
                              + ((h_evs[-2],) if len(h_evs) >= 2 else ())
                              + ((last_h_cons.get(m),) if m in last_h_cons else ()))
                ev = p.op("scalar", lambda sc, m=m, pb=pb: sc.activation(
                    h_sb[m][:, 0:TT], pb[:, 0:TT], AF.Relu, bias=b1_sb[:, m:m + 1]),
                    after=(mm, gb1))
                h_evs.append(ev)
            last_xn_cons[par] = h_evs[-1]
            for o in range(PT):
                opar = o % 2
                py = p_y[opar]
                mm = mm_chain(p, py[:, 0:TT],
                              [wtile(w2_sb, mi, o, D) for mi in range(MH)],
                              [h_sb[mi][:, 0:TT] for mi in range(MH)],
                              after=tuple(h_evs) + tuple(gops)
                              + ((last_y[opar],) if last_y[opar] else ()))
                for mi in range(MH):
                    last_h_cons[mi] = mm
                bi = p.op("scalar", lambda sc, o=o, opar=opar, py=py: sc.activation(
                    y_sb[opar][:, 0:TT], py[:, 0:TT], AF.Identity,
                    bias=b2_sb[:, o:o + 1]), after=(mm, gb1)
                    + ((last_y[opar],) if last_y[opar] else ()))
                al = p.op("gpsimd", lambda eng, s=s, o=o, opar=opar: [
                    eng.dma_start(out=a_t[opar][:],
                                  in_=aacc_d[128 * o:128 * (o + 1), TT * s:TT * (s + 1)])],
                    units=1, dma=True, stream=f"al{opar}",
                    after=((last_y[opar],) if last_y[opar] else ()) + entry)
                ax = p.op("vector", lambda v, opar=opar: v.tensor_tensor(
                    y_sb[opar][:, 0:TT], y_sb[opar][:, 0:TT], a_t[opar][:], ALU.add),
                    after=(bi, al))
                st = p.op("sync", lambda eng, s=s, o=o, opar=opar: [
                    eng.dma_start(out=x3_out[128 * o:128 * (o + 1), TT * s:TT * (s + 1)],
                                  in_=y_sb[opar][:, 0:TT])], units=1, dma=True,
                    stream="x3o", after=(ax,))
                last_y[opar] = st
                fins.append(st)
        return (fins[-1],)


def build_L2(T=4096):
    nc = bass.Bass("TRN2", target_bir_lowering=False)
    x2 = nc.dram_tensor("x2", [D, T], F32, kind="ExternalInput").ap()
    WinT = nc.dram_tensor("WinT", [D, 2 * D], F32, kind="ExternalInput").ap()
    wconvc = nc.dram_tensor("wconvc", [128, 4 * PT], F32, kind="ExternalInput").ap()
    bconvc = nc.dram_tensor("bconvc", [128, PT], F32, kind="ExternalInput").ap()
    WxT = nc.dram_tensor("WxT", [D, 32], F32, kind="ExternalInput").ap()
    WdtT = nc.dram_tensor("WdtT", [D, D], F32, kind="ExternalInput").ap()
    bdtc = nc.dram_tensor("bdtc", [128, PT], F32, kind="ExternalInput").ap()
    AlogC = nc.dram_tensor("AlogC", [128, 16 * PT], F32, kind="ExternalInput").ap()
    lngcol = nc.dram_tensor("lngcol", [128, PT], F32, kind="ExternalInput").ap()
    lnbcol = nc.dram_tensor("lnbcol", [128, PT], F32, kind="ExternalInput").ap()
    WoutT = nc.dram_tensor("WoutT", [D, D], F32, kind="ExternalInput").ap()
    boutc = nc.dram_tensor("boutc", [128, PT], F32, kind="ExternalInput").ap()
    WrT = nc.dram_tensor("WrT", [D, 8], F32, kind="ExternalInput").ap()
    br = nc.dram_tensor("br", [1, 8], F32, kind="ExternalInput").ap()
    W1T = nc.dram_tensor("W1T", [8, D, 4096], F32, kind="ExternalInput").ap()
    b1col = nc.dram_tensor("b1col", [8, 128, 32], F32, kind="ExternalInput").ap()
    W2T = nc.dram_tensor("W2T", [8, 4096, D], F32, kind="ExternalInput").ap()
    b2col = nc.dram_tensor("b2col", [8, 128, PT], F32, kind="ExternalInput").ap()
    x3 = nc.dram_tensor("x3", [D, T], F32, kind="ExternalOutput").ap()
    with ExitStack() as ctx:
        p = Prog(nc, ctx)
        ph1, scratch, l2r = emit_l2(p, nc, ctx, T, x2, WinT, wconvc, bconvc, WxT,
                                    WdtT, bdtc, AlogC, lngcol, lnbcol, WoutT, boutc,
                                    WrT, br, W1T, b1col, W2T, b2col, x3)
        ph2 = emit_l2_scan(p, nc, ctx, T, scratch, AlogC, entry=tuple(ph1))
        ph3 = emit_l2_out(p, nc, ctx, T, x2, scratch, lngcol, lnbcol, WoutT, boutc,
                          entry=tuple(ph2))
        fins = emit_l2_moe(p, nc, ctx, T, scratch, l2r, W1T, b1col, W2T, b2col,
                           x3, entry=tuple(ph3))
        p.op("gpsimd", lambda g: g.nop(), after=tuple(fins))
        p.emit()
    return nc


def _cols(v, nb=8):
    return np.ascontiguousarray(np.asarray(v, np.float32).reshape(nb, 128).T)


def _t(a):
    return np.ascontiguousarray(np.asarray(a, np.float32).T)


def kernel(**I):
    T = 4096
    x = np.asarray(I["L0_attn_x"], np.float32)
    n = 8

    ncA = build_L0A(T=T, R=512, NSTEP=8)
    common_A = {
        "qkvT": _t(I["L0_attn_qkv_weight"]),
        "projT": _t(I["L0_attn_proj_weight"]),
        "pbias": _cols(I["L0_attn_proj_bias"]),
        "wfT": _t(I["L0_ffn_wf"]), "wiT": _t(I["L0_ffn_wi"]),
        "wcT": _t(I["L0_ffn_wc"]), "ufT": _t(I["L0_ffn_uf"]),
        "uiT": _t(I["L0_ffn_ui"]), "ucT": _t(I["L0_ffn_uc"]),
        "uoT": _t(I["L0_ffn_uo"]),
        "bfc": _cols(I["L0_ffn_bf"]), "bic": _cols(I["L0_ffn_bi"]),
        "bcc": _cols(I["L0_ffn_bc"]), "incc": _cols(I["L0_ffn_in_c"]),
    }
    in_maps = []
    for i in range(n):
        m = dict(common_A)
        m["xb"] = np.ascontiguousarray(x[i].T)
        m["xr"] = np.ascontiguousarray(x[:, 512 * i:512 * (i + 1), :].transpose(0, 2, 1))
        in_maps.append(m)
    resA = run_bass_kernel_spmd(ncA, in_maps, core_ids=list(range(n)))
    attn_acc = [r["attn_acc"] for r in resA.results]
    xnb = [r["xnb"] for r in resA.results]
    cuo_full = np.concatenate([r["cuo"] for r in resA.results], axis=1)

    ncB = build_L0B(T=T)
    common_B = {"woT": _t(I["L0_ffn_wo"]), "bocol": _cols(I["L0_ffn_bo"]),
                "cuo": np.ascontiguousarray(cuo_full)}
    in_maps = [dict(common_B, attn_acc=attn_acc[i], xnb=xnb[i]) for i in range(n)]
    resB = run_bass_kernel_spmd(ncB, in_maps, core_ids=list(range(n)))
    x1 = [r["x1"] for r in resB.results]

    nc1 = build_L1(T=T)
    common_1 = {
        "WqT": _t(I["L1_attn_Wq"]), "WkT": _t(I["L1_attn_Wk"]),
        "WvT": _t(I["L1_attn_Wv"]), "WgT": _t(I["L1_attn_Wg"]),
        "bg": np.asarray(I["L1_attn_bg"], np.float32)[None, :],
        "WdlT": _t(I["L1_attn_Wdl"]),
        "Lpar": np.ascontiguousarray(np.asarray(I["L1_attn_Lparam"], np.float32).reshape(1, 96)),
        "WoT": _t(I["L1_attn_Wo"]),
        "lngcol": _cols(I["L1_attn_ln_g"]), "lnbcol": _cols(I["L1_attn_ln_b"]),
        "rwT": _t(I["L1_ffn_router_w"]),
        "rb": np.asarray(I["L1_ffn_router_b"], np.float32)[None, :],
        "w1T": np.ascontiguousarray(np.asarray(I["L1_ffn_weights1"], np.float32)),
        "b1col": np.ascontiguousarray(np.asarray(I["L1_ffn_bias1"], np.float32)
                                      .reshape(32, 2, 128).transpose(0, 2, 1)),
        "w2T": np.ascontiguousarray(np.asarray(I["L1_ffn_weights2"], np.float32)),
        "b2col": np.ascontiguousarray(np.asarray(I["L1_ffn_bias2"], np.float32)
                                      .reshape(32, 8, 128).transpose(0, 2, 1)),
    }
    in_maps = [dict(common_1, x1=x1[i]) for i in range(n)]
    res1 = run_bass_kernel_spmd(nc1, in_maps, core_ids=list(range(n)))
    x2 = [r["x2"] for r in res1.results]

    nc2 = build_L2(T=T)
    wconv = np.asarray(I["L2_attn_W_conv"], np.float32)[:, 0, :]
    common_2 = {
        "WinT": _t(I["L2_attn_W_in"]),
        "wconvc": np.ascontiguousarray(wconv.reshape(8, 128, 4).transpose(1, 0, 2)
                                       .reshape(128, 32)),
        "bconvc": _cols(I["L2_attn_b_conv"]),
        "WxT": _t(I["L2_attn_W_xproj"]),
        "WdtT": _t(I["L2_attn_W_dt"]),
        "bdtc": _cols(I["L2_attn_b_dt"]),
        "AlogC": np.ascontiguousarray(np.asarray(I["L2_attn_A_log"], np.float32)
                                      .reshape(8, 128, 16).transpose(1, 0, 2)
                                      .reshape(128, 128)),
        "lngcol": _cols(I["L2_attn_ln_gamma"]), "lnbcol": _cols(I["L2_attn_ln_beta"]),
        "WoutT": _t(I["L2_attn_W_out"]),
        "boutc": _cols(I["L2_attn_b_out"]),
        "WrT": np.ascontiguousarray(np.asarray(I["L2_ffn_Wr"], np.float32)),
        "br": np.asarray(I["L2_ffn_br"], np.float32)[None, :],
        "W1T": np.ascontiguousarray(np.asarray(I["L2_ffn_W1"], np.float32)),
        "b1col": np.ascontiguousarray(np.asarray(I["L2_ffn_b1"], np.float32)
                                      .reshape(8, 32, 128).transpose(0, 2, 1)),
        "W2T": np.ascontiguousarray(np.asarray(I["L2_ffn_W2"], np.float32)),
        "b2col": np.ascontiguousarray(np.asarray(I["L2_ffn_b2"], np.float32)
                                      .reshape(8, 8, 128).transpose(0, 2, 1)),
    }
    in_maps = [dict(common_2, x2=x2[i]) for i in range(n)]
    res2 = run_bass_kernel_spmd(nc2, in_maps, core_ids=list(range(n)))
    x3 = [r["x3"] for r in res2.results]

    nc3 = build_L3(T=T)
    common_3 = {
        "wqkvT": _t(I["L3_attn_w_qkv"]),
        "wprojT": _t(I["L3_attn_w_proj"]),
        "pbcol": _cols(I["L3_attn_b_proj"]),
        "relbias": np.ascontiguousarray(np.asarray(I["L3_attn_rel_bias_table"], np.float32)),
        "wgate": np.ascontiguousarray(np.asarray(I["L3_ffn_w_gate"], np.float32)),
        "bgate": np.asarray(I["L3_ffn_b_gate"], np.float32)[None, :],
        "W1T": np.ascontiguousarray(np.asarray(I["L3_ffn_W1"], np.float32)),
        "b1col": np.ascontiguousarray(np.asarray(I["L3_ffn_b1"], np.float32)
                                      .reshape(8, 32, 128).transpose(0, 2, 1)),
        "W2T": np.ascontiguousarray(np.asarray(I["L3_ffn_W2"], np.float32)),
        "b2col": np.ascontiguousarray(np.asarray(I["L3_ffn_b2"], np.float32)
                                      .reshape(8, 8, 128).transpose(0, 2, 1)),
    }
    in_maps = [dict(common_3, x3=x3[i]) for i in range(n)]
    res3 = run_bass_kernel_spmd(nc3, in_maps, core_ids=list(range(n)))
    out = np.stack([res3.results[i]["x4"].T for i in range(n)], axis=0)
    return out


def build_full(T=4096):
    nc = bass.Bass("TRN2", target_bir_lowering=False)
    xb = nc.dram_tensor("xb", [D, T], F32, kind="ExternalInput").ap()
    xfull = nc.dram_tensor("xfull", [8, D, T], F32, kind="ExternalInput").ap()
    qkvT = nc.dram_tensor("qkvT", [D, 3 * D], F32, kind="ExternalInput").ap()
    projT = nc.dram_tensor("projT", [D, D], F32, kind="ExternalInput").ap()
    pbias = nc.dram_tensor("pbias", [128, PT], F32, kind="ExternalInput").ap()
    wT = {k: nc.dram_tensor(k, [D, D], F32, kind="ExternalInput").ap()
          for k in ["wfT", "wiT", "wcT", "ufT", "uiT", "ucT", "uoT", "woT"]}
    bcol = {k: nc.dram_tensor(k, [128, PT], F32, kind="ExternalInput").ap()
            for k in ["bfc", "bic", "bcc", "incc", "bocol"]}
    WqT = nc.dram_tensor("WqT", [D, D], F32, kind="ExternalInput").ap()
    WkT = nc.dram_tensor("WkT", [D, D], F32, kind="ExternalInput").ap()
    WvT = nc.dram_tensor("WvT", [D, D], F32, kind="ExternalInput").ap()
    WgT = nc.dram_tensor("WgT", [D, 8], F32, kind="ExternalInput").ap()
    bg = nc.dram_tensor("bg", [1, 8], F32, kind="ExternalInput").ap()
    WdlT = nc.dram_tensor("WdlT", [D, 96], F32, kind="ExternalInput").ap()
    Lpar = nc.dram_tensor("Lpar", [1, 96], F32, kind="ExternalInput").ap()
    WoT1 = nc.dram_tensor("WoT1", [D, D], F32, kind="ExternalInput").ap()
    lngcol1 = nc.dram_tensor("lngcol1", [128, PT], F32, kind="ExternalInput").ap()
    lnbcol1 = nc.dram_tensor("lnbcol1", [128, PT], F32, kind="ExternalInput").ap()
    rwT1 = nc.dram_tensor("rwT1", [D, 32], F32, kind="ExternalInput").ap()
    rb1 = nc.dram_tensor("rb1", [1, 32], F32, kind="ExternalInput").ap()
    w1T1 = nc.dram_tensor("w1T1", [32, D, 256], F32, kind="ExternalInput").ap()
    b1col1 = nc.dram_tensor("b1col1", [32, 128, 2], F32, kind="ExternalInput").ap()
    w2T1 = nc.dram_tensor("w2T1", [32, 256, D], F32, kind="ExternalInput").ap()
    b2col1 = nc.dram_tensor("b2col1", [32, 128, PT], F32, kind="ExternalInput").ap()
    WinT = nc.dram_tensor("WinT", [D, 2 * D], F32, kind="ExternalInput").ap()
    wconvc = nc.dram_tensor("wconvc", [128, 4 * PT], F32, kind="ExternalInput").ap()
    bconvc = nc.dram_tensor("bconvc", [128, PT], F32, kind="ExternalInput").ap()
    WxT = nc.dram_tensor("WxT", [D, 32], F32, kind="ExternalInput").ap()
    WdtT = nc.dram_tensor("WdtT", [D, D], F32, kind="ExternalInput").ap()
    bdtc = nc.dram_tensor("bdtc", [128, PT], F32, kind="ExternalInput").ap()
    AlogC = nc.dram_tensor("AlogC", [128, 16 * PT], F32, kind="ExternalInput").ap()
    lngcol2 = nc.dram_tensor("lngcol2", [128, PT], F32, kind="ExternalInput").ap()
    lnbcol2 = nc.dram_tensor("lnbcol2", [128, PT], F32, kind="ExternalInput").ap()
    WoutT = nc.dram_tensor("WoutT", [D, D], F32, kind="ExternalInput").ap()
    boutc = nc.dram_tensor("boutc", [128, PT], F32, kind="ExternalInput").ap()
    WrT2 = nc.dram_tensor("WrT2", [D, 8], F32, kind="ExternalInput").ap()
    br2 = nc.dram_tensor("br2", [1, 8], F32, kind="ExternalInput").ap()
    W1T2 = nc.dram_tensor("W1T2", [8, D, 4096], F32, kind="ExternalInput").ap()
    b1col2 = nc.dram_tensor("b1col2", [8, 128, 32], F32, kind="ExternalInput").ap()
    W2T2 = nc.dram_tensor("W2T2", [8, 4096, D], F32, kind="ExternalInput").ap()
    b2col2 = nc.dram_tensor("b2col2", [8, 128, PT], F32, kind="ExternalInput").ap()
    wqkvT3 = nc.dram_tensor("wqkvT3", [D, 3 * D], F32, kind="ExternalInput").ap()
    wprojT3 = nc.dram_tensor("wprojT3", [D, D], F32, kind="ExternalInput").ap()
    pbcol3 = nc.dram_tensor("pbcol3", [128, PT], F32, kind="ExternalInput").ap()
    relbias3 = nc.dram_tensor("relbias3", [32, 16], F32, kind="ExternalInput").ap()
    wgate3 = nc.dram_tensor("wgate3", [D, 8], F32, kind="ExternalInput").ap()
    bgate3 = nc.dram_tensor("bgate3", [1, 8], F32, kind="ExternalInput").ap()
    W1T3 = nc.dram_tensor("W1T3", [8, D, 4096], F32, kind="ExternalInput").ap()
    b1col3 = nc.dram_tensor("b1col3", [8, 128, 32], F32, kind="ExternalInput").ap()
    W2T3 = nc.dram_tensor("W2T3", [8, 4096, D], F32, kind="ExternalInput").ap()
    b2col3 = nc.dram_tensor("b2col3", [8, 128, PT], F32, kind="ExternalInput").ap()
    x4 = nc.dram_tensor("x4", [D, T], F32, kind="ExternalOutput").ap()

    attn_acc_d = nc.dram_tensor(_nm("aacc0"), [D, T], F32, kind="Internal").ap()
    xnb_d = nc.dram_tensor(_nm("xnb0"), [D, T], BF16, kind="Internal").ap()
    cuo_d = nc.dram_tensor(_nm("cuo0"), [D, T], F32, kind="Internal").ap()
    x1_d = nc.dram_tensor(_nm("x1d"), [D, T], F32, kind="Internal").ap()
    x2_d = nc.dram_tensor(_nm("x2d"), [D, T], F32, kind="Internal").ap()
    x3_d = nc.dram_tensor(_nm("x3d"), [D, T], F32, kind="Internal").ap()

    with ExitStack() as ctx:
        p = Prog(nc, ctx)
        ent = ()
        for rc in range(8):
            outs = emit_l0_ffn(p, nc, ctx, 512, 8,
                               xfull[:, :, 512 * rc:512 * (rc + 1)],
                               wT, bcol, cuo_d[:, 512 * rc:512 * (rc + 1)],
                               entry=ent)
            ent = tuple(outs)
        o2 = emit_l0_attn(p, nc, ctx, T, xb, qkvT, projT, pbias, attn_acc_d,
                          xnb_d, entry=ent)
        o3 = emit_l0b(p, nc, ctx, T, attn_acc_d, xnb_d, cuo_d, wT["woT"],
                      bcol["bocol"], x1_d, entry=tuple(o2))
        with ExitStack() as lctx1:
            ph1, scr1, ohp1 = emit_l1(p, nc, lctx1, T, x1_d, WqT, WkT, WvT, WgT, bg,
                                      WdlT, Lpar, WoT1, lngcol1, lnbcol1, rwT1, rb1,
                                      w1T1, b1col1, w2T1, b2col1, x2_d,
                                      entry=tuple(o3))
            ph2 = emit_l1_attn_heads(p, nc, lctx1, T, scr1, WoT1, lngcol1, lnbcol1,
                                     x1_d, scr1[4], entry=tuple(ph1))
            o4 = emit_l1_moe(p, nc, lctx1, T, scr1[0], scr1[4], ohp1, w1T1, b1col1,
                             w2T1, b2col1, x2_d, entry=tuple(ph2))
        with ExitStack() as lctx2:
            p1, scr2, l2r = emit_l2(p, nc, lctx2, T, x2_d, WinT, wconvc, bconvc, WxT,
                                    WdtT, bdtc, AlogC, lngcol2, lnbcol2, WoutT, boutc,
                                    WrT2, br2, W1T2, b1col2, W2T2, b2col2, x3_d,
                                    entry=tuple(o4))
            p2 = emit_l2_scan(p, nc, lctx2, T, scr2, AlogC, entry=tuple(p1))
            p3 = emit_l2_out(p, nc, lctx2, T, x2_d, scr2, lngcol2, lnbcol2, WoutT,
                             boutc, entry=tuple(p2))
            o5 = emit_l2_moe(p, nc, lctx2, T, scr2, l2r, W1T2, b1col2, W2T2, b2col2,
                             x3_d, entry=tuple(p3))
        with ExitStack() as lctx3:
            ohp3 = SB(nc, lctx3, [8, T], BF16, "ohp3")
            q1, scr3 = emit_l3_phase1(p, nc, lctx3, T, x3_d, wqkvT3, relbias3, wgate3,
                                      bgate3, ohp3, entry=tuple(o5))
            q2 = emit_l3_attn_heads(p, nc, lctx3, T, scr3, wprojT3, pbcol3, x3_d,
                                    entry=tuple(q1))
            fins = emit_l3_moe_main(p, nc, lctx3, T, scr3[3], scr3[5],
                                    ohp3, W1T3, b1col3, W2T3, b2col3,
                                    x4, entry=tuple(q2))
        p.op("gpsimd", lambda g: g.nop(), after=tuple(fins))
        p.emit()
    return nc


def kernel_fused(**I):
    T = 4096
    x = np.asarray(I["L0_attn_x"], np.float32)
    n = 8
    nc = build_full(T=T)
    xfull = np.ascontiguousarray(x.transpose(0, 2, 1))
    wconv = np.asarray(I["L2_attn_W_conv"], np.float32)[:, 0, :]
    common = {
        "xfull": xfull,
        "qkvT": _t(I["L0_attn_qkv_weight"]),
        "projT": _t(I["L0_attn_proj_weight"]),
        "pbias": _cols(I["L0_attn_proj_bias"]),
        "wfT": _t(I["L0_ffn_wf"]), "wiT": _t(I["L0_ffn_wi"]),
        "wcT": _t(I["L0_ffn_wc"]), "ufT": _t(I["L0_ffn_uf"]),
        "uiT": _t(I["L0_ffn_ui"]), "ucT": _t(I["L0_ffn_uc"]),
        "uoT": _t(I["L0_ffn_uo"]), "woT": _t(I["L0_ffn_wo"]),
        "bfc": _cols(I["L0_ffn_bf"]), "bic": _cols(I["L0_ffn_bi"]),
        "bcc": _cols(I["L0_ffn_bc"]), "incc": _cols(I["L0_ffn_in_c"]),
        "bocol": _cols(I["L0_ffn_bo"]),
        "WqT": _t(I["L1_attn_Wq"]), "WkT": _t(I["L1_attn_Wk"]),
        "WvT": _t(I["L1_attn_Wv"]), "WgT": _t(I["L1_attn_Wg"]),
        "bg": np.asarray(I["L1_attn_bg"], np.float32)[None, :],
        "WdlT": _t(I["L1_attn_Wdl"]),
        "Lpar": np.ascontiguousarray(np.asarray(I["L1_attn_Lparam"], np.float32).reshape(1, 96)),
        "WoT1": _t(I["L1_attn_Wo"]),
        "lngcol1": _cols(I["L1_attn_ln_g"]), "lnbcol1": _cols(I["L1_attn_ln_b"]),
        "rwT1": _t(I["L1_ffn_router_w"]),
        "rb1": np.asarray(I["L1_ffn_router_b"], np.float32)[None, :],
        "w1T1": np.ascontiguousarray(np.asarray(I["L1_ffn_weights1"], np.float32)),
        "b1col1": np.ascontiguousarray(np.asarray(I["L1_ffn_bias1"], np.float32)
                                       .reshape(32, 2, 128).transpose(0, 2, 1)),
        "w2T1": np.ascontiguousarray(np.asarray(I["L1_ffn_weights2"], np.float32)),
        "b2col1": np.ascontiguousarray(np.asarray(I["L1_ffn_bias2"], np.float32)
                                       .reshape(32, 8, 128).transpose(0, 2, 1)),
        "WinT": _t(I["L2_attn_W_in"]),
        "wconvc": np.ascontiguousarray(wconv.reshape(8, 128, 4).transpose(1, 0, 2)
                                       .reshape(128, 32)),
        "bconvc": _cols(I["L2_attn_b_conv"]),
        "WxT": _t(I["L2_attn_W_xproj"]),
        "WdtT": _t(I["L2_attn_W_dt"]),
        "bdtc": _cols(I["L2_attn_b_dt"]),
        "AlogC": np.ascontiguousarray(np.asarray(I["L2_attn_A_log"], np.float32)
                                      .reshape(8, 128, 16).transpose(1, 0, 2)
                                      .reshape(128, 128)),
        "lngcol2": _cols(I["L2_attn_ln_gamma"]), "lnbcol2": _cols(I["L2_attn_ln_beta"]),
        "WoutT": _t(I["L2_attn_W_out"]),
        "boutc": _cols(I["L2_attn_b_out"]),
        "WrT2": np.ascontiguousarray(np.asarray(I["L2_ffn_Wr"], np.float32)),
        "br2": np.asarray(I["L2_ffn_br"], np.float32)[None, :],
        "W1T2": np.ascontiguousarray(np.asarray(I["L2_ffn_W1"], np.float32)),
        "b1col2": np.ascontiguousarray(np.asarray(I["L2_ffn_b1"], np.float32)
                                       .reshape(8, 32, 128).transpose(0, 2, 1)),
        "W2T2": np.ascontiguousarray(np.asarray(I["L2_ffn_W2"], np.float32)),
        "b2col2": np.ascontiguousarray(np.asarray(I["L2_ffn_b2"], np.float32)
                                       .reshape(8, 8, 128).transpose(0, 2, 1)),
        "wqkvT3": _t(I["L3_attn_w_qkv"]),
        "wprojT3": _t(I["L3_attn_w_proj"]),
        "pbcol3": _cols(I["L3_attn_b_proj"]),
        "relbias3": np.ascontiguousarray(np.asarray(I["L3_attn_rel_bias_table"], np.float32)),
        "wgate3": np.ascontiguousarray(np.asarray(I["L3_ffn_w_gate"], np.float32)),
        "bgate3": np.asarray(I["L3_ffn_b_gate"], np.float32)[None, :],
        "W1T3": np.ascontiguousarray(np.asarray(I["L3_ffn_W1"], np.float32)),
        "b1col3": np.ascontiguousarray(np.asarray(I["L3_ffn_b1"], np.float32)
                                       .reshape(8, 32, 128).transpose(0, 2, 1)),
        "W2T3": np.ascontiguousarray(np.asarray(I["L3_ffn_W2"], np.float32)),
        "b2col3": np.ascontiguousarray(np.asarray(I["L3_ffn_b2"], np.float32)
                                       .reshape(8, 8, 128).transpose(0, 2, 1)),
    }
    in_maps = [dict(common, xb=np.ascontiguousarray(x[i].T)) for i in range(n)]
    res = run_bass_kernel_spmd(nc, in_maps, core_ids=list(range(n)))
    return np.stack([res.results[i]["x4"].T for i in range(n)], axis=0)


_KERNEL_MULTI = kernel


def kernel(**I):
    import os
    if os.environ.get("MK_MULTI"):
        return _KERNEL_MULTI(**I)
    return kernel_fused(**I)
